# version 09
# baseline (speedup 1.0000x reference)
import os
import numpy as np
import ml_dtypes
import concourse.bass as bass
import concourse.mybir as mybir
from concourse.bass_utils import run_bass_kernel_spmd

F32 = mybir.dt.float32
BF16 = mybir.dt.bfloat16
AF = mybir.ActivationFunctionType
AL = mybir.AluOpType
AX = mybir.AxisListType.X

P, D, FC, HH, NHC = 128, 1024, 8, 4096, 32
TOK, NTB, TT, NTT = 2048, 16, 512, 4
H, DH, C = 8, 128, 64
DBG = set(s for s in os.environ.get("KDBG", "").split(",") if s)
PHASE = int(os.environ.get("KPHASE", "9"))
KSUB = os.environ.get("KSUB", "full")
KWIN = os.environ.get("KWIN", "full")


class Sem:
    def __init__(self, cm, name):
        self.h = cm.__enter__()
        self.n = 0
        self.name = name


class Stream:
    def __init__(self, name, auto=None):
        self.name = name
        self.auto = auto
        self.ops = []
        self.waited = {}

    def w(self, sem, val=None):
        val = sem.n if val is None else val
        if not val or val <= 0 or self.waited.get(sem.name, -1) >= val:
            return
        self.waited[sem.name] = val
        self.ops.append(lambda e, s=sem, v=val: e.wait_ge(s.h, v))

    def op(self, f, inc=None, k=1):
        if self.auto is not None:
            self.w(self.auto, self.auto.n)
            if inc is None:
                inc, k = self.auto, 1
        if inc is None:
            self.ops.append(f)
            return None
        inc.n += k
        self.ops.append(lambda e, f=f, s=inc, k=k: f(e).then_inc(s.h, k))
        return inc.n


def _r(ap, shape):
    if len(shape) == 1:
        return ap
    pat = "p (" + " ".join(f"d{i}" for i in range(len(shape))) + ") -> p " + \
          " ".join(f"d{i}" for i in range(len(shape)))
    return ap.rearrange(pat, **{f"d{i}": s for i, s in enumerate(shape)})


def build_nc(dbg=DBG):
    nc = bass.Bass("TRN2", target_bir_lowering=False)
    ctxs = []

    def ent(cm):
        ctxs.append(cm)
        return cm.__enter__()

    sbuf = lambda n, s, d: ent(nc.sbuf_tensor(n, s, d))
    psumt = lambda n, s, d: ent(nc.psum_tensor(n, s, d))
    sem = lambda n: Sem(nc.semaphore(n), n)

    dram = {}
    for name, shape in [
        ("x0", (D, TOK)), ("ltsrc", (16, D)),
        ("w1t", (D, HH)), ("w2t", (HH, D)), ("wwt", (D, D)), ("wrt", (D, D)),
        ("wbt", (D, H)), ("b1c", (P, NHC)), ("b2c", (P, FC)), ("scl", (1, 1)),
        ("decay", (1, H)), ("dijs", (P, P)), ("bdstrictneg", (P, P)),
        ("bdinclT", (P, P)), ("njcol", (P, 1)), ("iprow", (1, C)),
        ("dwcol", (P, 1)), ("gpcol", (P, 1)), ("smask", (1, 8)),
        ("smask2", (1, 8)),
    ]:
        dram[name] = nc.dram_tensor(name, list(shape), F32, kind="ExternalInput").ap()
    OUT = nc.dram_tensor("out", [D, TOK], F32, kind="ExternalOutput").ap()

    ltd = nc.dram_tensor("ltd", [16, D], F32).ap()
    pg_in = nc.dram_tensor("pg_in", [P, FC], F32)
    pg_out = nc.dram_tensor("pg_out", [8, P, FC], F32)
    sg_in = nc.dram_tensor("sg_in", [P, H * DH], F32)
    sg_out = nc.dram_tensor("sg_out", [8, P, H * DH], F32)
    lg_in = nc.dram_tensor("lg_in", [P, FC * 16], F32)
    lg_out = nc.dram_tensor("lg_out", [8, P, FC * 16], F32)
    xspill = nc.dram_tensor("xspill", [NTB, P, H * 256], BF16).ap()
    wspill = nc.dram_tensor("wspill", [NTB, P, H * DH], BF16).ap()
    aspill = nc.dram_tensor("aspill", [NTB, P, H * DH], BF16).ap()
    dspill = nc.dram_tensor("dspill", [NTB, P, H * DH], BF16).ap()
    w1b = nc.dram_tensor("w1b", [D, HH], BF16).ap()
    w2b = nc.dram_tensor("w2b", [HH, D], BF16).ap()

    XR = sbuf("XR", [P, FC, TOK], F32)
    YN = sbuf("YN", [P, H, TOK + 1], BF16)
    BETA = sbuf("BETA", [P, NTB, H], F32)
    BGAM = sbuf("BGAM", [P, H], F32)
    SF = sbuf("SF", [P, H, DH], F32)
    SBT = sbuf("SBT", [P, H, DH], BF16)
    VNS = sbuf("VNS", [P, H * DH], BF16)
    IDC = sbuf("IDC", [P, P], BF16)
    MST = sbuf("MST", [P, H, P], BF16)
    MIT = sbuf("MIT", [P, H, P], BF16)
    DPOS = sbuf("DPOS", [P, H, C], BF16)
    COLS = sbuf("COLS", [P, 64], F32)
    GPC, DWC, CTC, NGC, LGC, SMCo, SCLCo, SCLHo, PRSB = 0, 8, 16, 24, 32, 40, 56, 57, 58
    B1C = sbuf("B1Cs", [P, NHC], F32)
    B2CH = sbuf("B2CHs", [P, FC], F32)
    ROWF = sbuf("ROWF", [P, 2048], F32)
    ROWB = sbuf("ROWB", [P, TT], BF16)
    ONE128B = sbuf("ONE128B", [P, 1], BF16)
    ONESFW = sbuf("ONESFW", [P, P], F32)
    CONSTV = sbuf("CONSTV", [P, 4], F32)
    E2TMP = ROWB[0:4, 256:512]
    LN2A1 = ROWF[32:33, 0:TOK]
    LN2B = ROWF[64:65, 0:TOK]
    LTT = sbuf("LTT", [P, FC, 16], BF16)
    E1P = sbuf("E1P", [4, NHC, P], BF16)
    O2P = sbuf("O2P", [P, NHC, 16], BF16)
    E2P = sbuf("E2P", [16, FC, P], BF16)
    PVT = sbuf("PVT", [4, TT], BF16)
    GVT = sbuf("GVT", [16, TT], BF16)
    QTI = sbuf("QTI", [P, 8], BF16)
    PRF = sbuf("PRF", [P, FC], F32)
    PRB = sbuf("PRB", [P, FC], BF16)
    PR8 = sbuf("PR8", [P, FC], F32)
    CBUF = {}
    for nm in ["njcol", "iprow", "dwcol", "gpcol", "smask", "smask2",
               "b1c", "b2c", "scl", "decay"]:
        CBUF[nm] = sbuf("c_" + nm, list(dram[nm].shape), F32)

    ARB = sbuf("ARB", [P, 28672], BF16)
    ARF = sbuf("ARF", [P, 4608], F32)

    def _arena(T, off, shape):
        n = int(np.prod(shape[1:]))
        ap = T[0:shape[0], off:off + n]
        if len(shape) > 2:
            pat = "p (" + " ".join(f"d{i}" for i in range(len(shape) - 1)) + ") -> p " + \
                  " ".join(f"d{i}" for i in range(len(shape) - 1))
            ap = ap.rearrange(pat, **{f"d{i}": s for i, s in enumerate(shape[1:])})
        return ap
    arb = lambda off, shape: _arena(ARB, off, shape)
    arf = lambda off, shape: _arena(ARF, off, shape)

    PS = psumt("PS", [P, 4096], F32)
    bank = lambda i, off=0, n=512: PS[:, i * 512 + off:i * 512 + off + n]

    sSY, sGD, sTE, sVE, sSC, sGP, sCC, sST = (sem(x) for x in
        ["sSY", "sGD", "sTE", "sVE", "sSC", "sGP", "sCC", "sST"])
    sX0 = sem("sX0")
    sWW = sem("sWW")
    sWW1 = sem("sWW1")
    sWS = [sem("sWS0"), sem("sWS1")]
    sRL = [sem("sRL0"), sem("sRL1")]
    TE = Stream("te")
    VE = Stream("ve", auto=sVE)
    SC = Stream("sc", auto=sSC)
    GP = Stream("gp", auto=sGP)
    SY = Stream("sy")
    taps = {"n": 0}

    def _finish():
        blk = ent(nc.Block())
        @blk.sync
        def _(e):
            for f in SY.ops:
                f(e)
        @blk.tensor
        def _(e):
            for f in TE.ops:
                f(e)
        @blk.vector
        def _(e):
            for f in VE.ops:
                f(e)
        @blk.scalar
        def _(e):
            for f in SC.ops:
                f(e)
        @blk.gpsimd
        def _(e):
            for f in GP.ops:
                f(e)
        for cm in reversed(ctxs):
            cm.__exit__(None, None, None)

    def tap(name, src_fn, shape, dt=F32):
        if name not in dbg:
            return
        t = nc.dram_tensor("dbg_" + name, list(shape), dt, kind="ExternalOutput").ap()
        GP.w(sVE)
        GP.w(sSC)
        GP.w(sTE)
        GP.op(lambda e, t=t, f=src_fn: e.dma_start(out=t, in_=f()), sST, 16)
        taps["n"] += 1

    ONESF = ROWF[0:1, 0:P]
    ONESB = ROWB[0:1, 0:P]
    col = lambda o, k=1: COLS[:, o:o + k]

    SY.op(lambda e: e.dma_start(out=XR[:, :, :], in_=dram["x0"].rearrange("(c p) t -> p c t", p=P)), sX0, 16)
    v_x0 = sX0.n
    for nm in CBUF:
        GP.op(lambda e, nm=nm: e.dma_start(out=CBUF[nm][:, :], in_=dram[nm]), sGD, 16)
    CBUF["dijs"] = arf(256, (P, P))
    CBUF["bdstrictneg"] = arf(384, (P, P))
    CBUF["bdinclT"] = arf(512, (P, P))
    for nm in ["dijs", "bdstrictneg", "bdinclT"]:
        GP.op(lambda e, nm=nm: e.dma_start(out=CBUF[nm], in_=dram[nm]), sGD, 16)
    v_consts = sGD.n

    GP.w(sGD, v_consts)
    GP.op(lambda e: e.memset(IDC[:, :], 0.0), sGP, 1)
    GP.w(sGP)
    def _init(e):
        e.affine_select(out=IDC[:, :], in_=IDC[:, :], compare_op=AL.not_equal,
                        fill=1.0, base=0, pattern=[[-1, P]], channel_multiplier=1)
        e.memset(ROWF[0:1, 0:P], 1.0)
        e.memset(ONESFW[:, :], 1.0)
        e.memset(CONSTV[:, 0:1], 1e-5)
        e.memset(CONSTV[:, 1:2], -0.5)
        e.memset(CONSTV[:, 2:3], 64.0)
        e.memset(CONSTV[:, 3:4], 0.5)
        e.memset(ROWB[0:1, 0:P], 1.0)
        e.memset(ONE128B[:, :], 1.0)
        return e.tensor_copy(B1C[:, :], CBUF["b1c"][:, :])
    GP.op(_init, sGP, 1)
    v_init = sGP.n

    SC.w(sGD, v_consts)
    SC.op(lambda e: e.activation(ROWF[0:1, 128:128 + H], CBUF["decay"][0:1, :], AF.Sigmoid))
    SC.op(lambda e: e.activation(ROWF[0:1, 128:128 + H], ROWF[0:1, 128:128 + H], AF.Ln), sSC, 1)
    TE.w(sSC)
    TE.w(sGP, v_init)
    def _bc0(e):
        e.matmul(bank(0, 0, 8), ONESF, ROWF[0:1, 128:128 + H], start=True, stop=True)
        e.matmul(bank(0, 8, 8), ONESF, CBUF["smask"][0:1, :], start=True, stop=True)
        e.matmul(bank(0, 16, 8), ONESF, CBUF["smask2"][0:1, :], start=True, stop=True)
        return e.matmul(bank(0, 24, 1), ONESF, CBUF["scl"][0:1, :], start=True, stop=True)
    TE.op(_bc0, sTE, 1)
    SC.w(sTE)
    SC.op(lambda e: e.copy(col(LGC, 8), bank(0, 0, 8)))
    SC.op(lambda e: e.copy(col(SMCo, 8), bank(0, 8, 8)))
    SC.op(lambda e: e.copy(col(48, 8), bank(0, 16, 8)))
    SC.op(lambda e: e.copy(col(SCLCo), bank(0, 24, 1)), sSC, 1)
    SC.op(lambda e: e.mul(col(SCLHo), col(SCLCo), CONSTV[:, 3:4]))
    SC.op(lambda e: e.activation(col(CTC, 8), col(LGC, 8), AF.Exp, scale=CONSTV[:, 2:3]))
    SC.op(lambda e: e.activation(col(NGC, 8), col(LGC, 8), AF.Exp))
    for h in range(H):
        SC.op(lambda e, h=h: e.activation(col(GPC + h), CBUF["gpcol"][:, :], AF.Exp, scale=col(LGC + h)))
        SC.op(lambda e, h=h: e.activation(col(DWC + h), CBUF["dwcol"][:, :], AF.Exp, scale=col(LGC + h)))
    VE.w(sSC)
    VE.op(lambda e: e.tensor_scalar_mul(col(NGC, 8), col(NGC, 8), -1.0))
    VE.w(sGD, v_consts)
    VE.op(lambda e: e.tensor_scalar_mul(B2CH[:, :], CBUF["b2c"][:, :], 0.5), sVE, 1)

    MTMP = arf(0, (P, P))
    NJX = arf(128, (P, 1))
    for h in range(H):
        SC.w(sVE)
        SC.op(lambda e, h=h: e.activation(MTMP, CBUF["dijs"], AF.Exp, scale=col(LGC + h)))
        SC.op(lambda e, h=h: e.activation(NJX, CBUF["njcol"][:, :], AF.Exp, scale=col(LGC + h)))
        SC.op(lambda e, h=h: e.activation(ROWF[0:1, 192:192 + C], CBUF["iprow"][0:1, :], AF.Exp,
                                          scale=COLS[0:1, LGC + h:LGC + h + 1]), sSC, 1)
        VE.w(sSC)
        VE.op(lambda e, h=h: e.tensor_mul(MST[:, h, :], MTMP, CBUF["bdstrictneg"]))
        VE.op(lambda e, h=h: e.tensor_scalar(MIT[:, h, :], CBUF["bdinclT"], NJX[:, 0:1], None, AL.mult))
        VE.op(lambda e, h=h: e.tensor_copy(ROWB[0:1, 128:128 + C], ROWF[0:1, 192:192 + C]), sVE, 1)
        TE.w(sVE)
        TE.op(lambda e, h=h: e.matmul(bank(1, 0, C), ONESB, ROWB[0:1, 128:128 + C], start=True, stop=True), sTE, 1)
        SC.w(sTE)
        SC.op(lambda e, h=h: e.copy(DPOS[:, h, :], bank(1, 0, C)), sSC, 1)
    v_masks = sSC.n

    LTS = ARF[0:16, 1024:1024 + D]
    LTSQ = ARF[0:16, 2048:2048 + D]
    LTM = ARF[0:16, 3072:3072 + 8]
    AB_LTB = 22528

    def emit_lt():
        VE.op(lambda e: e.reduce_sum(LTM[:, 0:1], LTS, axis=AX))
        VE.op(lambda e: e.tensor_scalar_mul(LTM[:, 0:1], LTM[:, 0:1], 1.0 / D), sVE, 1)
        SC.w(sVE)
        SC.op(lambda e: e.activation(LTSQ, LTS, AF.Square), sSC, 1)
        VE.w(sSC)
        VE.op(lambda e: e.reduce_sum(LTM[:, 1:2], LTSQ, axis=AX))
        VE.op(lambda e: e.tensor_scalar_mul(LTM[:, 1:2], LTM[:, 1:2], 1.0 / D))
        VE.op(lambda e: e.tensor_mul(LTM[:, 2:3], LTM[:, 0:1], LTM[:, 0:1]))
        VE.op(lambda e: e.tensor_sub(LTM[:, 1:2], LTM[:, 1:2], LTM[:, 2:3]), sVE, 1)
        SC.w(sVE)
        SC.op(lambda e: e.activation(LTM[:, 2:3], LTM[:, 1:2], AF.Sqrt, bias=CONSTV[0:16, 0:1]), sSC, 1)
        VE.w(sSC)
        VE.op(lambda e: e.reciprocal(LTM[:, 3:4], LTM[:, 2:3]))
        VE.op(lambda e: e.tensor_scalar(LTS, LTS, LTM[:, 0:1],
                                        LTM[:, 3:4], AL.subtract, AL.mult), sVE, 1)
        v_ln = sVE.n
        GP.w(sVE, v_ln)
        GP.op(lambda e: e.dma_start(out=ltd, in_=LTS), sGD, 16)
        v_ltd = sGD.n
        LTB16 = ARB[0:16, AB_LTB:AB_LTB + D]
        VE.op(lambda e: e.tensor_copy(LTB16, LTS), sVE, 1)
        TE.w(sVE)
        for fc in range(FC):
            TE.w(sSC)
            TE.op(lambda e, fc=fc: e.matmul(bank(1, 0, 16), LTB16[:, fc * P:(fc + 1) * P],
                                            IDC[0:16, 0:16], start=True, stop=True), sTE, 1)
            SC.w(sTE)
            SC.op(lambda e, fc=fc: e.copy(LTT[:, fc, :], bank(1, 0, 16)), sSC, 1)
        v_ltt = sSC.n
        GP.w(sGD, v_ltd)
        GP.op(lambda e: e.memset(E1P[0:4, :, :], 0.0), sGP, 1)
        GP.op(lambda e: e.dma_start(out=LTSQ[0:4, :], in_=ltd[0:8:2, :]), sGD, 16)
        VE.w(sGD)
        VE.w(sGP)
        LTB4 = ARB[0:4, AB_LTB:AB_LTB + D]
        VE.w(sTE)
        VE.op(lambda e: e.tensor_copy(LTB4, LTSQ[0:4, :]), sVE, 1)
        GP.w(sVE)
        GP.w(sGP)
        for k in range(4):
            GP.op(lambda e, k=k: e.dma_start(
                out=E1P[k:k + 1, 8 * k:8 * k + 8, :].rearrange("a b c -> a (b c)"),
                in_=LTB4[k:k + 1, :]), sGD, 16)
        GP.w(sGD)
        GP.w(sVE)
        GP.op(lambda e: e.memset(O2P[:, :, :], 0.0), sGP, 1)
        VE.w(sGP)
        VE.w(sSC, v_ltt)
        for hc in range(NHC):
            tp0 = hc // 8
            VE.op(lambda e, hc=hc, tp0=tp0: e.tensor_copy(
                O2P[:, hc, tp0:tp0 + 13:4], LTT[:, hc % 8, 9:16:2]), sVE, 1)
        v_o2p = sVE.n
        GP.op(lambda e: e.memset(E2P[0:16, :, :], 0.0), sGP, 1)
        v_ez = sGP.n
        for k in range(4):
            GP.w(sGP, v_ez)
            GP.w(sSC)
            GP.w(sTE)
            def gq(e, k=k):
                src = ltd[8 + 2 * k:8 + 2 * k + 1, :].rearrange(
                    "a (qh q p) -> (a q) qh p", p=4, qh=2)
                return e.dma_start(out=_r(QTI[:, :], (2, 4)), in_=src)
            GP.op(gq, sGD, 16)
            TE.w(sGD)
            for qh in range(2):
                TE.op(lambda e, qh=qh: e.matmul(bank(1, qh * 128, 128)[0:4, :],
                                                _r(QTI[:, :], (2, 4))[:, qh, :],
                                                IDC[:, :], start=True, stop=True), sTE, 1)
            SC.w(sTE)
            SC.op(lambda e: e.copy(E2TMP, bank(1, 0, 256)[0:4, :]), sSC, 1)
            GP.w(sSC)
            def ge(e, k=k):
                return e.dma_start(
                    out=E2P.rearrange("a b c -> a (b c)")[4 * k:4 * k + 4, 256 * k:256 * k + 256],
                    in_=E2TMP)
            GP.op(ge, sGD, 16)
        return dict(ve=sVE.n, gd=sGD.n, sc=sSC.n)

    if PHASE < 1:
        GP.w(sGD, v_consts)
        GP.w(sGP)
        GP.w(sSC)
        GP.w(sVE)
        GP.w(sTE)
    GP.w(sGD, v_consts)
    GP.op(lambda e: e.dma_start(out=LTS, in_=dram["ltsrc"]), sGD, 16)
    VE.w(sGD)
    v_lt1 = emit_lt()

    AB_XB, AB_XSQ, AB_Y1, AB_H, AB_WG0, AB_WG1 = 0, 4096, 0, 8192, 24576, 26624
    AF_WS = [0, 2048]
    AF_T1 = 4096

    lns = {}

    def ln_ttile(src_slice, ydst, rows_a1b=None):
        XB = arb(AB_XB, (P, FC, TT))
        XSQ = arb(AB_XSQ, (P, FC, TT))
        for fc in range(FC):
            VE.op(lambda e, fc=fc: e.tensor_copy(XB[:, fc, :], src_slice(fc)), sVE, 1)
        v_cast = sVE.n
        SC.w(sVE, v_cast)
        SC.op(lambda e: e.activation(XSQ[:, :, :], XB[:, :, :], AF.Square), sSC, 1)
        v_sq = sSC.n
        TE.w(sVE, v_cast)
        TE.w(sSC, v_sq)
        def _stat(e):
            last = None
            for blk in range(4):
                for fc in range(FC):
                    last = e.matmul(PS[0:1, 3 * 512 + blk * 128:3 * 512 + (blk + 1) * 128],
                                    ONE128B[:, :], XB[:, fc, blk * 128:(blk + 1) * 128],
                                    start=(fc == 0), stop=(fc == FC - 1))
            for blk in range(4):
                for fc in range(FC):
                    last = e.matmul(PS[0:1, 4 * 512 + blk * 128:4 * 512 + (blk + 1) * 128],
                                    ONE128B[:, :], XSQ[:, fc, blk * 128:(blk + 1) * 128],
                                    start=(fc == 0), stop=(fc == FC - 1))
            return last
        TE.op(_stat, sTE, 1)
        v_stat = sTE.n
        lns["stat"] = v_stat
        rm = ROWF[0:1, 1536:1536 + TT]
        rs = ROWF[0:1, 512:512 + TT]
        rb = ROWF[0:1, 1024:1024 + TT]
        VE.w(sTE, v_stat)
        VE.op(lambda e: e.tensor_scalar_mul(rm, bank(3)[0:1, :], 1.0 / D))
        VE.op(lambda e: e.tensor_mul(rb, rm, rm))
        VE.op(lambda e: e.scalar_tensor_tensor(rs, bank(4)[0:1, :], 1.0 / D, rb, AL.mult, AL.subtract), sVE, 1)
        SC.w(sVE)
        SC.op(lambda e: e.activation(rs, rs, AF.Ln, bias=CONSTV[0:1, 0:1]))
        SC.op(lambda e: e.activation(rs, rs, AF.Exp, scale=CONSTV[0:1, 1:2]), sSC, 1)
        VE.w(sSC)
        VE.op(lambda e: e.scalar_tensor_tensor(rb, rm, -1.0, rs, AL.mult, AL.mult), sVE, 1)
        if rows_a1b is not None:
            a1d, bd = rows_a1b
            VE.op(lambda e: e.tensor_scalar_add(a1d, rs, 1.0))
            VE.op(lambda e: e.tensor_copy(bd, rb), sVE, 1)
        TE.w(sVE, sVE.n)
        def _bcast(e):
            e.matmul(bank(4), ONESF, rs, start=True, stop=True)
            return e.matmul(bank(5), ONESF, rb, start=True, stop=True)
        TE.op(_bcast, sTE, 1)
        T1 = arf(AF_T1, (P, TT))
        VE.w(sTE)
        for fc in range(FC):
            VE.op(lambda e, fc=fc: e.tensor_mul(T1, src_slice(fc), bank(4)))
            VE.op(lambda e, fc=fc: e.tensor_add(ydst(fc), T1, bank(5)), sVE, 1)
        return sVE.n

    state = dict(gelu={}, wg={0: None, 1: None}, res={})

    def fire_bcastP():
        GP.w(sVE, state["res"][(1, 3)])
        GP.op(lambda e: e.dma_start(out=pg_in.ap().rearrange("p (c t) -> p c t", t=1),
                                    in_=XR[:, :, 2047:2048],
                                    allow_slow_non_contiguous=True), sGD, 16)
        GP.w(sGD)
        GP.op(lambda e: e.collective_compute(
            "AllGather", AL.bypass, replica_groups=[list(range(8))],
            ins=[pg_in.ap()], outs=[pg_out.ap()]), sCC, 1)

    def emit_ffn(callid, final_store=False):
        Y1 = arb(AB_Y1, (P, FC, TT))
        HT = arb(AB_H, (P, NHC, TT))
        WG = [arb(AB_WG0, (P, FC, 2, P)), arb(AB_WG1, (P, FC, 2, P))]
        WG2 = [arb(AB_WG0, (P, 2, FC, P)), arb(AB_WG1, (P, 2, FC, P))]
        WGF = [AB_WG0, AB_WG1]
        tts = [3, 0, 1, 2] if callid == 1 else list(range(NTT))
        for ti, tt in enumerate(tts):
            ts = tt * TT
            src = lambda fc, ts=ts: XR[:, fc, ts:ts + TT]
            if callid == 1 and ti == 0:
                TE.w(sX0, v_x0)
                VE.w(sX0, v_x0)
            TE.w(sVE, state["res"].get((callid, tts[ti - 1])) if ti else None)
            v_y1 = ln_ttile(src, lambda fc: Y1[:, fc, :])
            TE.w(sVE, v_y1)
            def _l1(e):
                last = None
                for fc in range(FC):
                    last = e.matmul(PS[0:4, 2 * 512:2 * 512 + 512], LTT[:, fc, 1:8:2],
                                    Y1[:, fc, :], start=(fc == 0), stop=(fc == FC - 1))
                return last
            TE.op(_l1, sTE, 1)
            SC.w(sTE)
            SC.op(lambda e: e.mul(PVT[0:4, :], PS[0:4, 2 * 512:2 * 512 + 512],
                                  col(SCLCo)[0:4, :]), sSC, 1)
            v_pv = sSC.n
            first = (callid == 1 and ti == 0)
            for g in range(16):
                bsel = g % 2
                Q = SY if bsel == 0 else GP
                sq_ = sWS[bsel]
                Q.w(sTE, state["wg"][bsel])
                if first:
                    if g >= 2 and g - 2 in state.get("w1c", {}):
                        Q.w(*state["w1c"][g - 2])
                    def _wld(e, g=g, bsel=bsel):
                        src = dram["w1t"].rearrange("(c p) n -> p c n", p=P)[:, :, g * 256:(g + 1) * 256]
                        return e.dma_start(out=_r(ARF[:, AF_WS[bsel]:AF_WS[bsel] + 2048], (FC, 256)), in_=src)
                    Q.op(_wld, sq_, 16)
                    SC.w(sq_)
                    SC.op(lambda e, bsel=bsel: e.copy(
                        ARB[:, WGF[bsel]:WGF[bsel] + 2048],
                        ARF[:, AF_WS[bsel]:AF_WS[bsel] + 2048]), sSC, 1)
                    v_wc = sSC.n
                    Q.w(sSC, v_wc)
                    Q.op(lambda e, g=g, bsel=bsel: e.dma_start(
                        out=w1b.rearrange("(c p) n -> p c n", p=P)[:, :, g * 256:(g + 1) * 256],
                        in_=_r(ARB[:, WGF[bsel]:WGF[bsel] + 2048], (FC, 256))), sq_, 16)
                    state.setdefault("w1c", {})[g] = (sq_, sq_.n)
                    Q.w(sq_)
                else:
                    for vv in state.get("wcfin", []):
                        Q.w(*vv)
                    Q.w(*state["w1c"][g])
                    def _wldc(e, g=g, bsel=bsel):
                        src = w1b.rearrange("(c p) n -> p c n", p=P)[:, :, g * 256:(g + 1) * 256]
                        return e.dma_start(out=_r(ARB[:, WGF[bsel]:WGF[bsel] + 2048], (FC, 256)), in_=src)
                    Q.op(_wldc, sq_, 16)
                    v_wc = None
                for hci in range(2):
                    hc = 2 * g + hci
                    pb = hc % 2
                    if first:
                        TE.w(sSC, v_wc)
                    else:
                        TE.w(sWS[bsel])
                    TE.w(sSC, state["gelu"].get((callid, tt, hc - 2)))
                    TE.w(sSC, v_pv)
                    def _h(e, hci=hci, hc=hc, pb=pb, bsel=bsel):
                        for fc in range(FC):
                            e.matmul(bank(pb), WG[bsel][:, fc, hci, :], Y1[:, fc, :],
                                     start=(fc == 0), stop=False)
                        return e.matmul(bank(pb), E1P[0:4, hc, :], PVT[0:4, :],
                                        start=False, stop=True)
                    TE.op(_h, sTE, 1)
                    SC.w(sTE)
                    state["gelu"][(callid, tt, hc)] = SC.op(
                        lambda e, hc=hc, pb=pb: e.activation(HT[:, hc, :], bank(pb), AF.Gelu,
                                                             bias=B1C[:, hc:hc + 1]), sSC, 1)
                state["wg"][bsel] = sTE.n
            TE.w(sSC, state["gelu"][(callid, tt, NHC - 1)])
            def _g2(e):
                last = None
                for hc in range(NHC):
                    last = e.matmul(PS[0:16, 2 * 512:2 * 512 + 512], O2P[:, hc, :],
                                    HT[:, hc, :], start=(hc == 0), stop=(hc == NHC - 1))
                return last
            TE.op(_g2, sTE, 1)
            SC.w(sTE)
            SC.op(lambda e: e.mul(GVT[0:16, :], PS[0:16, 2 * 512:2 * 512 + 512],
                                  col(SCLHo)[0:16, :]), sSC, 1)
            v_gv = sSC.n
            for g in range(16):
                bsel = g % 2
                Q = SY if bsel == 0 else GP
                sq_ = sWS[bsel]
                Q.w(sTE, state["wg"][bsel])
                if first:
                    if g >= 2 and g - 2 in state.get("w2c", {}):
                        Q.w(*state["w2c"][g - 2])
                    def _w2ld(e, g=g, bsel=bsel):
                        src = dram["w2t"].rearrange("(hc p) n -> p hc n", p=P)[:, 2 * g:2 * g + 2, :]
                        return e.dma_start(out=_r(ARF[:, AF_WS[bsel]:AF_WS[bsel] + 2048], (2, D)), in_=src)
                    Q.op(_w2ld, sq_, 16)
                    VE.w(sq_)
                    VE.op(lambda e, bsel=bsel: e.tensor_scalar(
                        ARB[:, WGF[bsel]:WGF[bsel] + 2048],
                        ARF[:, AF_WS[bsel]:AF_WS[bsel] + 2048],
                        0.5, None, AL.mult), sVE, 1)
                    v_wc2 = sVE.n
                    Q.w(sVE, v_wc2)
                    state[("w2cast", bsel)] = v_wc2
                    Q.op(lambda e, g=g, bsel=bsel: e.dma_start(
                        out=w2b.rearrange("(hc p) n -> p hc n", p=P)[:, 2 * g:2 * g + 2, :],
                        in_=_r(ARB[:, WGF[bsel]:WGF[bsel] + 2048], (2, D))), sq_, 16)
                    state.setdefault("w2c", {})[g] = (sq_, sq_.n)
                    Q.w(sq_)
                else:
                    for vv in state.get("wcfin", []):
                        Q.w(*vv)
                    Q.w(*state["w2c"][g])
                    def _w2ldc(e, g=g, bsel=bsel):
                        src = w2b.rearrange("(hc p) n -> p hc n", p=P)[:, 2 * g:2 * g + 2, :]
                        return e.dma_start(out=_r(ARB[:, WGF[bsel]:WGF[bsel] + 2048], (2, D)), in_=src)
                    Q.op(_w2ldc, sq_, 16)
                if first:
                    TE.w(sVE, v_wc2)
                else:
                    TE.w(sWS[bsel])
                if g == 0:
                    TE.w(sSC, v_gv)
                def _w2(e, g=g, bsel=bsel):
                    last = None
                    for hci in range(2):
                        hc = 2 * g + hci
                        for oc in range(FC):
                            last = e.matmul(bank(oc), WG2[bsel][:, hci, oc, :], HT[:, hc, :],
                                            start=(hc == 0), stop=False)
                    return last
                TE.op(_w2, sTE, 1)
                state["wg"][bsel] = sTE.n
            if first:
                state["wcfin"] = [state["w1c"][14], state["w1c"][15],
                                  state["w2c"][14], state["w2c"][15]]
            TE.w(sSC, v_gv)
            def _up(e):
                last = None
                for oc in range(FC):
                    last = e.matmul(bank(oc), E2P[0:16, oc, :], GVT[0:16, :],
                                    start=False, stop=True)
                return last
            TE.op(_up, sTE, 1)
            v_w2 = sTE.n
            VE.w(sTE, v_w2)
            for oc in range(FC):
                VE.op(lambda e, oc=oc, ts=ts: e.scalar_tensor_tensor(
                    XR[:, oc, ts:ts + TT], bank(oc), B2CH[:, oc:oc + 1],
                    XR[:, oc, ts:ts + TT], AL.add, AL.add), sVE, 1)
            state["res"][(callid, tt)] = sVE.n
            if callid == 1 and tt == 3:
                fire_bcastP()
            tap(f"x_f{callid}_t{tt}", lambda ts=ts: XR[:, :, ts:ts + TT], (P, FC, TT))
            if final_store:
                SY.w(sVE)
                SY.op(lambda e, ts=ts: e.dma_start(
                    out=OUT.rearrange("(c p) t -> p c t", p=P)[:, :, ts:ts + TT],
                    in_=XR[:, :, ts:ts + TT]), sST, 16)
        return

    SY.w(sVE, v_lt1["ve"])
    SY.w(sGD, v_lt1["gd"])
    if PHASE >= 1:
        emit_ffn(1)
    v_ffn1_ve = sVE.n

    def partial_store():
        for tt in range(NTT):
            ts = tt * TT
            SY.w(sVE)
            SY.w(sSC)
            SY.op(lambda e, ts=ts: e.dma_start(
                out=OUT.rearrange("(c p) t -> p c t", p=P)[:, :, ts:ts + TT],
                in_=XR[:, :, ts:ts + TT]), sST, 16)
        SY.w(sST, sST.n)
        GP.w(sST, sST.n)
        _finish()
        return nc, sum(len(s.ops) for s in [TE, VE, SC, GP, SY])

    if PHASE < 2:
        TE.w(sX0, v_x0)
        SY.w(sGD)
        SY.w(sGP)
        SY.w(sSC)
        SY.w(sVE)
        SY.w(sTE)
        return partial_store()


    for tt in range(NTT):
        ts = tt * TT
        TE.w(sVE, state["res"][(1, 2)])
        src = lambda fc, ts=ts: XR[:, fc, ts:ts + TT]
        ydst = lambda fc, ts=ts: YN[:, fc, 1 + ts:1 + ts + TT]
        ln_ttile(src, ydst, rows_a1b=(LN2A1[0:1, ts:ts + TT], LN2B[0:1, ts:ts + TT]))
    v_ln2 = sVE.n
    v_ln2stat = lns["stat"]

    tap("y2", lambda: YN[:, :, :], (P, H, TOK + 1), BF16)

    if PHASE < 3:
        return partial_store()

    AT_WWB, AT_V, AT_XW, AT_XWF = 0, 8192, 12288, 16384
    AT_WD, AT_ATW, AT_UT, AT_Y2S = 18432, 20480, 22528, 24064
    AT_WBB, AT_OW = 25096, AT_V
    WWB = arb(AT_WWB, (P, FC, D))
    WBB = arb(AT_WBB, (P, FC, H))
    wwcast = {0: None, 1: None}
    for piece in range(4):
        SY.w(sTE, v_ln2stat)
        bsel = piece % 2
        def _wwl(e, piece=piece, bsel=bsel):
            src = dram["wwt"].rearrange("(c p) n -> p c n", p=P)[:, :, piece * 256:(piece + 1) * 256]
            return e.dma_start(out=_r(ARF[:, AF_WS[bsel]:AF_WS[bsel] + 2048], (FC, 256)), in_=src)
        if wwcast[bsel] is not None:
            SY.w(sVE, wwcast[bsel])
        else:
            SY.w(sVE, state[("w2cast", bsel)])
        sww_ = sWW if bsel == 0 else sWW1
        SY.op(_wwl, sww_, 16)
        VE.w(sww_)
        VE.op(lambda e, piece=piece, bsel=bsel: e.tensor_copy(
            WWB[:, :, piece * 256:(piece + 1) * 256],
            _r(ARF[:, AF_WS[bsel]:AF_WS[bsel] + 2048], (FC, 256))), sVE, 1)
        wwcast[bsel] = sVE.n
    SY.w(sVE)
    SY.op(lambda e: e.dma_start(out=_r(ARF[:, 0:64], (FC, H)),
                                in_=dram["wbt"].rearrange("(c p) n -> p c n", p=P)), sWW, 16)
    VE.w(sWW)
    VE.op(lambda e: e.tensor_copy(WBB[:, :, :], _r(ARF[:, 0:64], (FC, H))), sVE, 1)
    v_ww = sVE.n
    PCH = arf(3088, (P, FC))
    GP.w(sCC)
    VE.w(sCC)
    for r in range(8):
        GP.w(sVE)
        GP.op(lambda e, r=r: e.dma_start(out=PCH, in_=pg_out.ap()[r, :, :]), sGD, 16)
        VE.w(sGD)
        if r == 0:
            VE.op(lambda e, r=r: e.tensor_scalar(PRF[:, :], PCH, col(SMCo + r), None, AL.mult), sVE, 1)
        else:
            VE.op(lambda e, r=r: e.scalar_tensor_tensor(PRF[:, :], PCH, col(SMCo + r),
                                                        PRF[:, :], AL.mult, AL.add), sVE, 1)
    VE.op(lambda e: e.tensor_copy(PRB[:, :], PRF[:, :]), sVE, 1)
    SC.w(sVE)
    SC.op(lambda e: e.activation(PR8[:, :], PRF[:, :], AF.Square), sSC, 1)
    VE.w(sSC)
    VE.op(lambda e: e.tensor_copy(QTI[:, :], PR8[:, :]), sVE, 1)
    TE.w(sVE)
    def _pst(e):
        e.matmul(bank(1, 0, 1)[0:8, :], PRB[:, :], ONE128B[:, :], start=True, stop=True)
        return e.matmul(bank(1, 1, 1)[0:8, :], QTI[:, :], ONE128B[:, :], start=True, stop=True)
    TE.op(_pst, sTE, 1)
    SC.w(sTE)
    SC.op(lambda e: e.copy(PR8[0:8, 0:2], bank(1, 0, 2)[0:8, :]), sSC, 1)
    VE.w(sSC)
    VE.op(lambda e: e.tensor_copy(QTI[0:8, 0:2], PR8[0:8, 0:2]), sVE, 1)
    TE.w(sVE)
    TE.op(lambda e: e.matmul(bank(1, 2, 2)[0:1, :], ONE128B[0:8, :], QTI[0:8, 0:2], start=True, stop=True), sTE, 1)
    SC.w(sTE)
    SC.op(lambda e: e.copy(ROWF[0:1, 136:138], bank(1, 2, 2)[0:1, :]), sSC, 1)
    VE.w(sSC)
    VE.op(lambda e: e.tensor_scalar_mul(ROWF[0:1, 136:138], ROWF[0:1, 136:138], 1.0 / D))
    VE.op(lambda e: e.tensor_mul(ROWF[0:1, 138:139], ROWF[0:1, 136:137], ROWF[0:1, 136:137]))
    VE.op(lambda e: e.tensor_sub(ROWF[0:1, 137:138], ROWF[0:1, 137:138], ROWF[0:1, 138:139]), sVE, 1)
    SC.w(sVE)
    SC.op(lambda e: e.activation(ROWF[0:1, 137:138], ROWF[0:1, 137:138], AF.Ln, bias=CONSTV[0:1, 0:1]))
    SC.op(lambda e: e.activation(ROWF[0:1, 137:138], ROWF[0:1, 137:138], AF.Exp, scale=CONSTV[0:1, 1:2]), sSC, 1)
    VE.w(sSC)
    VE.op(lambda e: e.scalar_tensor_tensor(ROWF[0:1, 138:139], ROWF[0:1, 136:137], -1.0,
                                           ROWF[0:1, 137:138], AL.mult, AL.mult), sVE, 1)
    TE.w(sVE)
    TE.op(lambda e: e.matmul(bank(1, 8, 2), ONESF, ROWF[0:1, 137:139], start=True, stop=True), sTE, 1)
    SC.w(sTE)
    SC.op(lambda e: e.copy(col(PRSB, 2), bank(1, 8, 2)), sSC, 1)
    VE.w(sSC)
    VE.op(lambda e: e.tensor_scalar(PR8[:, :], PRF[:, :], col(PRSB), col(PRSB + 1),
                                    AL.mult, AL.add), sVE, 1)
    SC.w(sVE)
    SC.op(lambda e: e.activation(PRF[:, :], PR8[:, :], AF.Square), sSC, 1)
    VE.w(sSC)
    VE.op(lambda e: e.tensor_copy(QTI[:, :], PRF[:, :]), sVE, 1)
    TE.w(sVE)
    TE.op(lambda e: e.matmul(bank(1, 16, 1)[0:8, :], QTI[:, :], ONE128B[:, :], start=True, stop=True), sTE, 1)
    SC.w(sTE)
    SC.op(lambda e: e.copy(PRF[0:8, 0:1], bank(1, 16, 1)[0:8, :]))
    SC.op(lambda e: e.activation(PRF[0:8, 0:1], PRF[0:8, 0:1], AF.Sqrt), sSC, 1)
    VE.w(sSC)
    VE.op(lambda e: e.tensor_scalar_max(PRF[0:8, 0:1], PRF[0:8, 0:1], 1e-12))
    VE.op(lambda e: e.reciprocal(PRF[0:8, 0:1], PRF[0:8, 0:1]))
    VE.op(lambda e: e.tensor_copy(QTI[0:8, 0:1], PRF[0:8, 0:1]), sVE, 1)
    TE.w(sVE)
    TE.op(lambda e: e.matmul(bank(1, 24, 8)[0:1, :], QTI[0:8, 0:1], IDC[0:8, 0:8], start=True, stop=True), sTE, 1)
    SC.w(sTE)
    SC.op(lambda e: e.copy(ROWF[0:1, 144:152], bank(1, 24, 8)[0:1, :]), sSC, 1)
    TE.w(sSC)
    TE.op(lambda e: e.matmul(bank(1, 32, 8), ONESF, ROWF[0:1, 144:152], start=True, stop=True), sTE, 1)
    VE.w(sTE)
    VE.op(lambda e: e.tensor_mul(PRF[:, :], PR8[:, :], bank(1, 32, 8)))
    VE.op(lambda e: e.tensor_copy(YN[:, :, 0], PRF[:, :]), sVE, 1)
    v_ynprev = sVE.n
    VE.op(lambda e: e.memset(SF[:, :, :], 0.0))
    VE.op(lambda e: e.memset(SBT[:, :, :], 0.0))
    VE.op(lambda e: e.memset(bank(2), 0.0))
    VE.op(lambda e: e.memset(bank(6), 0.0), sVE, 1)
    v_sz = sVE.n

    Y2S = arb(AT_Y2S, (P, FC, 129))
    AT_UT2 = 25160
    UTS = [[arb(AT_UT, (P, 2, P)), arb(AT_UT + 256, (P, 2, P))],
           [arb(AT_UT2, (P, 2, P)), arb(AT_UT2 + 256, (P, 2, P))]]
    VPD = [arb(AT_UT + 512, (P, 2, 256)), arb(AT_UT2 + 512, (P, 2, 256))]

    def UTPn(p, j):
        return UTS[p][j % 2][:, 0, :]

    def UTPc(p, j):
        return UTS[p][j % 2][:, 1, :]

    def UTPx(p, j):
        return UTS[p][j % 2]
    Vw = lambda tb: arb(AT_V + (tb % 2) * 2048, (P, H, 256))
    XWw = lambda tb: arb(AT_XW + (tb % 2) * 2048, (P, H, 256))
    XWFw = lambda tb: arb(AT_XWF + (tb % 2) * 1024, (P, H, DH))
    WDw = lambda tb: arb(AT_WD + (tb % 2) * 1024, (P, H, DH))
    ATWw = lambda tb: arb(AT_ATW + (tb % 2) * 1024, (P, H, DH))
    OWw = arb(AT_OW, (P, H, TT))

    scan1_done = {}
    spill_done = {}
    win_prev = {}

    def emit_window(tb, spill=True):
        t0 = tb * 128
        if tb >= 2:
            TE.w(sVE, scan1_done.get(tb - 2))
            TE.w(sGD, spill_done.get(tb - 2))
        if win_prev:
            TE.w(sVE, win_prev["ve"])
            TE.w(sSC, win_prev["sc"])
        else:
            TE.w(sVE)
            TE.w(sSC)
        TE.w(sVE, v_ww)
        if tb == 0:
            TE.w(sVE, v_sz)
            TE.w(sVE, v_ln2)
        SC.w(sTE, win_prev.get("q"))
        SC.op(lambda e, t0=t0: e.activation(Y2S[:, :, 1:129], YN[:, :, 1 + t0:1 + t0 + 128],
                                            AF.Square), sSC, 1)
        v_y2s = sSC.n
        def _vp(e, t0=t0):
            last = None
            for half in range(2):
                for fc in range(FC):
                    last = e.matmul(bank(half), YN[:, fc, 1 + t0:1 + t0 + 128],
                                    WWB[:, fc, half * 512:(half + 1) * 512],
                                    start=(fc == 0), stop=(fc == FC - 1))
            for fc in range(FC):
                last = e.matmul(bank(3, 0, 8), YN[:, fc, 1 + t0:1 + t0 + 128],
                                WBB[:, fc, :], start=(fc == 0), stop=(fc == FC - 1))
            return last
        TE.op(_vp, sTE, 1)
        v_vp = sTE.n
        TE.w(sSC, v_y2s)
        def _q(e):
            last = None
            for h in range(H):
                last = e.matmul(bank(3, 128 + h, 1), Y2S[:, h, 1:129], ONE128B[:, :],
                                start=True, stop=True)
            return last
        TE.op(_q, sTE, 1)
        win_prev["q"] = sTE.n
        SC.w(sTE, v_vp)
        SC.op(lambda e, tb=tb: e.activation(BETA[:, tb, :], bank(3, 0, 8), AF.Sigmoid), sSC, 1)
        v_beta = sSC.n
        SC.w(sTE)
        SC.op(lambda e: e.copy(PR8[:, 0:8], bank(3, 128, 8)))
        SC.op(lambda e: e.activation(PR8[:, 0:8], PR8[:, 0:8], AF.Sqrt), sSC, 1)
        v_sqr = sSC.n
        VE.w(sSC, v_beta)
        VE.op(lambda e, tb=tb: e.tensor_mul(BGAM[:, :], BETA[:, tb, :], col(GPC, 8)), sVE, 1)
        VE.w(sSC, v_sqr)
        VE.op(lambda e: e.tensor_scalar_max(PR8[:, 0:8], PR8[:, 0:8], 1e-12))
        VE.op(lambda e: e.reciprocal(PR8[:, 0:8], PR8[:, 0:8]))
        VE.op(lambda e: e.tensor_copy(QTI[:, :], PR8[:, 0:8]), sVE, 1)
        v_rn = sVE.n
        for h in range(H):
            SC.op(lambda e, h=h, tb=tb: e.mul(Vw(tb)[:, h, 0:128],
                                              bank(h // 4, (h % 4) * 128, 128),
                                              BETA[:, tb, h:h + 1]), sSC, 1)
        v_vev = sSC.n
        if KWIN == "vp" or KWIN == "q":
            return dict(ve=sVE.n, sc=sSC.n)
        TE.w(sVE, v_rn)
        def _rr(e):
            last = None
            for h in range(H):
                b, o = (2, h * 128) if h < 4 else (3, (h - 4) * 128)
                last = e.matmul(PS[0:1, b * 512 + o:b * 512 + o + 128],
                                QTI[:, h:h + 1], IDC[:, :], start=True, stop=True)
            return last
        TE.op(_rr, sTE, 1)
        SC.w(sTE)
        SC.op(lambda e: e.copy(ROWF[0:1, 512:1024], bank(2)[0:1, :]))
        SC.op(lambda e: e.copy(ROWF[0:1, 1024:1536], bank(3)[0:1, :]), sSC, 1)
        v_rrow = sSC.n
        TE.w(sSC, v_rrow)
        def _rbc(e):
            e.matmul(bank(2), ONESF, ROWF[0:1, 512:1024], start=True, stop=True)
            return e.matmul(bank(3), ONESF, ROWF[0:1, 1024:1536], start=True, stop=True)
        TE.op(_rbc, sTE, 1)
        VE.w(sTE)
        VE.w(sSC, v_y2s)
        VE.op(lambda e, t0=t0: e.tensor_mul(YN[:, :, 1 + t0:1 + t0 + 128],
                                            YN[:, :, 1 + t0:1 + t0 + 128],
                                            PS[:, 2 * 512:2 * 512 + 1024]), sVE, 1)
        v_yn = sVE.n
        if KWIN == "yn":
            return dict(ve=sVE.n, sc=sSC.n)
        TE.w(sVE, v_yn)
        if tb == 0:
            TE.w(sVE, v_ynprev)
        def _wkt(e, t0=t0):
            last = None
            for h in range(H):
                dst = bank(2 + h // 4, (h % 4) * 128, 128)
                last = e.matmul(dst, YN[:, h, t0:t0 + 128], IDC[:, :],
                                start=True, stop=True)
            return last
        TE.op(_wkt, sTE, 1)
        if KWIN == "wkmm":
            return dict(ve=sVE.n, sc=sSC.n)
        VE.w(sTE)
        SC.w(sTE)
        for h in range(4):
            psrc = bank(2, h * 128, 128)
            VE.op(lambda e, h=h, tb=tb, psrc=psrc: e.tensor_scalar(
                Vw(tb)[:, h, 128:256], psrc, BGAM[:, h:h + 1], None, AL.mult), sVE, 1)
            VE.op(lambda e, h=h, tb=tb, psrc=psrc: e.tensor_scalar(
                WDw(tb)[:, h, :], psrc, col(DWC + h), None, AL.mult), sVE, 1)
        for h in range(4, H):
            psrc = bank(3, (h - 4) * 128, 128)
            SC.op(lambda e, h=h, tb=tb, psrc=psrc: e.mul(
                Vw(tb)[:, h, 128:256], psrc, BGAM[:, h:h + 1]), sSC, 1)
            SC.op(lambda e, h=h, tb=tb, psrc=psrc: e.mul(
                WDw(tb)[:, h, :], psrc, col(DWC + h)), sSC, 1)
        v_wk = sSC.n
        v_wk_ve = sVE.n
        if KWIN == "wk":
            return dict(ve=sVE.n, sc=sSC.n)
        if tb == 0:
            for vv in state.get("wcfin", []):
                VE.w(*vv)
                SC.w(*vv)
        ut_free = {0: [(sSC, v_vev)], 1: [(sVE, v_wk_ve)]}

        def EV(p):
            return (VE, sVE) if p == 0 else (SC, sSC)

        def emit_pair(h0):
            hs = (h0, h0 + 1)
            ev = {}

            def gaq(e, h, t0=t0):
                b0 = (h % 2) * 2 * 512
                return e.matmul(PS[:, b0:b0 + 128], YN[:, h, t0:t0 + 128],
                                YN[:, h, t0:t0 + 128], start=True, stop=True)

            for h in hs:
                p = h % 2
                for s_, v_ in ut_free[p]:
                    TE.w(s_, v_)
                TE.op(lambda e, h=h: gaq(e, h), sTE, 1)
                VE.w(sTE, sTE.n)
                VE.op(lambda e, h=h, p=p, tb=tb: e.scalar_tensor_tensor(
                    UTPn(p, 0), bank(2 * p, 0, 128), BETA[:, tb, h:h + 1],
                    MST[:, h, :], AL.mult, AL.mult), sVE, 1)
                ev[(h, "a")] = sVE.n
            for h in hs:
                p = h % 2
                E, sE = EV(p)
                TE.w(sVE, ev[(h, "a")])
                TE.op(lambda e, p=p: e.matmul(bank(2 * p, 128, 128), UTPn(p, 0),
                                              IDC[:, :], start=True, stop=True), sTE, 1)
                E.w(sTE, sTE.n)
                if p == 0:
                    E.op(lambda e: e.tensor_copy(UTPc(0, 0), bank(0, 128, 128)), sE, 1)
                else:
                    E.op(lambda e: e.copy(UTPc(1, 0), bank(2, 128, 128)), sE, 1)
                ev[(h, "at")] = sE.n
            cur = {h: 0 for h in hs}
            for j in range(6):
                for h in hs:
                    p = h % 2
                    E, sE = EV(p)
                    TE.w(sE, ev[(h, "at")] if j == 0 else ev[(h, "u", j - 1)])
                    def _app(e, h=h, p=p, j=j, c=cur[h], tb=tb):
                        rv = Vw(tb)[:, h, :] if j == 0 else VPD[p][:, c, :]
                        e.matmul(bank(2 * p + 1, 0, 256), IDC[:, :], rv,
                                 start=True, stop=False)
                        return e.matmul(bank(2 * p + 1, 0, 256), UTPc(p, j), rv,
                                        start=False, stop=True)
                    TE.op(_app, sTE, 1)
                    v_appm = sTE.n
                    if j < 5:
                        def _sq(e, p=p, j=j):
                            e.matmul(bank(2 * p, 256, 128), UTPc(p, j), UTPn(p, j),
                                     start=True, stop=True)
                            return e.matmul(bank(2 * p, 384, 128), UTPn(p, j), UTPc(p, j),
                                            start=True, stop=True)
                        TE.op(_sq, sTE, 1)
                        v_sqm = sTE.n
                    E.w(sTE, v_appm)
                    if j == 5:
                        if p == 0:
                            E.op(lambda e, h=h, tb=tb: e.tensor_copy(
                                XWw(tb)[:, h, :], bank(1, 0, 256)), sE, 1)
                        else:
                            E.op(lambda e, h=h, tb=tb: e.copy(
                                XWw(tb)[:, h, :], bank(3, 0, 256)), sE, 1)
                    else:
                        c = cur[h]
                        if p == 0:
                            E.op(lambda e, c=c: e.tensor_copy(
                                VPD[0][:, 1 - c, :], bank(1, 0, 256)), sE, 1)
                        else:
                            E.op(lambda e, c=c: e.copy(
                                VPD[1][:, 1 - c, :], bank(3, 0, 256)), sE, 1)
                        cur[h] = 1 - c
                        E.w(sTE, v_sqm)
                        psq = _r(PS[:, (2 * p) * 512 + 256:(2 * p) * 512 + 512], (2, P))
                        if p == 0:
                            E.op(lambda e, j=j, psq=psq: e.tensor_copy(
                                UTPx(0, j + 1)[:, :, :], psq), sE, 1)
                        else:
                            E.op(lambda e, j=j, psq=psq: e.copy(
                                UTPx(1, j + 1)[:, :, :], psq), sE, 1)
                    ev[(h, "u", j)] = sE.n
            vat = {}
            for h in hs:
                p = h % 2
                E, sE = EV(p)
                TE.w(sE, ev[(h, "u", 4)])
                def _at2(e, h=h, p=p, t0=t0):
                    b0 = 2 * p * 512
                    return e.matmul(PS[:, b0 + 256:b0 + 384], YN[:, h, t0:t0 + 128],
                                    YN[:, h, 1 + t0:1 + t0 + 128],
                                    start=True, stop=True)
                TE.op(_at2, sTE, 1)
                vat[h] = sTE.n
            for h in hs:
                p = h % 2
                VE.w(sTE, vat[h])
                VE.op(lambda e, h=h, p=p, tb=tb: e.tensor_mul(
                    ATWw(tb)[:, h, :], bank(2 * p, 256, 128), MIT[:, h, :]), sVE, 1)
                ev[(h, "att")] = sVE.n
            for h in hs:
                p = h % 2
                E, sE = EV(p)
                TE.w(sVE, ev[(h, "att")])
                TE.w(sE, ev[(h, "u", 5)])
                def _xwf(e, h=h, p=p, tb=tb):
                    return e.matmul(bank(2 * p, 384, 128), XWw(tb)[:, h, 128:256],
                                    IDC[:, :], start=True, stop=True)
                TE.op(_xwf, sTE, 1)
                E.w(sTE, sTE.n)
                if p == 0:
                    E.op(lambda e, h=h, tb=tb: e.tensor_scalar(
                        XWFw(tb)[:, h, :], bank(0, 384, 128), col(NGC + h),
                        None, AL.mult), sE, 1)
                else:
                    E.op(lambda e, h=h, tb=tb: e.mul(
                        XWFw(tb)[:, h, :], bank(2, 384, 128), col(NGC + h)), sE, 1)
                ut_free[p] = [(sE, sE.n)]

        for h0 in range(0, H, 2):
            emit_pair(h0)
        win_prev["ve"] = ut_free[0][0][1]
        win_prev["sc"] = ut_free[1][0][1]
        v_win = dict(ve=sVE.n, sc=sSC.n)
        if not spill:
            spill_done[tb] = sGD.n
            return v_win
        GP.w(sVE, v_win["ve"])
        GP.w(sSC, v_win["sc"])
        GP.op(lambda e, tb=tb: e.dma_start(out=xspill[tb], in_=XWw(tb).rearrange("p a b -> p (a b)")), sGD, 16)
        GP.op(lambda e, tb=tb: e.dma_start(out=wspill[tb], in_=XWFw(tb).rearrange("p a b -> p (a b)")), sGD, 16)
        GP.op(lambda e, tb=tb: e.dma_start(out=aspill[tb], in_=ATWw(tb).rearrange("p a b -> p (a b)")), sGD, 16)
        GP.op(lambda e, tb=tb: e.dma_start(out=dspill[tb], in_=WDw(tb).rearrange("p a b -> p (a b)")), sGD, 16)
        spill_done[tb] = sGD.n
        GP.w(sGD)
        return v_win

    def _sudst(h):
        return bank(5, h * 128, 128) if h < 4 else bank(7, (h - 4) * 128, 128)

    scanp = {"A": {}, "B": {}}

    def emit_scan_step(n, with_o, ow_slot=None, win=None):
        tb, c = n // 2, n % 2
        r0 = c * 64
        halves = (("A", 0, 4), ("B", 4, 6))
        pv = {X: dict(scanp[X]) for X, _, _ in halves}
        if win is not None:
            TE.w(sVE, win["ve"])
            TE.w(sSC, win["sc"])
        vne = {}
        vnv = {}
        for X, h0, bk in halves:
            if not pv[X]:
                TE.w(sSC)
                TE.w(sVE)
            else:
                TE.w(sSC, pv[X]["sbt"])
                TE.w(sVE, pv[X].get("oev"))
            def _vn(e, tb=tb, r0=r0, h0=h0, bk=bk):
                e.matmul(PS[r0:r0 + 64, bk * 512:bk * 512 + 512],
                         IDC[r0:r0 + 64, r0:r0 + 64],
                         XWw(tb)[r0:r0 + 64, h0:h0 + 4, 0:128],
                         start=True, stop=False)
                last = None
                for h in range(h0, h0 + 4):
                    dst = PS[r0:r0 + 64, bk * 512 + (h % 4) * 128:bk * 512 + (h % 4) * 128 + 128]
                    last = e.matmul(dst, XWFw(tb)[:, h, r0:r0 + 64], SBT[:, h, :],
                                    start=False, stop=(h % 4 == 3))
                return last
            TE.op(_vn, sTE, 1)
            vnv[X] = sTE.n
        for X, h0, bk in halves:
            SC.w(sTE, vnv[X])
            SC.op(lambda e, r0=r0, h0=h0, bk=bk: e.copy(
                _r(VNS[r0:r0 + 64, h0 * 128:h0 * 128 + 512], (4, 128)),
                PS[r0:r0 + 64, bk * 512:bk * 512 + 512]), sSC, 1)
            vne[X] = sSC.n
        VNSv = _r(VNS[:, :], (H, 128))
        ovv = {}
        suv = {}
        for X, h0, bk in halves:
            if with_o:
                TE.w(sSC, vne[X])
                def _o(e, tb=tb, n=n, r0=r0, h0=h0, bk=bk):
                    last = None
                    for h in range(h0, h0 + 4):
                        dst = bank(bk, (h - h0) * 64, 64)
                        e.matmul(dst, SBT[:, h, :], YN[:, h, 1 + n * 64:1 + n * 64 + 64],
                                 start=True, stop=False)
                        last = e.matmul(dst, VNSv[r0:r0 + 64, h, :],
                                        ATWw(tb)[r0:r0 + 64, h, r0:r0 + 64],
                                        start=False, stop=True)
                    return last
                TE.op(_o, sTE, 1)
                ovv[X] = sTE.n
            else:
                TE.w(sSC, vne[X])
            TE.w(sVE, pv[X].get("sfm"))
            def _su(e, tb=tb, r0=r0, h0=h0):
                last = None
                for h in range(h0, h0 + 4):
                    last = e.matmul(_sudst(h), WDw(tb)[r0:r0 + 64, h, :],
                                    VNSv[r0:r0 + 64, h, :], start=True, stop=True)
                return last
            TE.op(_su, sTE, 1)
            suv[X] = sTE.n
        for X, h0, bk in halves:
            oev = pv[X].get("oev")
            if with_o:
                VE.w(sTE, ovv[X])
                VE.op(lambda e, ow_slot=ow_slot, h0=h0, bk=bk: e.tensor_mul(
                    OWw[:, h0:h0 + 4, ow_slot * 64:ow_slot * 64 + 64],
                    _r(PS[:, bk * 512:bk * 512 + 256], (4, 64)),
                    DPOS[:, h0:h0 + 4, :]), sVE, 1)
                oev = sVE.n
            VE.w(sTE, suv[X])
            for h in range(h0, h0 + 4):
                VE.op(lambda e, h=h: e.scalar_tensor_tensor(
                    SF[:, h, :], SF[:, h, :], col(CTC + h), _sudst(h),
                    AL.mult, AL.add), sVE, 1)
            v_sf = sVE.n
            SC.w(sVE, v_sf)
            SC.op(lambda e, h0=h0: e.copy(SBT[:, h0:h0 + 4, :], SF[:, h0:h0 + 4, :]),
                  sSC, 1)
            scanp[X] = dict(vne=vne[X], oev=oev, sbt=sSC.n, sfm=v_sf)
        return dict(te=sTE.n, ve=sVE.n, sc=sSC.n)

    for tb in range(NTB):
        vwin = emit_window(tb, spill=(KSUB != "w"))
        if KSUB == "full":
            for c in range(2):
                st = emit_scan_step(2 * tb + c, with_o=False,
                                    win=(vwin if c == 0 else None))
            scan1_done[tb] = st["ve"]
        else:
            scan1_done[tb] = sVE.n
    v_scan1 = dict(te=sTE.n, ve=sVE.n, sc=sSC.n)
    if KSUB != "full":
        return partial_store()

    if PHASE < 4:
        return partial_store()

    GP.w(sVE, v_scan1["ve"])
    GP.w(sSC, v_scan1["sc"])
    GP.op(lambda e: e.dma_start(out=sg_in.ap(), in_=SF.rearrange("p a b -> p (a b)")), sGD, 16)
    GP.w(sGD)
    GP.op(lambda e: e.collective_compute(
        "AllGather", AL.bypass, replica_groups=[list(range(8))],
        ins=[sg_in.ap()], outs=[sg_out.ap()]), sCC, 1)
    SGCH = arf(0, (P, H * DH))
    GP.w(sCC)
    VE.w(sCC)
    for r in range(8):
        GP.w(sVE)
        GP.op(lambda e, r=r: e.dma_start(out=SGCH, in_=sg_out.ap()[r, :, :]), sGD, 16)
        VE.w(sGD)
        if r == 0:
            VE.op(lambda e, r=r: e.tensor_scalar(SF.rearrange("p a b -> p (a b)"), SGCH,
                                                 col(SMCo + r), None, AL.mult), sVE, 1)
        else:
            VE.op(lambda e, r=r: e.scalar_tensor_tensor(SF.rearrange("p a b -> p (a b)"), SGCH,
                                                        col(SMCo + r),
                                                        SF.rearrange("p a b -> p (a b)"),
                                                        AL.mult, AL.add), sVE, 1)
    v_sinit = sVE.n
    SC.w(sVE, v_sinit)
    SC.op(lambda e: e.copy(SBT[:, :, :], SF[:, :, :]), sSC, 1)
    v_sinit_sc = sSC.n

    if PHASE < 5:
        return partial_store()

    WRB = arb(AT_WWB, (P, FC, D))
    for piece in range(4):
        SY.w(sVE, v_scan1["ve"])
        bsel = 0
        def _wrl(e, piece=piece, bsel=bsel):
            src = dram["wrt"].rearrange("(c p) n -> p c n", p=P)[:, :, piece * 256:(piece + 1) * 256]
            return e.dma_start(out=_r(ARF[:, AF_WS[bsel]:AF_WS[bsel] + 2048], (FC, 256)), in_=src)
        SY.w(sVE)
        SY.op(_wrl, sWW, 16)
        VE.w(sWW)
        VE.op(lambda e, piece=piece, bsel=bsel: e.tensor_copy(
            WRB[:, :, piece * 256:(piece + 1) * 256],
            _r(ARF[:, AF_WS[bsel]:AF_WS[bsel] + 2048], (FC, 256))), sVE, 1)
    v_wr = sVE.n

    reload_done = {}
    scan2_done = {}

    def emit_reload(tb):
        SY.w(sVE, scan2_done.get(tb - 2))
        if tb < 2:
            SY.w(sGD, spill_done[tb + 14])
            SY.w(sVE, v_scan1["ve"])
            SY.w(sTE, v_scan1["te"])
        sr = sRL[tb % 2]
        SY.op(lambda e, tb=tb: e.dma_start(out=XWw(tb).rearrange("p a b -> p (a b)"), in_=xspill[tb]), sr, 16)
        SY.op(lambda e, tb=tb: e.dma_start(out=XWFw(tb).rearrange("p a b -> p (a b)"), in_=wspill[tb]), sr, 16)
        SY.op(lambda e, tb=tb: e.dma_start(out=ATWw(tb).rearrange("p a b -> p (a b)"), in_=aspill[tb]), sr, 16)
        SY.op(lambda e, tb=tb: e.dma_start(out=WDw(tb).rearrange("p a b -> p (a b)"), in_=dspill[tb]), sr, 16)
        reload_done[tb] = (sr, sr.n)

    emit_reload(0)
    emit_reload(1)
    stage6_done = {}
    bcast2_fired = [False]
    lt3v = [None]

    for w in range(4):
        for k in range(8):
            n = w * 8 + k
            tb, c = n // 2, n % 2
            if c == 0:
                TE.w(*reload_done[tb])
                if tb + 2 < NTB:
                    pass
            if n == 0:
                TE.w(sVE, v_sinit)
                TE.w(sSC, v_sinit_sc)
            st = emit_scan_step(n, with_o=True, ow_slot=k)
            if c == 1:
                scan2_done[tb] = st["ve"]
                if tb + 2 < NTB:
                    emit_reload(tb + 2)
        ws = w * TT
        TE.w(sVE, st["ve"])
        TE.w(sSC, st["sc"])
        TE.w(sVE, v_wr)
        def _a1(e, ws=ws):
            return e.matmul(bank(4), ONESFW[32:33, :], LN2A1[0:1, ws:ws + TT], start=True, stop=True)
        TE.op(_a1, sTE, 1)
        for ocg in range(2):
            TE.w(sVE)
            def _wrm(e, ocg=ocg, ws=ws):
                last = None
                for oci in range(4):
                    oc = ocg * 4 + oci
                    for h in range(H):
                        last = e.matmul(bank(oci), WRB[:, h, oc * 128:(oc + 1) * 128],
                                        OWw[:, h, :], start=(h == 0), stop=False)
                    last = e.matmul(bank(oci), ONESFW[64:65, :], LN2B[0:1, ws:ws + TT],
                                    start=False, stop=True)
                return last
            TE.op(_wrm, sTE, 1)
            v_wrm = sTE.n
            T1 = arf(AF_T1, (P, TT))
            VE.w(sTE, v_wrm)
            for oci in range(4):
                oc = ocg * 4 + oci
                VE.op(lambda e, oc=oc, ws=ws: e.tensor_mul(T1, XR[:, oc, ws:ws + TT], bank(4)))
                VE.op(lambda e, oc=oc, oci=oci, ws=ws: e.scalar_tensor_tensor(
                    XR[:, oc, ws:ws + TT], bank(oci), 1.0, T1, AL.mult, AL.add), sVE, 1)
        stage6_done[w] = sVE.n
        if scanp["A"]:
            scanp["A"]["oev"] = sVE.n
        tap(f"x2_w{w}", lambda ws=ws: XR[:, :, ws:ws + TT], (P, FC, TT))
        if w == 0:
            GP.w(sVE, stage6_done[0])
            GP.op(lambda e: e.dma_start(out=lg_in.ap().rearrange("p (c t) -> p c t", t=16),
                                        in_=XR[:, :, 1:17]), sGD, 16)
            GP.w(sGD)
            GP.op(lambda e: e.collective_compute(
                "AllGather", AL.bypass, replica_groups=[list(range(8))],
                ins=[lg_in.ap()], outs=[lg_out.ap()]), sCC, 1)
    v_scan2 = dict(te=sTE.n, ve=sVE.n, sc=sSC.n)
    tap("ow_last", lambda: OWw[:, :, :], (P, H, TT), BF16)

    if PHASE < 6:
        return partial_store()

    LCH = arf(0, (P, P))
    XLT = arf(512, (P, P))
    XLTB = arb(23552, (P, P))
    GP.w(sCC)
    VE.w(sCC)
    VE.w(sVE, v_scan2["ve"])
    for r in range(8):
        GP.w(sVE)
        GP.op(lambda e, r=r: e.dma_start(out=LCH, in_=lg_out.ap()[r, :, :]), sGD, 16)
        VE.w(sGD)
        if r == 0:
            VE.op(lambda e, r=r: e.tensor_scalar(XLT[:, :], LCH, COLS[:, 48 + r:48 + r + 1],
                                                 None, AL.mult), sVE, 1)
        else:
            VE.op(lambda e, r=r: e.scalar_tensor_tensor(XLT[:, :], LCH,
                                                        COLS[:, 48 + r:48 + r + 1],
                                                        XLT[:, :], AL.mult, AL.add), sVE, 1)
    VE.op(lambda e: e.tensor_copy(XLTB[:, :], XLT[:, :]), sVE, 1)
    v_xlt = sVE.n
    TE.w(sVE, v_xlt)
    TE.w(sSC, v_scan2["sc"])
    for fc in range(FC):
        TE.w(sSC)
        TE.op(lambda e, fc=fc: e.matmul(bank(1, 0, 128)[0:16, :], _r(XLTB[:, :], (FC, 16))[:, fc, :],
                                        IDC[:, 0:P], start=True, stop=True), sTE, 1)
        SC.w(sTE)
        SC.op(lambda e, fc=fc: e.copy(LTS[0:16, fc * P:(fc + 1) * P], bank(1, 0, 128)[0:16, :]), sSC, 1)
    VE.w(sSC)
    lt3v[0] = emit_lt()

    TE.w(sVE, lt3v[0]["ve"])
    TE.w(sGD, lt3v[0]["gd"])
    TE.w(sSC, lt3v[0]["sc"])
    state["wg"] = {0: sTE.n, 1: sTE.n}
    SY.w(sVE, lt3v[0]["ve"])
    SY.w(sGD, lt3v[0]["gd"])
    VE.w(sTE)
    emit_ffn(2, final_store=True)

    SY.w(sST, sST.n)
    GP.w(sST, sST.n)
    _finish()
    n_ops = sum(len(s.ops) for s in [TE, VE, SC, GP, SY])
    return nc, n_ops


def _host_constants():
    i = np.arange(P)
    im = (i % 64).astype(np.float32)
    x, y = im[:, None], im[None, :]
    blkm = (i[:, None] // 64) == (i[None, :] // 64)
    return dict(
        dijs=np.where(blkm, x - y, 0.0).astype(np.float32),
        bdstrictneg=np.where(blkm & (x > y), -1.0, 0.0).astype(np.float32),
        bdinclT=np.where(blkm & (x <= y), 1.0, 0.0).astype(np.float32),
        njcol=(-(im) - 1.0).astype(np.float32)[:, None],
        iprow=(np.arange(C) + 1.0).astype(np.float32)[None, :],
        dwcol=(63.0 - im).astype(np.float32)[:, None],
        gpcol=im.astype(np.float32)[:, None],
    )


_NC_CACHE = {}


def kernel(L0_attn_out, L0_attn_W_write, L0_attn_W_read, L0_attn_W_beta,
           L0_attn_decay, L0_ffn_W1, L0_ffn_b1, L0_ffn_W2, L0_ffn_b2,
           L0_ffn_scale):
    B, T, Dm = L0_attn_out.shape
    f32 = np.float32
    x0 = np.ascontiguousarray(L0_attn_out, f32)
    consts = _host_constants()
    w1t = np.ascontiguousarray(L0_ffn_W1.T, f32)
    w2t = np.ascontiguousarray(L0_ffn_W2.T, f32)
    wwt = np.ascontiguousarray(L0_attn_W_write.T, f32)
    wrt = np.ascontiguousarray(L0_attn_W_read.T, f32)
    wbt = np.ascontiguousarray(L0_attn_W_beta.T, f32)
    b1c = np.ascontiguousarray(np.asarray(L0_ffn_b1, f32).reshape(NHC, P).T, f32)
    b2c = np.ascontiguousarray(np.asarray(L0_ffn_b2, f32).reshape(FC, P).T, f32)
    scl = np.asarray(L0_ffn_scale, f32).reshape(1, 1)
    dec = np.asarray(L0_attn_decay, f32).reshape(1, H)

    key = "nc"
    if key not in _NC_CACHE:
        _NC_CACHE[key] = build_nc()
    nc, n_ops = _NC_CACHE[key]

    in_maps = []
    for c in range(8):
        b, j = c // 2, c % 2
        t0 = j * TOK
        sm = np.zeros((1, 8), f32)
        sm2 = np.zeros((1, 8), f32)
        sm2[0, 2 * b] = 1.0
        if j == 1:
            sm[0, 2 * b] = 1.0
        m = dict(
            x0=np.ascontiguousarray(x0[b, t0:t0 + TOK, :].T),
            ltsrc=np.ascontiguousarray(x0[b, 1:17, :]),
            w1t=w1t, w2t=w2t, wwt=wwt, wrt=wrt, wbt=wbt,
            b1c=b1c, b2c=b2c, scl=scl, decay=dec,
            smask=sm, smask2=sm2, **consts,
        )
        in_maps.append(m)

    res = run_bass_kernel_spmd(nc, in_maps, core_ids=list(range(8)))
    out = np.empty((B, T, Dm), f32)
    for c in range(8):
        b, j = c // 2, c % 2
        t0 = j * TOK
        out[b, t0:t0 + TOK, :] = res.results[c]["out"].T
    return out


# version 10
# speedup vs baseline: 1.0127x; 1.0127x over previous
import os
import numpy as np
import ml_dtypes
import concourse.bass as bass
import concourse.mybir as mybir
from concourse.bass_utils import run_bass_kernel_spmd

F32 = mybir.dt.float32
BF16 = mybir.dt.bfloat16
AF = mybir.ActivationFunctionType
AL = mybir.AluOpType
AX = mybir.AxisListType.X

P, D, FC, HH, NHC = 128, 1024, 8, 4096, 32
TOK, NTB, TT, NTT = 2048, 16, 512, 4
H, DH, C = 8, 128, 64
DBG = set(s for s in os.environ.get("KDBG", "").split(",") if s)
PHASE = int(os.environ.get("KPHASE", "9"))
KSUB = os.environ.get("KSUB", "full")
KWIN = os.environ.get("KWIN", "full")


class Sem:
    def __init__(self, cm, name):
        self.h = cm.__enter__()
        self.n = 0
        self.name = name


class Stream:
    def __init__(self, name, auto=None):
        self.name = name
        self.auto = auto
        self.ops = []
        self.waited = {}

    def w(self, sem, val=None):
        val = sem.n if val is None else val
        if not val or val <= 0 or self.waited.get(sem.name, -1) >= val:
            return
        self.waited[sem.name] = val
        self.ops.append(lambda e, s=sem, v=val: e.wait_ge(s.h, v))

    def op(self, f, inc=None, k=1):
        if self.auto is not None:
            self.w(self.auto, self.auto.n)
            if inc is None:
                inc, k = self.auto, 1
        if inc is None:
            self.ops.append(f)
            return None
        inc.n += k
        self.ops.append(lambda e, f=f, s=inc, k=k: f(e).then_inc(s.h, k))
        return inc.n


def _r(ap, shape):
    if len(shape) == 1:
        return ap
    pat = "p (" + " ".join(f"d{i}" for i in range(len(shape))) + ") -> p " + \
          " ".join(f"d{i}" for i in range(len(shape)))
    return ap.rearrange(pat, **{f"d{i}": s for i, s in enumerate(shape)})


def build_nc(dbg=DBG):
    nc = bass.Bass("TRN2", target_bir_lowering=False)
    ctxs = []

    def ent(cm):
        ctxs.append(cm)
        return cm.__enter__()

    sbuf = lambda n, s, d: ent(nc.sbuf_tensor(n, s, d))
    psumt = lambda n, s, d: ent(nc.psum_tensor(n, s, d))
    sem = lambda n: Sem(nc.semaphore(n), n)

    dram = {}
    for name, shape in [
        ("x0", (D, TOK)), ("ltsrc", (16, D)),
        ("w1t", (D, HH)), ("w2t", (HH, D)), ("wwt", (D, D)), ("wrt", (D, D)),
        ("wbt", (D, H)), ("b1c", (P, NHC)), ("b2c", (P, FC)), ("scl", (1, 1)),
        ("decay", (1, H)), ("dijs", (P, P)), ("bdstrictneg", (P, P)),
        ("bdinclT", (P, P)), ("njcol", (P, 1)), ("iprow", (1, C)),
        ("dwcol", (P, 1)), ("gpcol", (P, 1)), ("smask", (1, 8)),
        ("smask2", (1, 8)),
    ]:
        dram[name] = nc.dram_tensor(name, list(shape), F32, kind="ExternalInput").ap()
    OUT = nc.dram_tensor("out", [D, TOK], F32, kind="ExternalOutput").ap()

    ltd = nc.dram_tensor("ltd", [16, D], F32).ap()
    pg_in = nc.dram_tensor("pg_in", [P, FC], F32)
    pg_out = nc.dram_tensor("pg_out", [8, P, FC], F32)
    sg_in = nc.dram_tensor("sg_in", [P, H * DH], F32)
    sg_out = nc.dram_tensor("sg_out", [8, P, H * DH], F32)
    lg_in = nc.dram_tensor("lg_in", [P, FC * 16], F32)
    lg_out = nc.dram_tensor("lg_out", [8, P, FC * 16], F32)
    xspill = nc.dram_tensor("xspill", [NTB, P, H * 256], BF16).ap()
    wspill = nc.dram_tensor("wspill", [NTB, P, H * DH], BF16).ap()
    aspill = nc.dram_tensor("aspill", [NTB, P, H * DH], BF16).ap()
    dspill = nc.dram_tensor("dspill", [NTB, P, H * DH], BF16).ap()
    w1b = nc.dram_tensor("w1b", [D, HH], BF16).ap()
    w2b = nc.dram_tensor("w2b", [HH, D], BF16).ap()

    XR = sbuf("XR", [P, FC, TOK], F32)
    YN = sbuf("YN", [P, H, TOK + 1], BF16)
    BETA = sbuf("BETA", [P, NTB, H], F32)
    BGAM = sbuf("BGAM", [P, H], F32)
    SF = sbuf("SF", [P, H, DH], F32)
    SBT = sbuf("SBT", [P, H, DH], BF16)
    VNS = sbuf("VNS", [P, H * DH], BF16)
    IDC = sbuf("IDC", [P, P], BF16)
    MST = sbuf("MST", [P, H, P], BF16)
    MIT = sbuf("MIT", [P, H, P], BF16)
    DPOS = sbuf("DPOS", [P, H, C], BF16)
    COLS = sbuf("COLS", [P, 64], F32)
    GPC, DWC, CTC, NGC, LGC, SMCo, SCLCo, SCLHo, PRSB = 0, 8, 16, 24, 32, 40, 56, 57, 58
    B1C = sbuf("B1Cs", [P, NHC], F32)
    B2CH = sbuf("B2CHs", [P, FC], F32)
    ROWF = sbuf("ROWF", [P, 2048], F32)
    ROWB = sbuf("ROWB", [P, TT], BF16)
    ONE128B = sbuf("ONE128B", [P, 1], BF16)
    ONESFW = sbuf("ONESFW", [P, P], F32)
    CONSTV = sbuf("CONSTV", [P, 4], F32)
    E2TMP = ROWB[0:4, 256:512]
    LN2A1 = ROWF[32:33, 0:TOK]
    LN2B = ROWF[64:65, 0:TOK]
    LTT = sbuf("LTT", [P, FC, 16], BF16)
    E1P = sbuf("E1P", [4, NHC, P], BF16)
    O2P = sbuf("O2P", [P, NHC, 16], BF16)
    E2P = sbuf("E2P", [16, FC, P], BF16)
    PVT = sbuf("PVT", [4, TT], BF16)
    GVT = sbuf("GVT", [16, TT], BF16)
    QTI = sbuf("QTI", [P, 8], BF16)
    PRF = sbuf("PRF", [P, FC], F32)
    PRB = sbuf("PRB", [P, FC], BF16)
    PR8 = sbuf("PR8", [P, FC], F32)
    CBUF = {}
    for nm in ["njcol", "iprow", "dwcol", "gpcol", "smask", "smask2",
               "b1c", "b2c", "scl", "decay"]:
        CBUF[nm] = sbuf("c_" + nm, list(dram[nm].shape), F32)

    ARB = sbuf("ARB", [P, 28672], BF16)
    ARF = sbuf("ARF", [P, 4608], F32)

    def _arena(T, off, shape):
        n = int(np.prod(shape[1:]))
        ap = T[0:shape[0], off:off + n]
        if len(shape) > 2:
            pat = "p (" + " ".join(f"d{i}" for i in range(len(shape) - 1)) + ") -> p " + \
                  " ".join(f"d{i}" for i in range(len(shape) - 1))
            ap = ap.rearrange(pat, **{f"d{i}": s for i, s in enumerate(shape[1:])})
        return ap
    arb = lambda off, shape: _arena(ARB, off, shape)
    arf = lambda off, shape: _arena(ARF, off, shape)

    PS = psumt("PS", [P, 4096], F32)
    bank = lambda i, off=0, n=512: PS[:, i * 512 + off:i * 512 + off + n]

    sSY, sGD, sTE, sVE, sSC, sGP, sCC, sST = (sem(x) for x in
        ["sSY", "sGD", "sTE", "sVE", "sSC", "sGP", "sCC", "sST"])
    sX0 = sem("sX0")
    sWW = sem("sWW")
    sWW1 = sem("sWW1")
    sWS = [sem("sWS0"), sem("sWS1")]
    sRL = [sem("sRL0"), sem("sRL1")]
    TE = Stream("te")
    VE = Stream("ve", auto=sVE)
    SC = Stream("sc", auto=sSC)
    GP = Stream("gp", auto=sGP)
    SY = Stream("sy")
    taps = {"n": 0}

    def _finish():
        blk = ent(nc.Block())
        @blk.sync
        def _(e):
            for f in SY.ops:
                f(e)
        @blk.tensor
        def _(e):
            for f in TE.ops:
                f(e)
        @blk.vector
        def _(e):
            for f in VE.ops:
                f(e)
        @blk.scalar
        def _(e):
            for f in SC.ops:
                f(e)
        @blk.gpsimd
        def _(e):
            for f in GP.ops:
                f(e)
        for cm in reversed(ctxs):
            cm.__exit__(None, None, None)

    def tap(name, src_fn, shape, dt=F32):
        if name not in dbg:
            return
        t = nc.dram_tensor("dbg_" + name, list(shape), dt, kind="ExternalOutput").ap()
        GP.w(sVE)
        GP.w(sSC)
        GP.w(sTE)
        GP.op(lambda e, t=t, f=src_fn: e.dma_start(out=t, in_=f()), sST, 16)
        taps["n"] += 1

    ONESF = ROWF[0:1, 0:P]
    ONESB = ROWB[0:1, 0:P]
    col = lambda o, k=1: COLS[:, o:o + k]

    SY.op(lambda e: e.dma_start(out=XR[:, :, :], in_=dram["x0"].rearrange("(c p) t -> p c t", p=P)), sX0, 16)
    v_x0 = sX0.n
    for nm in CBUF:
        GP.op(lambda e, nm=nm: e.dma_start(out=CBUF[nm][:, :], in_=dram[nm]), sGD, 16)
    CBUF["dijs"] = arf(256, (P, P))
    CBUF["bdstrictneg"] = arf(384, (P, P))
    CBUF["bdinclT"] = arf(512, (P, P))
    for nm in ["dijs", "bdstrictneg", "bdinclT"]:
        GP.op(lambda e, nm=nm: e.dma_start(out=CBUF[nm], in_=dram[nm]), sGD, 16)
    v_consts = sGD.n

    GP.w(sGD, v_consts)
    GP.op(lambda e: e.memset(IDC[:, :], 0.0), sGP, 1)
    GP.w(sGP)
    def _init(e):
        e.affine_select(out=IDC[:, :], in_=IDC[:, :], compare_op=AL.not_equal,
                        fill=1.0, base=0, pattern=[[-1, P]], channel_multiplier=1)
        e.memset(ROWF[0:1, 0:P], 1.0)
        e.memset(ONESFW[:, :], 1.0)
        e.memset(CONSTV[:, 0:1], 1e-5)
        e.memset(CONSTV[:, 1:2], -0.5)
        e.memset(CONSTV[:, 2:3], 64.0)
        e.memset(CONSTV[:, 3:4], 0.5)
        e.memset(ROWB[0:1, 0:P], 1.0)
        e.memset(ONE128B[:, :], 1.0)
        return e.tensor_copy(B1C[:, :], CBUF["b1c"][:, :])
    GP.op(_init, sGP, 1)
    v_init = sGP.n

    SC.w(sGD, v_consts)
    SC.op(lambda e: e.activation(ROWF[0:1, 128:128 + H], CBUF["decay"][0:1, :], AF.Sigmoid))
    SC.op(lambda e: e.activation(ROWF[0:1, 128:128 + H], ROWF[0:1, 128:128 + H], AF.Ln), sSC, 1)
    TE.w(sSC)
    TE.w(sGP, v_init)
    def _bc0(e):
        e.matmul(bank(0, 0, 8), ONESF, ROWF[0:1, 128:128 + H], start=True, stop=True)
        e.matmul(bank(0, 8, 8), ONESF, CBUF["smask"][0:1, :], start=True, stop=True)
        e.matmul(bank(0, 16, 8), ONESF, CBUF["smask2"][0:1, :], start=True, stop=True)
        return e.matmul(bank(0, 24, 1), ONESF, CBUF["scl"][0:1, :], start=True, stop=True)
    TE.op(_bc0, sTE, 1)
    SC.w(sTE)
    SC.op(lambda e: e.copy(col(LGC, 8), bank(0, 0, 8)))
    SC.op(lambda e: e.copy(col(SMCo, 8), bank(0, 8, 8)))
    SC.op(lambda e: e.copy(col(48, 8), bank(0, 16, 8)))
    SC.op(lambda e: e.copy(col(SCLCo), bank(0, 24, 1)), sSC, 1)
    SC.op(lambda e: e.mul(col(SCLHo), col(SCLCo), CONSTV[:, 3:4]))
    SC.op(lambda e: e.activation(col(CTC, 8), col(LGC, 8), AF.Exp, scale=CONSTV[:, 2:3]))
    SC.op(lambda e: e.activation(col(NGC, 8), col(LGC, 8), AF.Exp))
    for h in range(H):
        SC.op(lambda e, h=h: e.activation(col(GPC + h), CBUF["gpcol"][:, :], AF.Exp, scale=col(LGC + h)))
        SC.op(lambda e, h=h: e.activation(col(DWC + h), CBUF["dwcol"][:, :], AF.Exp, scale=col(LGC + h)))
    VE.w(sSC)
    VE.op(lambda e: e.tensor_scalar_mul(col(NGC, 8), col(NGC, 8), -1.0))
    VE.w(sGD, v_consts)
    VE.op(lambda e: e.tensor_scalar_mul(B2CH[:, :], CBUF["b2c"][:, :], 0.5), sVE, 1)

    MTMP = arf(0, (P, P))
    NJX = arf(128, (P, 1))
    for h in range(H):
        SC.w(sVE)
        SC.op(lambda e, h=h: e.activation(MTMP, CBUF["dijs"], AF.Exp, scale=col(LGC + h)))
        SC.op(lambda e, h=h: e.activation(NJX, CBUF["njcol"][:, :], AF.Exp, scale=col(LGC + h)))
        SC.op(lambda e, h=h: e.activation(ROWF[0:1, 192:192 + C], CBUF["iprow"][0:1, :], AF.Exp,
                                          scale=COLS[0:1, LGC + h:LGC + h + 1]), sSC, 1)
        VE.w(sSC)
        VE.op(lambda e, h=h: e.tensor_mul(MST[:, h, :], MTMP, CBUF["bdstrictneg"]))
        VE.op(lambda e, h=h: e.tensor_scalar(MIT[:, h, :], CBUF["bdinclT"], NJX[:, 0:1], None, AL.mult))
        VE.op(lambda e, h=h: e.tensor_copy(ROWB[0:1, 128:128 + C], ROWF[0:1, 192:192 + C]), sVE, 1)
        TE.w(sVE)
        TE.op(lambda e, h=h: e.matmul(bank(1, 0, C), ONESB, ROWB[0:1, 128:128 + C], start=True, stop=True), sTE, 1)
        SC.w(sTE)
        SC.op(lambda e, h=h: e.copy(DPOS[:, h, :], bank(1, 0, C)), sSC, 1)
    v_masks = sSC.n

    LTS = ARF[0:16, 1024:1024 + D]
    LTSQ = ARF[0:16, 2048:2048 + D]
    LTM = ARF[0:16, 3072:3072 + 8]
    AB_LTB = 22528

    def emit_lt():
        VE.op(lambda e: e.reduce_sum(LTM[:, 0:1], LTS, axis=AX))
        VE.op(lambda e: e.tensor_scalar_mul(LTM[:, 0:1], LTM[:, 0:1], 1.0 / D), sVE, 1)
        SC.w(sVE)
        SC.op(lambda e: e.activation(LTSQ, LTS, AF.Square), sSC, 1)
        VE.w(sSC)
        VE.op(lambda e: e.reduce_sum(LTM[:, 1:2], LTSQ, axis=AX))
        VE.op(lambda e: e.tensor_scalar_mul(LTM[:, 1:2], LTM[:, 1:2], 1.0 / D))
        VE.op(lambda e: e.tensor_mul(LTM[:, 2:3], LTM[:, 0:1], LTM[:, 0:1]))
        VE.op(lambda e: e.tensor_sub(LTM[:, 1:2], LTM[:, 1:2], LTM[:, 2:3]), sVE, 1)
        SC.w(sVE)
        SC.op(lambda e: e.activation(LTM[:, 2:3], LTM[:, 1:2], AF.Sqrt, bias=CONSTV[0:16, 0:1]), sSC, 1)
        VE.w(sSC)
        VE.op(lambda e: e.reciprocal(LTM[:, 3:4], LTM[:, 2:3]))
        VE.op(lambda e: e.tensor_scalar(LTS, LTS, LTM[:, 0:1],
                                        LTM[:, 3:4], AL.subtract, AL.mult), sVE, 1)
        v_ln = sVE.n
        GP.w(sVE, v_ln)
        GP.op(lambda e: e.dma_start(out=ltd, in_=LTS), sGD, 16)
        v_ltd = sGD.n
        LTB16 = ARB[0:16, AB_LTB:AB_LTB + D]
        VE.op(lambda e: e.tensor_copy(LTB16, LTS), sVE, 1)
        TE.w(sVE)
        for fc in range(FC):
            TE.w(sSC)
            TE.op(lambda e, fc=fc: e.matmul(bank(1, 0, 16), LTB16[:, fc * P:(fc + 1) * P],
                                            IDC[0:16, 0:16], start=True, stop=True), sTE, 1)
            SC.w(sTE)
            SC.op(lambda e, fc=fc: e.copy(LTT[:, fc, :], bank(1, 0, 16)), sSC, 1)
        v_ltt = sSC.n
        GP.w(sGD, v_ltd)
        GP.op(lambda e: e.memset(E1P[0:4, :, :], 0.0), sGP, 1)
        GP.op(lambda e: e.dma_start(out=LTSQ[0:4, :], in_=ltd[0:8:2, :]), sGD, 16)
        VE.w(sGD)
        VE.w(sGP)
        LTB4 = ARB[0:4, AB_LTB:AB_LTB + D]
        VE.w(sTE)
        VE.op(lambda e: e.tensor_copy(LTB4, LTSQ[0:4, :]), sVE, 1)
        GP.w(sVE)
        GP.w(sGP)
        for k in range(4):
            GP.op(lambda e, k=k: e.dma_start(
                out=E1P[k:k + 1, 8 * k:8 * k + 8, :].rearrange("a b c -> a (b c)"),
                in_=LTB4[k:k + 1, :]), sGD, 16)
        GP.w(sGD)
        GP.w(sVE)
        GP.op(lambda e: e.memset(O2P[:, :, :], 0.0), sGP, 1)
        VE.w(sGP)
        VE.w(sSC, v_ltt)
        for hc in range(NHC):
            tp0 = hc // 8
            VE.op(lambda e, hc=hc, tp0=tp0: e.tensor_copy(
                O2P[:, hc, tp0:tp0 + 13:4], LTT[:, hc % 8, 9:16:2]), sVE, 1)
        v_o2p = sVE.n
        GP.op(lambda e: e.memset(E2P[0:16, :, :], 0.0), sGP, 1)
        v_ez = sGP.n
        for k in range(4):
            GP.w(sGP, v_ez)
            GP.w(sSC)
            GP.w(sTE)
            def gq(e, k=k):
                src = ltd[8 + 2 * k:8 + 2 * k + 1, :].rearrange(
                    "a (qh q p) -> (a q) qh p", p=4, qh=2)
                return e.dma_start(out=_r(QTI[:, :], (2, 4)), in_=src)
            GP.op(gq, sGD, 16)
            TE.w(sGD)
            for qh in range(2):
                TE.op(lambda e, qh=qh: e.matmul(bank(1, qh * 128, 128)[0:4, :],
                                                _r(QTI[:, :], (2, 4))[:, qh, :],
                                                IDC[:, :], start=True, stop=True), sTE, 1)
            SC.w(sTE)
            SC.op(lambda e: e.copy(E2TMP, bank(1, 0, 256)[0:4, :]), sSC, 1)
            GP.w(sSC)
            def ge(e, k=k):
                return e.dma_start(
                    out=E2P.rearrange("a b c -> a (b c)")[4 * k:4 * k + 4, 256 * k:256 * k + 256],
                    in_=E2TMP)
            GP.op(ge, sGD, 16)
        return dict(ve=sVE.n, gd=sGD.n, sc=sSC.n)

    if PHASE < 1:
        GP.w(sGD, v_consts)
        GP.w(sGP)
        GP.w(sSC)
        GP.w(sVE)
        GP.w(sTE)
    GP.w(sGD, v_consts)
    GP.op(lambda e: e.dma_start(out=LTS, in_=dram["ltsrc"]), sGD, 16)
    VE.w(sGD)
    v_lt1 = emit_lt()

    AB_XB, AB_XSQ, AB_Y1, AB_H, AB_WG0, AB_WG1 = 0, 4096, 0, 8192, 24576, 26624
    AF_WS = [0, 2048]
    AF_T1 = 4096

    lns = {}

    def ln_ttile(src_slice, ydst, rows_a1b=None):
        XB = arb(AB_XB, (P, FC, TT))
        XSQ = arb(AB_XSQ, (P, FC, TT))
        for fc in range(FC):
            VE.op(lambda e, fc=fc: e.tensor_copy(XB[:, fc, :], src_slice(fc)), sVE, 1)
        v_cast = sVE.n
        SC.w(sVE, v_cast)
        SC.op(lambda e: e.activation(XSQ[:, :, :], XB[:, :, :], AF.Square), sSC, 1)
        v_sq = sSC.n
        TE.w(sVE, v_cast)
        TE.w(sSC, v_sq)
        def _stat(e):
            last = None
            for blk in range(4):
                for fc in range(FC):
                    last = e.matmul(PS[0:1, 3 * 512 + blk * 128:3 * 512 + (blk + 1) * 128],
                                    ONE128B[:, :], XB[:, fc, blk * 128:(blk + 1) * 128],
                                    start=(fc == 0), stop=(fc == FC - 1))
            for blk in range(4):
                for fc in range(FC):
                    last = e.matmul(PS[0:1, 4 * 512 + blk * 128:4 * 512 + (blk + 1) * 128],
                                    ONE128B[:, :], XSQ[:, fc, blk * 128:(blk + 1) * 128],
                                    start=(fc == 0), stop=(fc == FC - 1))
            return last
        TE.op(_stat, sTE, 1)
        v_stat = sTE.n
        lns["stat"] = v_stat
        rm = ROWF[0:1, 1536:1536 + TT]
        rs = ROWF[0:1, 512:512 + TT]
        rb = ROWF[0:1, 1024:1024 + TT]
        VE.w(sTE, v_stat)
        VE.op(lambda e: e.tensor_scalar_mul(rm, bank(3)[0:1, :], 1.0 / D))
        VE.op(lambda e: e.tensor_mul(rb, rm, rm))
        VE.op(lambda e: e.scalar_tensor_tensor(rs, bank(4)[0:1, :], 1.0 / D, rb, AL.mult, AL.subtract), sVE, 1)
        SC.w(sVE)
        SC.op(lambda e: e.activation(rs, rs, AF.Ln, bias=CONSTV[0:1, 0:1]))
        SC.op(lambda e: e.activation(rs, rs, AF.Exp, scale=CONSTV[0:1, 1:2]), sSC, 1)
        VE.w(sSC)
        VE.op(lambda e: e.scalar_tensor_tensor(rb, rm, -1.0, rs, AL.mult, AL.mult), sVE, 1)
        if rows_a1b is not None:
            a1d, bd = rows_a1b
            VE.op(lambda e: e.tensor_scalar_add(a1d, rs, 1.0))
            VE.op(lambda e: e.tensor_copy(bd, rb), sVE, 1)
        TE.w(sVE, sVE.n)
        def _bcast(e):
            e.matmul(bank(4), ONESF, rs, start=True, stop=True)
            return e.matmul(bank(5), ONESF, rb, start=True, stop=True)
        TE.op(_bcast, sTE, 1)
        T1 = arf(AF_T1, (P, TT))
        VE.w(sTE)
        for fc in range(FC):
            VE.op(lambda e, fc=fc: e.tensor_mul(T1, src_slice(fc), bank(4)))
            VE.op(lambda e, fc=fc: e.tensor_add(ydst(fc), T1, bank(5)), sVE, 1)
        return sVE.n

    state = dict(gelu={}, wg={0: None, 1: None}, res={})

    def fire_bcastP():
        GP.w(sVE, state["res"][(1, 3)])
        GP.op(lambda e: e.dma_start(out=pg_in.ap().rearrange("p (c t) -> p c t", t=1),
                                    in_=XR[:, :, 2047:2048],
                                    allow_slow_non_contiguous=True), sGD, 16)
        GP.w(sGD)
        GP.op(lambda e: e.collective_compute(
            "AllGather", AL.bypass, replica_groups=[list(range(8))],
            ins=[pg_in.ap()], outs=[pg_out.ap()]), sCC, 1)

    def emit_ffn(callid, final_store=False):
        Y1 = arb(AB_Y1, (P, FC, TT))
        HT = arb(AB_H, (P, NHC, TT))
        WG = [arb(AB_WG0, (P, FC, 2, P)), arb(AB_WG1, (P, FC, 2, P))]
        WG2 = [arb(AB_WG0, (P, 2, FC, P)), arb(AB_WG1, (P, 2, FC, P))]
        WGF = [AB_WG0, AB_WG1]
        tts = [3, 0, 1, 2] if callid == 1 else list(range(NTT))
        for ti, tt in enumerate(tts):
            ts = tt * TT
            src = lambda fc, ts=ts: XR[:, fc, ts:ts + TT]
            if callid == 1 and ti == 0:
                TE.w(sX0, v_x0)
                VE.w(sX0, v_x0)
            TE.w(sVE, state["res"].get((callid, tts[ti - 1])) if ti else None)
            v_y1 = ln_ttile(src, lambda fc: Y1[:, fc, :])
            TE.w(sVE, v_y1)
            def _l1(e):
                last = None
                for fc in range(FC):
                    last = e.matmul(PS[0:4, 2 * 512:2 * 512 + 512], LTT[:, fc, 1:8:2],
                                    Y1[:, fc, :], start=(fc == 0), stop=(fc == FC - 1))
                return last
            TE.op(_l1, sTE, 1)
            SC.w(sTE)
            SC.op(lambda e: e.mul(PVT[0:4, :], PS[0:4, 2 * 512:2 * 512 + 512],
                                  col(SCLCo)[0:4, :]), sSC, 1)
            v_pv = sSC.n
            first = (callid == 1 and ti == 0)
            for g in range(16):
                bsel = g % 2
                Q = SY if bsel == 0 else GP
                sq_ = sWS[bsel]
                Q.w(sTE, state["wg"][bsel])
                if first:
                    if g >= 2 and g - 2 in state.get("w1c", {}):
                        Q.w(*state["w1c"][g - 2])
                    def _wld(e, g=g, bsel=bsel):
                        src = dram["w1t"].rearrange("(c p) n -> p c n", p=P)[:, :, g * 256:(g + 1) * 256]
                        return e.dma_start(out=_r(ARF[:, AF_WS[bsel]:AF_WS[bsel] + 2048], (FC, 256)), in_=src)
                    Q.op(_wld, sq_, 16)
                    SC.w(sq_)
                    SC.op(lambda e, bsel=bsel: e.copy(
                        ARB[:, WGF[bsel]:WGF[bsel] + 2048],
                        ARF[:, AF_WS[bsel]:AF_WS[bsel] + 2048]), sSC, 1)
                    v_wc = sSC.n
                    Q.w(sSC, v_wc)
                    Q.op(lambda e, g=g, bsel=bsel: e.dma_start(
                        out=w1b.rearrange("(c p) n -> p c n", p=P)[:, :, g * 256:(g + 1) * 256],
                        in_=_r(ARB[:, WGF[bsel]:WGF[bsel] + 2048], (FC, 256))), sq_, 16)
                    state.setdefault("w1c", {})[g] = (sq_, sq_.n)
                    Q.w(sq_)
                else:
                    for vv in state.get("wcfin", []):
                        Q.w(*vv)
                    Q.w(*state["w1c"][g])
                    def _wldc(e, g=g, bsel=bsel):
                        src = w1b.rearrange("(c p) n -> p c n", p=P)[:, :, g * 256:(g + 1) * 256]
                        return e.dma_start(out=_r(ARB[:, WGF[bsel]:WGF[bsel] + 2048], (FC, 256)), in_=src)
                    Q.op(_wldc, sq_, 16)
                    v_wc = None
                for hci in range(2):
                    hc = 2 * g + hci
                    pb = hc % 2
                    if first:
                        TE.w(sSC, v_wc)
                    else:
                        TE.w(sWS[bsel])
                    TE.w(sSC, state["gelu"].get((callid, tt, hc - 2)))
                    TE.w(sSC, v_pv)
                    def _h(e, hci=hci, hc=hc, pb=pb, bsel=bsel):
                        for fc in range(FC):
                            e.matmul(bank(pb), WG[bsel][:, fc, hci, :], Y1[:, fc, :],
                                     start=(fc == 0), stop=False)
                        return e.matmul(bank(pb), E1P[0:4, hc, :], PVT[0:4, :],
                                        start=False, stop=True)
                    TE.op(_h, sTE, 1)
                    SC.w(sTE)
                    state["gelu"][(callid, tt, hc)] = SC.op(
                        lambda e, hc=hc, pb=pb: e.activation(HT[:, hc, :], bank(pb), AF.Gelu,
                                                             bias=B1C[:, hc:hc + 1]), sSC, 1)
                state["wg"][bsel] = sTE.n
            TE.w(sSC, state["gelu"][(callid, tt, NHC - 1)])
            def _g2(e):
                last = None
                for hc in range(NHC):
                    last = e.matmul(PS[0:16, 2 * 512:2 * 512 + 512], O2P[:, hc, :],
                                    HT[:, hc, :], start=(hc == 0), stop=(hc == NHC - 1))
                return last
            TE.op(_g2, sTE, 1)
            SC.w(sTE)
            SC.op(lambda e: e.mul(GVT[0:16, :], PS[0:16, 2 * 512:2 * 512 + 512],
                                  col(SCLHo)[0:16, :]), sSC, 1)
            v_gv = sSC.n
            for g in range(16):
                bsel = g % 2
                Q = SY if bsel == 0 else GP
                sq_ = sWS[bsel]
                Q.w(sTE, state["wg"][bsel])
                if first:
                    if g >= 2 and g - 2 in state.get("w2c", {}):
                        Q.w(*state["w2c"][g - 2])
                    def _w2ld(e, g=g, bsel=bsel):
                        src = dram["w2t"].rearrange("(hc p) n -> p hc n", p=P)[:, 2 * g:2 * g + 2, :]
                        return e.dma_start(out=_r(ARF[:, AF_WS[bsel]:AF_WS[bsel] + 2048], (2, D)), in_=src)
                    Q.op(_w2ld, sq_, 16)
                    VE.w(sq_)
                    VE.op(lambda e, bsel=bsel: e.tensor_scalar(
                        ARB[:, WGF[bsel]:WGF[bsel] + 2048],
                        ARF[:, AF_WS[bsel]:AF_WS[bsel] + 2048],
                        0.5, None, AL.mult), sVE, 1)
                    v_wc2 = sVE.n
                    Q.w(sVE, v_wc2)
                    state[("w2cast", bsel)] = v_wc2
                    Q.op(lambda e, g=g, bsel=bsel: e.dma_start(
                        out=w2b.rearrange("(hc p) n -> p hc n", p=P)[:, 2 * g:2 * g + 2, :],
                        in_=_r(ARB[:, WGF[bsel]:WGF[bsel] + 2048], (2, D))), sq_, 16)
                    state.setdefault("w2c", {})[g] = (sq_, sq_.n)
                    Q.w(sq_)
                else:
                    for vv in state.get("wcfin", []):
                        Q.w(*vv)
                    Q.w(*state["w2c"][g])
                    def _w2ldc(e, g=g, bsel=bsel):
                        src = w2b.rearrange("(hc p) n -> p hc n", p=P)[:, 2 * g:2 * g + 2, :]
                        return e.dma_start(out=_r(ARB[:, WGF[bsel]:WGF[bsel] + 2048], (2, D)), in_=src)
                    Q.op(_w2ldc, sq_, 16)
                if first:
                    TE.w(sVE, v_wc2)
                else:
                    TE.w(sWS[bsel])
                if g == 0:
                    TE.w(sSC, v_gv)
                def _w2(e, g=g, bsel=bsel):
                    last = None
                    for hci in range(2):
                        hc = 2 * g + hci
                        for oc in range(FC):
                            last = e.matmul(bank(oc), WG2[bsel][:, hci, oc, :], HT[:, hc, :],
                                            start=(hc == 0), stop=False)
                    return last
                TE.op(_w2, sTE, 1)
                state["wg"][bsel] = sTE.n
            if first:
                state["wcfin"] = [state["w1c"][14], state["w1c"][15],
                                  state["w2c"][14], state["w2c"][15]]
            TE.w(sSC, v_gv)
            def _up(e):
                last = None
                for oc in range(FC):
                    last = e.matmul(bank(oc), E2P[0:16, oc, :], GVT[0:16, :],
                                    start=False, stop=True)
                return last
            TE.op(_up, sTE, 1)
            v_w2 = sTE.n
            VE.w(sTE, v_w2)
            for oc in range(FC):
                VE.op(lambda e, oc=oc, ts=ts: e.scalar_tensor_tensor(
                    XR[:, oc, ts:ts + TT], bank(oc), B2CH[:, oc:oc + 1],
                    XR[:, oc, ts:ts + TT], AL.add, AL.add), sVE, 1)
            state["res"][(callid, tt)] = sVE.n
            if callid == 1 and tt == 3:
                fire_bcastP()
            tap(f"x_f{callid}_t{tt}", lambda ts=ts: XR[:, :, ts:ts + TT], (P, FC, TT))
            if final_store:
                SY.w(sVE)
                SY.op(lambda e, ts=ts: e.dma_start(
                    out=OUT.rearrange("(c p) t -> p c t", p=P)[:, :, ts:ts + TT],
                    in_=XR[:, :, ts:ts + TT]), sST, 16)
        return

    SY.w(sVE, v_lt1["ve"])
    SY.w(sGD, v_lt1["gd"])
    if PHASE >= 1:
        emit_ffn(1)
    v_ffn1_ve = sVE.n

    def partial_store():
        for tt in range(NTT):
            ts = tt * TT
            SY.w(sVE)
            SY.w(sSC)
            SY.op(lambda e, ts=ts: e.dma_start(
                out=OUT.rearrange("(c p) t -> p c t", p=P)[:, :, ts:ts + TT],
                in_=XR[:, :, ts:ts + TT]), sST, 16)
        SY.w(sST, sST.n)
        GP.w(sST, sST.n)
        _finish()
        return nc, sum(len(s.ops) for s in [TE, VE, SC, GP, SY])

    if PHASE < 2:
        TE.w(sX0, v_x0)
        SY.w(sGD)
        SY.w(sGP)
        SY.w(sSC)
        SY.w(sVE)
        SY.w(sTE)
        return partial_store()


    for tt in range(NTT):
        ts = tt * TT
        TE.w(sVE, state["res"][(1, 2)])
        src = lambda fc, ts=ts: XR[:, fc, ts:ts + TT]
        ydst = lambda fc, ts=ts: YN[:, fc, 1 + ts:1 + ts + TT]
        ln_ttile(src, ydst, rows_a1b=(LN2A1[0:1, ts:ts + TT], LN2B[0:1, ts:ts + TT]))
    v_ln2 = sVE.n
    v_ln2stat = lns["stat"]

    tap("y2", lambda: YN[:, :, :], (P, H, TOK + 1), BF16)

    if PHASE < 3:
        return partial_store()

    AT_WWB, AT_V, AT_XW, AT_XWF = 0, 8192, 12288, 16384
    AT_WD, AT_ATW, AT_UT, AT_Y2S = 18432, 20480, 22528, 24064
    AT_WBB, AT_OW = 25096, AT_V
    WWB = arb(AT_WWB, (P, FC, D))
    WBB = arb(AT_WBB, (P, FC, H))
    wwcast = {0: None, 1: None}
    for piece in range(4):
        SY.w(sTE, v_ln2stat)
        bsel = piece % 2
        def _wwl(e, piece=piece, bsel=bsel):
            src = dram["wwt"].rearrange("(c p) n -> p c n", p=P)[:, :, piece * 256:(piece + 1) * 256]
            return e.dma_start(out=_r(ARF[:, AF_WS[bsel]:AF_WS[bsel] + 2048], (FC, 256)), in_=src)
        if wwcast[bsel] is not None:
            SY.w(sVE, wwcast[bsel])
        else:
            SY.w(sVE, state[("w2cast", bsel)])
        sww_ = sWW if bsel == 0 else sWW1
        SY.op(_wwl, sww_, 16)
        VE.w(sww_)
        VE.op(lambda e, piece=piece, bsel=bsel: e.tensor_copy(
            WWB[:, :, piece * 256:(piece + 1) * 256],
            _r(ARF[:, AF_WS[bsel]:AF_WS[bsel] + 2048], (FC, 256))), sVE, 1)
        wwcast[bsel] = sVE.n
    SY.w(sVE)
    SY.op(lambda e: e.dma_start(out=_r(ARF[:, 0:64], (FC, H)),
                                in_=dram["wbt"].rearrange("(c p) n -> p c n", p=P)), sWW, 16)
    VE.w(sWW)
    VE.op(lambda e: e.tensor_copy(WBB[:, :, :], _r(ARF[:, 0:64], (FC, H))), sVE, 1)
    v_ww = sVE.n
    PCH = arf(3088, (P, 8, FC))
    GP.w(sCC)
    GP.w(sWW, sWW.n)
    GP.w(sWW1, sWW1.n)
    GP.op(lambda e: e.dma_start(out=PCH, in_=pg_out.ap().rearrange("r p c -> p r c")), sGD, 16)
    VE.w(sCC)
    VE.w(sGD)
    for r in range(8):
        if r == 0:
            VE.op(lambda e, r=r: e.tensor_scalar(PRF[:, :], PCH[:, r, :], col(SMCo + r),
                                                 None, AL.mult), sVE, 1)
        else:
            VE.op(lambda e, r=r: e.scalar_tensor_tensor(PRF[:, :], PCH[:, r, :], col(SMCo + r),
                                                        PRF[:, :], AL.mult, AL.add), sVE, 1)
    VE.op(lambda e: e.tensor_copy(PRB[:, :], PRF[:, :]), sVE, 1)
    SC.w(sVE)
    SC.op(lambda e: e.activation(PR8[:, :], PRF[:, :], AF.Square), sSC, 1)
    VE.w(sSC)
    VE.op(lambda e: e.tensor_copy(QTI[:, :], PR8[:, :]), sVE, 1)
    TE.w(sVE)
    def _pst(e):
        e.matmul(bank(1, 0, 1)[0:8, :], PRB[:, :], ONE128B[:, :], start=True, stop=True)
        return e.matmul(bank(1, 1, 1)[0:8, :], QTI[:, :], ONE128B[:, :], start=True, stop=True)
    TE.op(_pst, sTE, 1)
    SC.w(sTE)
    SC.op(lambda e: e.copy(PR8[0:8, 0:2], bank(1, 0, 2)[0:8, :]), sSC, 1)
    VE.w(sSC)
    VE.op(lambda e: e.tensor_copy(QTI[0:8, 0:2], PR8[0:8, 0:2]), sVE, 1)
    TE.w(sVE)
    TE.op(lambda e: e.matmul(bank(1, 2, 2)[0:1, :], ONE128B[0:8, :], QTI[0:8, 0:2], start=True, stop=True), sTE, 1)
    SC.w(sTE)
    SC.op(lambda e: e.copy(ROWF[0:1, 136:138], bank(1, 2, 2)[0:1, :]), sSC, 1)
    VE.w(sSC)
    VE.op(lambda e: e.tensor_scalar_mul(ROWF[0:1, 136:138], ROWF[0:1, 136:138], 1.0 / D))
    VE.op(lambda e: e.tensor_mul(ROWF[0:1, 138:139], ROWF[0:1, 136:137], ROWF[0:1, 136:137]))
    VE.op(lambda e: e.tensor_sub(ROWF[0:1, 137:138], ROWF[0:1, 137:138], ROWF[0:1, 138:139]), sVE, 1)
    SC.w(sVE)
    SC.op(lambda e: e.activation(ROWF[0:1, 137:138], ROWF[0:1, 137:138], AF.Ln, bias=CONSTV[0:1, 0:1]))
    SC.op(lambda e: e.activation(ROWF[0:1, 137:138], ROWF[0:1, 137:138], AF.Exp, scale=CONSTV[0:1, 1:2]), sSC, 1)
    VE.w(sSC)
    VE.op(lambda e: e.scalar_tensor_tensor(ROWF[0:1, 138:139], ROWF[0:1, 136:137], -1.0,
                                           ROWF[0:1, 137:138], AL.mult, AL.mult), sVE, 1)
    TE.w(sVE)
    TE.op(lambda e: e.matmul(bank(1, 8, 2), ONESF, ROWF[0:1, 137:139], start=True, stop=True), sTE, 1)
    SC.w(sTE)
    SC.op(lambda e: e.copy(col(PRSB, 2), bank(1, 8, 2)), sSC, 1)
    VE.w(sSC)
    VE.op(lambda e: e.tensor_scalar(PR8[:, :], PRF[:, :], col(PRSB), col(PRSB + 1),
                                    AL.mult, AL.add), sVE, 1)
    SC.w(sVE)
    SC.op(lambda e: e.activation(PRF[:, :], PR8[:, :], AF.Square), sSC, 1)
    VE.w(sSC)
    VE.op(lambda e: e.tensor_copy(QTI[:, :], PRF[:, :]), sVE, 1)
    TE.w(sVE)
    TE.op(lambda e: e.matmul(bank(1, 16, 1)[0:8, :], QTI[:, :], ONE128B[:, :], start=True, stop=True), sTE, 1)
    SC.w(sTE)
    SC.op(lambda e: e.copy(PRF[0:8, 0:1], bank(1, 16, 1)[0:8, :]))
    SC.op(lambda e: e.activation(PRF[0:8, 0:1], PRF[0:8, 0:1], AF.Sqrt), sSC, 1)
    VE.w(sSC)
    VE.op(lambda e: e.tensor_scalar_max(PRF[0:8, 0:1], PRF[0:8, 0:1], 1e-12))
    VE.op(lambda e: e.reciprocal(PRF[0:8, 0:1], PRF[0:8, 0:1]))
    VE.op(lambda e: e.tensor_copy(QTI[0:8, 0:1], PRF[0:8, 0:1]), sVE, 1)
    TE.w(sVE)
    TE.op(lambda e: e.matmul(bank(1, 24, 8)[0:1, :], QTI[0:8, 0:1], IDC[0:8, 0:8], start=True, stop=True), sTE, 1)
    SC.w(sTE)
    SC.op(lambda e: e.copy(ROWF[0:1, 144:152], bank(1, 24, 8)[0:1, :]), sSC, 1)
    TE.w(sSC)
    TE.op(lambda e: e.matmul(bank(1, 32, 8), ONESF, ROWF[0:1, 144:152], start=True, stop=True), sTE, 1)
    VE.w(sTE)
    VE.op(lambda e: e.tensor_mul(PRF[:, :], PR8[:, :], bank(1, 32, 8)))
    VE.op(lambda e: e.tensor_copy(YN[:, :, 0], PRF[:, :]), sVE, 1)
    v_ynprev = sVE.n
    VE.op(lambda e: e.memset(SF[:, :, :], 0.0))
    VE.op(lambda e: e.memset(SBT[:, :, :], 0.0))
    VE.op(lambda e: e.memset(bank(2), 0.0))
    VE.op(lambda e: e.memset(bank(6), 0.0), sVE, 1)
    v_sz = sVE.n

    Y2S = arb(AT_Y2S, (P, FC, 129))
    AT_UT2 = 25160
    UTS = [[arb(AT_UT, (P, 2, P)), arb(AT_UT + 256, (P, 2, P))],
           [arb(AT_UT2, (P, 2, P)), arb(AT_UT2 + 256, (P, 2, P))]]
    VPD = [arb(AT_UT + 512, (P, 2, 256)), arb(AT_UT2 + 512, (P, 2, 256))]

    def UTPn(p, j):
        return UTS[p][j % 2][:, 0, :]

    def UTPc(p, j):
        return UTS[p][j % 2][:, 1, :]

    def UTPx(p, j):
        return UTS[p][j % 2]
    Vw = lambda tb: arb(AT_V + (tb % 2) * 2048, (P, H, 256))
    XWw = lambda tb: arb(AT_XW + (tb % 2) * 2048, (P, H, 256))
    XWFw = lambda tb: arb(AT_XWF + (tb % 2) * 1024, (P, H, DH))
    WDw = lambda tb: arb(AT_WD + (tb % 2) * 1024, (P, H, DH))
    ATWw = lambda tb: arb(AT_ATW + (tb % 2) * 1024, (P, H, DH))
    OWw = arb(AT_OW, (P, H, TT))

    scan1_done = {}
    spill_done = {}
    win_prev = {}

    def emit_window(tb, spill=True):
        t0 = tb * 128
        if tb >= 2:
            TE.w(sVE, scan1_done.get(tb - 2))
            TE.w(sGD, spill_done.get(tb - 2))
        if win_prev:
            TE.w(sVE, win_prev["ve"])
            TE.w(sSC, win_prev["sc"])
        else:
            TE.w(sVE)
            TE.w(sSC)
        TE.w(sVE, v_ww)
        if tb == 0:
            TE.w(sVE, v_sz)
            TE.w(sVE, v_ln2)
        SC.w(sTE, win_prev.get("q"))
        SC.op(lambda e, t0=t0: e.activation(Y2S[:, :, 1:129], YN[:, :, 1 + t0:1 + t0 + 128],
                                            AF.Square), sSC, 1)
        v_y2s = sSC.n
        def _vp(e, t0=t0):
            last = None
            for half in range(2):
                for fc in range(FC):
                    last = e.matmul(bank(half), YN[:, fc, 1 + t0:1 + t0 + 128],
                                    WWB[:, fc, half * 512:(half + 1) * 512],
                                    start=(fc == 0), stop=(fc == FC - 1))
            for fc in range(FC):
                last = e.matmul(bank(3, 0, 8), YN[:, fc, 1 + t0:1 + t0 + 128],
                                WBB[:, fc, :], start=(fc == 0), stop=(fc == FC - 1))
            return last
        TE.op(_vp, sTE, 1)
        v_vp = sTE.n
        TE.w(sSC, v_y2s)
        def _q(e):
            last = None
            for h in range(H):
                last = e.matmul(bank(3, 128 + h, 1), Y2S[:, h, 1:129], ONE128B[:, :],
                                start=True, stop=True)
            return last
        TE.op(_q, sTE, 1)
        win_prev["q"] = sTE.n
        SC.w(sTE, v_vp)
        SC.op(lambda e, tb=tb: e.activation(BETA[:, tb, :], bank(3, 0, 8), AF.Sigmoid), sSC, 1)
        v_beta = sSC.n
        SC.w(sTE)
        SC.op(lambda e: e.copy(PR8[:, 0:8], bank(3, 128, 8)))
        SC.op(lambda e: e.activation(PR8[:, 0:8], PR8[:, 0:8], AF.Sqrt), sSC, 1)
        v_sqr = sSC.n
        VE.w(sSC, v_beta)
        VE.op(lambda e, tb=tb: e.tensor_mul(BGAM[:, :], BETA[:, tb, :], col(GPC, 8)), sVE, 1)
        VE.w(sSC, v_sqr)
        VE.op(lambda e: e.tensor_scalar_max(PR8[:, 0:8], PR8[:, 0:8], 1e-12))
        VE.op(lambda e: e.reciprocal(PR8[:, 0:8], PR8[:, 0:8]))
        VE.op(lambda e: e.tensor_copy(QTI[:, :], PR8[:, 0:8]), sVE, 1)
        v_rn = sVE.n
        for h in range(H):
            SC.op(lambda e, h=h, tb=tb: e.mul(Vw(tb)[:, h, 0:128],
                                              bank(h // 4, (h % 4) * 128, 128),
                                              BETA[:, tb, h:h + 1]), sSC, 1)
        v_vev = sSC.n
        if KWIN == "vp" or KWIN == "q":
            return dict(ve=sVE.n, sc=sSC.n)
        TE.w(sVE, v_rn)
        def _rr(e):
            last = None
            for h in range(H):
                b, o = (2, h * 128) if h < 4 else (3, (h - 4) * 128)
                last = e.matmul(PS[0:1, b * 512 + o:b * 512 + o + 128],
                                QTI[:, h:h + 1], IDC[:, :], start=True, stop=True)
            return last
        TE.op(_rr, sTE, 1)
        SC.w(sTE)
        SC.op(lambda e: e.copy(ROWF[0:1, 512:1024], bank(2)[0:1, :]))
        SC.op(lambda e: e.copy(ROWF[0:1, 1024:1536], bank(3)[0:1, :]), sSC, 1)
        v_rrow = sSC.n
        TE.w(sSC, v_rrow)
        def _rbc(e):
            e.matmul(bank(2), ONESF, ROWF[0:1, 512:1024], start=True, stop=True)
            return e.matmul(bank(3), ONESF, ROWF[0:1, 1024:1536], start=True, stop=True)
        TE.op(_rbc, sTE, 1)
        VE.w(sTE)
        VE.w(sSC, v_y2s)
        VE.op(lambda e, t0=t0: e.tensor_mul(YN[:, :, 1 + t0:1 + t0 + 128],
                                            YN[:, :, 1 + t0:1 + t0 + 128],
                                            PS[:, 2 * 512:2 * 512 + 1024]), sVE, 1)
        v_yn = sVE.n
        if KWIN == "yn":
            return dict(ve=sVE.n, sc=sSC.n)
        TE.w(sVE, v_yn)
        if tb == 0:
            TE.w(sVE, v_ynprev)
        def _wkt(e, t0=t0):
            last = None
            for h in range(H):
                dst = bank(2 + h // 4, (h % 4) * 128, 128)
                last = e.matmul(dst, YN[:, h, t0:t0 + 128], IDC[:, :],
                                start=True, stop=True)
            return last
        TE.op(_wkt, sTE, 1)
        if KWIN == "wkmm":
            return dict(ve=sVE.n, sc=sSC.n)
        VE.w(sTE)
        SC.w(sTE)
        for h in range(4):
            psrc = bank(2, h * 128, 128)
            VE.op(lambda e, h=h, tb=tb, psrc=psrc: e.tensor_scalar(
                Vw(tb)[:, h, 128:256], psrc, BGAM[:, h:h + 1], None, AL.mult), sVE, 1)
            VE.op(lambda e, h=h, tb=tb, psrc=psrc: e.tensor_scalar(
                WDw(tb)[:, h, :], psrc, col(DWC + h), None, AL.mult), sVE, 1)
        for h in range(4, H):
            psrc = bank(3, (h - 4) * 128, 128)
            SC.op(lambda e, h=h, tb=tb, psrc=psrc: e.mul(
                Vw(tb)[:, h, 128:256], psrc, BGAM[:, h:h + 1]), sSC, 1)
            SC.op(lambda e, h=h, tb=tb, psrc=psrc: e.mul(
                WDw(tb)[:, h, :], psrc, col(DWC + h)), sSC, 1)
        v_wk = sSC.n
        v_wk_ve = sVE.n
        if KWIN == "wk":
            return dict(ve=sVE.n, sc=sSC.n)
        if tb == 0:
            for vv in state.get("wcfin", []):
                VE.w(*vv)
                SC.w(*vv)
        ut_free = {0: [(sSC, v_vev)], 1: [(sVE, v_wk_ve)]}

        def EV(p):
            return (VE, sVE) if p == 0 else (SC, sSC)

        def emit_pair(h0):
            hs = (h0, h0 + 1)
            ev = {}

            def gaq(e, h, t0=t0):
                b0 = (h % 2) * 2 * 512
                return e.matmul(PS[:, b0:b0 + 128], YN[:, h, t0:t0 + 128],
                                YN[:, h, t0:t0 + 128], start=True, stop=True)

            for h in hs:
                p = h % 2
                for s_, v_ in ut_free[p]:
                    TE.w(s_, v_)
                TE.op(lambda e, h=h: gaq(e, h), sTE, 1)
                VE.w(sTE, sTE.n)
                VE.op(lambda e, h=h, p=p, tb=tb: e.scalar_tensor_tensor(
                    UTPn(p, 0), bank(2 * p, 0, 128), BETA[:, tb, h:h + 1],
                    MST[:, h, :], AL.mult, AL.mult), sVE, 1)
                ev[(h, "a")] = sVE.n
            for h in hs:
                p = h % 2
                E, sE = EV(p)
                TE.w(sVE, ev[(h, "a")])
                TE.op(lambda e, p=p: e.matmul(bank(2 * p, 128, 128), UTPn(p, 0),
                                              IDC[:, :], start=True, stop=True), sTE, 1)
                E.w(sTE, sTE.n)
                if p == 0:
                    E.op(lambda e: e.tensor_copy(UTPc(0, 0), bank(0, 128, 128)), sE, 1)
                else:
                    E.op(lambda e: e.copy(UTPc(1, 0), bank(2, 128, 128)), sE, 1)
                ev[(h, "at")] = sE.n
            cur = {h: 0 for h in hs}
            for j in range(6):
                for h in hs:
                    p = h % 2
                    E, sE = EV(p)
                    TE.w(sE, ev[(h, "at")] if j == 0 else ev[(h, "u", j - 1)])
                    def _app(e, h=h, p=p, j=j, c=cur[h], tb=tb):
                        rv = Vw(tb)[:, h, :] if j == 0 else VPD[p][:, c, :]
                        e.matmul(bank(2 * p + 1, 0, 256), IDC[:, :], rv,
                                 start=True, stop=False)
                        return e.matmul(bank(2 * p + 1, 0, 256), UTPc(p, j), rv,
                                        start=False, stop=True)
                    TE.op(_app, sTE, 1)
                    v_appm = sTE.n
                    if j < 5:
                        def _sq(e, p=p, j=j):
                            e.matmul(bank(2 * p, 256, 128), UTPc(p, j), UTPn(p, j),
                                     start=True, stop=True)
                            return e.matmul(bank(2 * p, 384, 128), UTPn(p, j), UTPc(p, j),
                                            start=True, stop=True)
                        TE.op(_sq, sTE, 1)
                        v_sqm = sTE.n
                    E.w(sTE, v_appm)
                    if j == 5:
                        if p == 0:
                            E.op(lambda e, h=h, tb=tb: e.tensor_copy(
                                XWw(tb)[:, h, :], bank(1, 0, 256)), sE, 1)
                        else:
                            E.op(lambda e, h=h, tb=tb: e.copy(
                                XWw(tb)[:, h, :], bank(3, 0, 256)), sE, 1)
                    else:
                        c = cur[h]
                        if p == 0:
                            E.op(lambda e, c=c: e.tensor_copy(
                                VPD[0][:, 1 - c, :], bank(1, 0, 256)), sE, 1)
                        else:
                            E.op(lambda e, c=c: e.copy(
                                VPD[1][:, 1 - c, :], bank(3, 0, 256)), sE, 1)
                        cur[h] = 1 - c
                        E.w(sTE, v_sqm)
                        psq = _r(PS[:, (2 * p) * 512 + 256:(2 * p) * 512 + 512], (2, P))
                        if p == 0:
                            E.op(lambda e, j=j, psq=psq: e.tensor_copy(
                                UTPx(0, j + 1)[:, :, :], psq), sE, 1)
                        else:
                            E.op(lambda e, j=j, psq=psq: e.copy(
                                UTPx(1, j + 1)[:, :, :], psq), sE, 1)
                    ev[(h, "u", j)] = sE.n
            vat = {}
            for h in hs:
                p = h % 2
                E, sE = EV(p)
                TE.w(sE, ev[(h, "u", 4)])
                def _at2(e, h=h, p=p, t0=t0):
                    b0 = 2 * p * 512
                    return e.matmul(PS[:, b0 + 256:b0 + 384], YN[:, h, t0:t0 + 128],
                                    YN[:, h, 1 + t0:1 + t0 + 128],
                                    start=True, stop=True)
                TE.op(_at2, sTE, 1)
                vat[h] = sTE.n
            for h in hs:
                p = h % 2
                VE.w(sTE, vat[h])
                VE.op(lambda e, h=h, p=p, tb=tb: e.tensor_mul(
                    ATWw(tb)[:, h, :], bank(2 * p, 256, 128), MIT[:, h, :]), sVE, 1)
                ev[(h, "att")] = sVE.n
            for h in hs:
                p = h % 2
                E, sE = EV(p)
                TE.w(sVE, ev[(h, "att")])
                TE.w(sE, ev[(h, "u", 5)])
                def _xwf(e, h=h, p=p, tb=tb):
                    return e.matmul(bank(2 * p, 384, 128), XWw(tb)[:, h, 128:256],
                                    IDC[:, :], start=True, stop=True)
                TE.op(_xwf, sTE, 1)
                E.w(sTE, sTE.n)
                if p == 0:
                    E.op(lambda e, h=h, tb=tb: e.tensor_scalar(
                        XWFw(tb)[:, h, :], bank(0, 384, 128), col(NGC + h),
                        None, AL.mult), sE, 1)
                else:
                    E.op(lambda e, h=h, tb=tb: e.mul(
                        XWFw(tb)[:, h, :], bank(2, 384, 128), col(NGC + h)), sE, 1)
                ut_free[p] = [(sE, sE.n)]

        for h0 in range(0, H, 2):
            emit_pair(h0)
        win_prev["ve"] = ut_free[0][0][1]
        win_prev["sc"] = ut_free[1][0][1]
        v_win = dict(ve=sVE.n, sc=sSC.n)
        if not spill:
            spill_done[tb] = sGD.n
            return v_win
        GP.w(sVE, v_win["ve"])
        GP.w(sSC, v_win["sc"])
        GP.op(lambda e, tb=tb: e.dma_start(out=xspill[tb], in_=XWw(tb).rearrange("p a b -> p (a b)")), sGD, 16)
        GP.op(lambda e, tb=tb: e.dma_start(out=wspill[tb], in_=XWFw(tb).rearrange("p a b -> p (a b)")), sGD, 16)
        GP.op(lambda e, tb=tb: e.dma_start(out=aspill[tb], in_=ATWw(tb).rearrange("p a b -> p (a b)")), sGD, 16)
        GP.op(lambda e, tb=tb: e.dma_start(out=dspill[tb], in_=WDw(tb).rearrange("p a b -> p (a b)")), sGD, 16)
        spill_done[tb] = sGD.n
        GP.w(sGD)
        return v_win

    def _sudst(h):
        return bank(5, h * 128, 128) if h < 4 else bank(7, (h - 4) * 128, 128)

    scanp = {"A": {}, "B": {}}

    def emit_scan_step(n, with_o, ow_slot=None, win=None):
        tb, c = n // 2, n % 2
        r0 = c * 64
        halves = (("A", 0, 4), ("B", 4, 6))
        pv = {X: dict(scanp[X]) for X, _, _ in halves}
        if win is not None:
            TE.w(sVE, win["ve"])
            TE.w(sSC, win["sc"])
        vne = {}
        vnv = {}
        for X, h0, bk in halves:
            if not pv[X]:
                TE.w(sSC)
                TE.w(sVE)
            else:
                TE.w(sSC, pv[X]["sbt"])
                TE.w(sVE, pv[X].get("oev"))
            def _vn(e, tb=tb, r0=r0, h0=h0, bk=bk):
                e.matmul(PS[r0:r0 + 64, bk * 512:bk * 512 + 512],
                         IDC[r0:r0 + 64, r0:r0 + 64],
                         XWw(tb)[r0:r0 + 64, h0:h0 + 4, 0:128],
                         start=True, stop=False)
                last = None
                for h in range(h0, h0 + 4):
                    dst = PS[r0:r0 + 64, bk * 512 + (h % 4) * 128:bk * 512 + (h % 4) * 128 + 128]
                    last = e.matmul(dst, XWFw(tb)[:, h, r0:r0 + 64], SBT[:, h, :],
                                    start=False, stop=(h % 4 == 3))
                return last
            TE.op(_vn, sTE, 1)
            vnv[X] = sTE.n
        for X, h0, bk in halves:
            SC.w(sTE, vnv[X])
            SC.op(lambda e, r0=r0, h0=h0, bk=bk: e.copy(
                _r(VNS[r0:r0 + 64, h0 * 128:h0 * 128 + 512], (4, 128)),
                PS[r0:r0 + 64, bk * 512:bk * 512 + 512]), sSC, 1)
            vne[X] = sSC.n
        VNSv = _r(VNS[:, :], (H, 128))
        ovv = {}
        suv = {}
        for X, h0, bk in halves:
            if with_o:
                TE.w(sSC, vne[X])
                def _o(e, tb=tb, n=n, r0=r0, h0=h0, bk=bk):
                    last = None
                    for h in range(h0, h0 + 4):
                        dst = bank(bk, (h - h0) * 64, 64)
                        e.matmul(dst, SBT[:, h, :], YN[:, h, 1 + n * 64:1 + n * 64 + 64],
                                 start=True, stop=False)
                        last = e.matmul(dst, VNSv[r0:r0 + 64, h, :],
                                        ATWw(tb)[r0:r0 + 64, h, r0:r0 + 64],
                                        start=False, stop=True)
                    return last
                TE.op(_o, sTE, 1)
                ovv[X] = sTE.n
            else:
                TE.w(sSC, vne[X])
            TE.w(sVE, pv[X].get("sfm"))
            def _su(e, tb=tb, r0=r0, h0=h0):
                last = None
                for h in range(h0, h0 + 4):
                    last = e.matmul(_sudst(h), WDw(tb)[r0:r0 + 64, h, :],
                                    VNSv[r0:r0 + 64, h, :], start=True, stop=True)
                return last
            TE.op(_su, sTE, 1)
            suv[X] = sTE.n
        for X, h0, bk in halves:
            oev = pv[X].get("oev")
            if with_o:
                VE.w(sTE, ovv[X])
                VE.op(lambda e, ow_slot=ow_slot, h0=h0, bk=bk: e.tensor_mul(
                    OWw[:, h0:h0 + 4, ow_slot * 64:ow_slot * 64 + 64],
                    _r(PS[:, bk * 512:bk * 512 + 256], (4, 64)),
                    DPOS[:, h0:h0 + 4, :]), sVE, 1)
                oev = sVE.n
            VE.w(sTE, suv[X])
            for h in range(h0, h0 + 4):
                VE.op(lambda e, h=h: e.scalar_tensor_tensor(
                    SF[:, h, :], SF[:, h, :], col(CTC + h), _sudst(h),
                    AL.mult, AL.add), sVE, 1)
            v_sf = sVE.n
            SC.w(sVE, v_sf)
            SC.op(lambda e, h0=h0: e.copy(SBT[:, h0:h0 + 4, :], SF[:, h0:h0 + 4, :]),
                  sSC, 1)
            scanp[X] = dict(vne=vne[X], oev=oev, sbt=sSC.n, sfm=v_sf)
        return dict(te=sTE.n, ve=sVE.n, sc=sSC.n)

    for tb in range(NTB):
        vwin = emit_window(tb, spill=(KSUB != "w"))
        if KSUB == "full":
            for c in range(2):
                st = emit_scan_step(2 * tb + c, with_o=False,
                                    win=(vwin if c == 0 else None))
            scan1_done[tb] = st["ve"]
        else:
            scan1_done[tb] = sVE.n
    v_scan1 = dict(te=sTE.n, ve=sVE.n, sc=sSC.n)
    if KSUB != "full":
        return partial_store()

    if PHASE < 4:
        return partial_store()

    GP.w(sVE, v_scan1["ve"])
    GP.w(sSC, v_scan1["sc"])
    GP.op(lambda e: e.dma_start(out=sg_in.ap(), in_=SF.rearrange("p a b -> p (a b)")), sGD, 16)
    GP.w(sGD)
    GP.op(lambda e: e.collective_compute(
        "AllGather", AL.bypass, replica_groups=[list(range(8))],
        ins=[sg_in.ap()], outs=[sg_out.ap()]), sCC, 1)
    SGCH = arf(0, (P, 4, H * DH))
    GP.w(sCC)
    VE.w(sCC)
    for b4 in range(2):
        GP.w(sVE)
        GP.op(lambda e, b4=b4: e.dma_start(
            out=SGCH, in_=sg_out.ap()[4 * b4:4 * b4 + 4].rearrange("r p c -> p r c")), sGD, 16)
        VE.w(sGD)
        for i in range(4):
            r = 4 * b4 + i
            if r == 0:
                VE.op(lambda e, r=r, i=i: e.tensor_scalar(
                    SF.rearrange("p a b -> p (a b)"), SGCH[:, i, :],
                    col(SMCo + r), None, AL.mult), sVE, 1)
            else:
                VE.op(lambda e, r=r, i=i: e.scalar_tensor_tensor(
                    SF.rearrange("p a b -> p (a b)"), SGCH[:, i, :], col(SMCo + r),
                    SF.rearrange("p a b -> p (a b)"), AL.mult, AL.add), sVE, 1)
    v_sinit = sVE.n
    SC.w(sVE, v_sinit)
    SC.op(lambda e: e.copy(SBT[:, :, :], SF[:, :, :]), sSC, 1)
    v_sinit_sc = sSC.n

    if PHASE < 5:
        return partial_store()

    WRB = arb(AT_WWB, (P, FC, D))
    for piece in range(4):
        SY.w(sVE, v_scan1["ve"])
        bsel = 0
        def _wrl(e, piece=piece, bsel=bsel):
            src = dram["wrt"].rearrange("(c p) n -> p c n", p=P)[:, :, piece * 256:(piece + 1) * 256]
            return e.dma_start(out=_r(ARF[:, AF_WS[bsel]:AF_WS[bsel] + 2048], (FC, 256)), in_=src)
        SY.w(sVE)
        SY.op(_wrl, sWW, 16)
        VE.w(sWW)
        VE.op(lambda e, piece=piece, bsel=bsel: e.tensor_copy(
            WRB[:, :, piece * 256:(piece + 1) * 256],
            _r(ARF[:, AF_WS[bsel]:AF_WS[bsel] + 2048], (FC, 256))), sVE, 1)
    v_wr = sVE.n

    reload_done = {}
    scan2_done = {}

    def emit_reload(tb):
        SY.w(sVE, scan2_done.get(tb - 2))
        if tb < 2:
            SY.w(sGD, spill_done[tb + 14])
            SY.w(sVE, v_scan1["ve"])
            SY.w(sTE, v_scan1["te"])
        sr = sRL[tb % 2]
        SY.op(lambda e, tb=tb: e.dma_start(out=XWw(tb).rearrange("p a b -> p (a b)"), in_=xspill[tb]), sr, 16)
        SY.op(lambda e, tb=tb: e.dma_start(out=XWFw(tb).rearrange("p a b -> p (a b)"), in_=wspill[tb]), sr, 16)
        SY.op(lambda e, tb=tb: e.dma_start(out=ATWw(tb).rearrange("p a b -> p (a b)"), in_=aspill[tb]), sr, 16)
        SY.op(lambda e, tb=tb: e.dma_start(out=WDw(tb).rearrange("p a b -> p (a b)"), in_=dspill[tb]), sr, 16)
        reload_done[tb] = (sr, sr.n)

    emit_reload(0)
    emit_reload(1)
    stage6_done = {}
    bcast2_fired = [False]
    lt3v = [None]

    for w in range(4):
        for k in range(8):
            n = w * 8 + k
            tb, c = n // 2, n % 2
            if c == 0:
                TE.w(*reload_done[tb])
                if tb + 2 < NTB:
                    pass
            if n == 0:
                TE.w(sVE, v_sinit)
                TE.w(sSC, v_sinit_sc)
            st = emit_scan_step(n, with_o=True, ow_slot=k)
            if c == 1:
                scan2_done[tb] = st["ve"]
                if tb + 2 < NTB:
                    emit_reload(tb + 2)
        ws = w * TT
        TE.w(sVE, st["ve"])
        TE.w(sSC, st["sc"])
        TE.w(sVE, v_wr)
        def _a1(e, ws=ws):
            return e.matmul(bank(4), ONESFW[32:33, :], LN2A1[0:1, ws:ws + TT], start=True, stop=True)
        TE.op(_a1, sTE, 1)
        for ocg in range(2):
            TE.w(sVE)
            def _wrm(e, ocg=ocg, ws=ws):
                last = None
                for oci in range(4):
                    oc = ocg * 4 + oci
                    for h in range(H):
                        last = e.matmul(bank(oci), WRB[:, h, oc * 128:(oc + 1) * 128],
                                        OWw[:, h, :], start=(h == 0), stop=False)
                    last = e.matmul(bank(oci), ONESFW[64:65, :], LN2B[0:1, ws:ws + TT],
                                    start=False, stop=True)
                return last
            TE.op(_wrm, sTE, 1)
            v_wrm = sTE.n
            T1 = arf(AF_T1, (P, TT))
            VE.w(sTE, v_wrm)
            for oci in range(4):
                oc = ocg * 4 + oci
                VE.op(lambda e, oc=oc, ws=ws: e.tensor_mul(T1, XR[:, oc, ws:ws + TT], bank(4)))
                VE.op(lambda e, oc=oc, oci=oci, ws=ws: e.scalar_tensor_tensor(
                    XR[:, oc, ws:ws + TT], bank(oci), 1.0, T1, AL.mult, AL.add), sVE, 1)
        stage6_done[w] = sVE.n
        if scanp["A"]:
            scanp["A"]["oev"] = sVE.n
        tap(f"x2_w{w}", lambda ws=ws: XR[:, :, ws:ws + TT], (P, FC, TT))
        if w == 0:
            GP.w(sVE, stage6_done[0])
            GP.op(lambda e: e.dma_start(out=lg_in.ap().rearrange("p (c t) -> p c t", t=16),
                                        in_=XR[:, :, 1:17]), sGD, 16)
            GP.w(sGD)
            GP.op(lambda e: e.collective_compute(
                "AllGather", AL.bypass, replica_groups=[list(range(8))],
                ins=[lg_in.ap()], outs=[lg_out.ap()]), sCC, 1)
    v_scan2 = dict(te=sTE.n, ve=sVE.n, sc=sSC.n)
    tap("ow_last", lambda: OWw[:, :, :], (P, H, TT), BF16)

    if PHASE < 6:
        return partial_store()

    LCH = arf(0, (P, 8, P))
    XLT = arf(3152, (P, P))
    XLTB = arb(23552, (P, P))
    GP.w(sCC)
    GP.op(lambda e: e.dma_start(out=LCH, in_=lg_out.ap().rearrange("r p c -> p r c")), sGD, 16)
    VE.w(sCC)
    VE.w(sVE, v_scan2["ve"])
    VE.w(sGD)
    for r in range(8):
        if r == 0:
            VE.op(lambda e, r=r: e.tensor_scalar(XLT[:, :], LCH[:, r, :], COLS[:, 48 + r:48 + r + 1],
                                                 None, AL.mult), sVE, 1)
        else:
            VE.op(lambda e, r=r: e.scalar_tensor_tensor(XLT[:, :], LCH[:, r, :],
                                                        COLS[:, 48 + r:48 + r + 1],
                                                        XLT[:, :], AL.mult, AL.add), sVE, 1)
    VE.op(lambda e: e.tensor_copy(XLTB[:, :], XLT[:, :]), sVE, 1)
    v_xlt = sVE.n
    TE.w(sVE, v_xlt)
    TE.w(sSC, v_scan2["sc"])
    for fc in range(FC):
        TE.w(sSC)
        TE.op(lambda e, fc=fc: e.matmul(bank(1, 0, 128)[0:16, :], _r(XLTB[:, :], (FC, 16))[:, fc, :],
                                        IDC[:, 0:P], start=True, stop=True), sTE, 1)
        SC.w(sTE)
        SC.op(lambda e, fc=fc: e.copy(LTS[0:16, fc * P:(fc + 1) * P], bank(1, 0, 128)[0:16, :]), sSC, 1)
    VE.w(sSC)
    lt3v[0] = emit_lt()

    TE.w(sVE, lt3v[0]["ve"])
    TE.w(sGD, lt3v[0]["gd"])
    TE.w(sSC, lt3v[0]["sc"])
    state["wg"] = {0: sTE.n, 1: sTE.n}
    SY.w(sVE, lt3v[0]["ve"])
    SY.w(sGD, lt3v[0]["gd"])
    VE.w(sTE)
    emit_ffn(2, final_store=True)

    SY.w(sST, sST.n)
    GP.w(sST, sST.n)
    _finish()
    n_ops = sum(len(s.ops) for s in [TE, VE, SC, GP, SY])
    return nc, n_ops


def _host_constants():
    i = np.arange(P)
    im = (i % 64).astype(np.float32)
    x, y = im[:, None], im[None, :]
    blkm = (i[:, None] // 64) == (i[None, :] // 64)
    return dict(
        dijs=np.where(blkm, x - y, 0.0).astype(np.float32),
        bdstrictneg=np.where(blkm & (x > y), -1.0, 0.0).astype(np.float32),
        bdinclT=np.where(blkm & (x <= y), 1.0, 0.0).astype(np.float32),
        njcol=(-(im) - 1.0).astype(np.float32)[:, None],
        iprow=(np.arange(C) + 1.0).astype(np.float32)[None, :],
        dwcol=(63.0 - im).astype(np.float32)[:, None],
        gpcol=im.astype(np.float32)[:, None],
    )


_NC_CACHE = {}


def kernel(L0_attn_out, L0_attn_W_write, L0_attn_W_read, L0_attn_W_beta,
           L0_attn_decay, L0_ffn_W1, L0_ffn_b1, L0_ffn_W2, L0_ffn_b2,
           L0_ffn_scale):
    B, T, Dm = L0_attn_out.shape
    f32 = np.float32
    x0 = np.ascontiguousarray(L0_attn_out, f32)
    consts = _host_constants()
    w1t = np.ascontiguousarray(L0_ffn_W1.T, f32)
    w2t = np.ascontiguousarray(L0_ffn_W2.T, f32)
    wwt = np.ascontiguousarray(L0_attn_W_write.T, f32)
    wrt = np.ascontiguousarray(L0_attn_W_read.T, f32)
    wbt = np.ascontiguousarray(L0_attn_W_beta.T, f32)
    b1c = np.ascontiguousarray(np.asarray(L0_ffn_b1, f32).reshape(NHC, P).T, f32)
    b2c = np.ascontiguousarray(np.asarray(L0_ffn_b2, f32).reshape(FC, P).T, f32)
    scl = np.asarray(L0_ffn_scale, f32).reshape(1, 1)
    dec = np.asarray(L0_attn_decay, f32).reshape(1, H)

    key = "nc"
    if key not in _NC_CACHE:
        _NC_CACHE[key] = build_nc()
    nc, n_ops = _NC_CACHE[key]

    in_maps = []
    for c in range(8):
        b, j = c // 2, c % 2
        t0 = j * TOK
        sm = np.zeros((1, 8), f32)
        sm2 = np.zeros((1, 8), f32)
        sm2[0, 2 * b] = 1.0
        if j == 1:
            sm[0, 2 * b] = 1.0
        m = dict(
            x0=np.ascontiguousarray(x0[b, t0:t0 + TOK, :].T),
            ltsrc=np.ascontiguousarray(x0[b, 1:17, :]),
            w1t=w1t, w2t=w2t, wwt=wwt, wrt=wrt, wbt=wbt,
            b1c=b1c, b2c=b2c, scl=scl, decay=dec,
            smask=sm, smask2=sm2, **consts,
        )
        in_maps.append(m)

    res = run_bass_kernel_spmd(nc, in_maps, core_ids=list(range(8)))
    out = np.empty((B, T, Dm), f32)
    for c in range(8):
        b, j = c // 2, c % 2
        t0 = j * TOK
        out[b, t0:t0 + TOK, :] = res.results[c]["out"].T
    return out


# version 11
# speedup vs baseline: 1.0242x; 1.0113x over previous
import os
import numpy as np
import ml_dtypes
import concourse.bass as bass
import concourse.mybir as mybir
from concourse.bass_utils import run_bass_kernel_spmd

F32 = mybir.dt.float32
BF16 = mybir.dt.bfloat16
AF = mybir.ActivationFunctionType
AL = mybir.AluOpType
AX = mybir.AxisListType.X

P, D, FC, HH, NHC = 128, 1024, 8, 4096, 32
TOK, NTB, TT, NTT = 2048, 16, 512, 4
H, DH, C = 8, 128, 64
DBG = set(s for s in os.environ.get("KDBG", "").split(",") if s)
PHASE = int(os.environ.get("KPHASE", "9"))
KSUB = os.environ.get("KSUB", "full")
KWIN = os.environ.get("KWIN", "full")


class Sem:
    def __init__(self, cm, name):
        self.h = cm.__enter__()
        self.n = 0
        self.name = name


class Stream:
    def __init__(self, name, auto=None):
        self.name = name
        self.auto = auto
        self.ops = []
        self.waited = {}

    def w(self, sem, val=None):
        val = sem.n if val is None else val
        if not val or val <= 0 or self.waited.get(sem.name, -1) >= val:
            return
        self.waited[sem.name] = val
        self.ops.append(lambda e, s=sem, v=val: e.wait_ge(s.h, v))

    def op(self, f, inc=None, k=1):
        if self.auto is not None:
            self.w(self.auto, self.auto.n)
            if inc is None:
                inc, k = self.auto, 1
        if inc is None:
            self.ops.append(f)
            return None
        inc.n += k
        self.ops.append(lambda e, f=f, s=inc, k=k: f(e).then_inc(s.h, k))
        return inc.n


def _r(ap, shape):
    if len(shape) == 1:
        return ap
    pat = "p (" + " ".join(f"d{i}" for i in range(len(shape))) + ") -> p " + \
          " ".join(f"d{i}" for i in range(len(shape)))
    return ap.rearrange(pat, **{f"d{i}": s for i, s in enumerate(shape)})


def build_nc(dbg=DBG):
    nc = bass.Bass("TRN2", target_bir_lowering=False)
    ctxs = []

    def ent(cm):
        ctxs.append(cm)
        return cm.__enter__()

    sbuf = lambda n, s, d: ent(nc.sbuf_tensor(n, s, d))
    psumt = lambda n, s, d: ent(nc.psum_tensor(n, s, d))
    sem = lambda n: Sem(nc.semaphore(n), n)

    dram = {}
    for name, shape in [
        ("x0", (D, TOK)), ("ltsrc", (16, D)),
        ("w1t", (D, HH)), ("w2t", (HH, D)), ("wwt", (D, D)), ("wrt", (D, D)),
        ("wbt", (D, H)), ("b1c", (P, NHC)), ("b2c", (P, FC)), ("scl", (1, 1)),
        ("decay", (1, H)), ("dijs", (P, P)), ("bdstrictneg", (P, P)),
        ("bdinclT", (P, P)), ("njcol", (P, 1)), ("iprow", (1, C)),
        ("dwcol", (P, 1)), ("gpcol", (P, 1)), ("smask", (1, 8)),
        ("smask2", (1, 8)),
    ]:
        dram[name] = nc.dram_tensor(name, list(shape), F32, kind="ExternalInput").ap()
    OUT = nc.dram_tensor("out", [D, TOK], F32, kind="ExternalOutput").ap()

    ltd = nc.dram_tensor("ltd", [16, D], F32).ap()
    pg_in = nc.dram_tensor("pg_in", [P, FC], F32)
    pg_out = nc.dram_tensor("pg_out", [8, P, FC], F32)
    sg_in = nc.dram_tensor("sg_in", [P, H * DH], F32)
    sg_out = nc.dram_tensor("sg_out", [8, P, H * DH], F32)
    lg_in = nc.dram_tensor("lg_in", [P, FC * 16], F32)
    lg_out = nc.dram_tensor("lg_out", [8, P, FC * 16], F32)
    xspill = nc.dram_tensor("xspill", [NTB, P, H * 256], BF16).ap()
    wspill = nc.dram_tensor("wspill", [NTB, P, H * DH], BF16).ap()
    aspill = nc.dram_tensor("aspill", [NTB, P, H * DH], BF16).ap()
    dspill = nc.dram_tensor("dspill", [NTB, P, H * DH], BF16).ap()
    w1b = nc.dram_tensor("w1b", [D, HH], BF16).ap()
    w2b = nc.dram_tensor("w2b", [HH, D], BF16).ap()

    XR = sbuf("XR", [P, FC, TOK], F32)
    YN = sbuf("YN", [P, H, TOK + 1], BF16)
    BETA = sbuf("BETA", [P, NTB, H], F32)
    BGAM = sbuf("BGAM", [P, H], F32)
    SF = sbuf("SF", [P, H, DH], F32)
    SBT = sbuf("SBT", [P, H, DH], BF16)
    VNS = sbuf("VNS", [P, H * DH], BF16)
    IDC = sbuf("IDC", [P, P], BF16)
    MST = sbuf("MST", [P, H, P], BF16)
    MIT = sbuf("MIT", [P, H, P], BF16)
    DPOS = sbuf("DPOS", [P, H, C], BF16)
    COLS = sbuf("COLS", [P, 64], F32)
    GPC, DWC, CTC, NGC, LGC, SMCo, SCLCo, SCLHo, PRSB = 0, 8, 16, 24, 32, 40, 56, 57, 58
    B1C = sbuf("B1Cs", [P, NHC], F32)
    B2CH = sbuf("B2CHs", [P, FC], F32)
    ROWF = sbuf("ROWF", [P, 2048], F32)
    ROWB = sbuf("ROWB", [P, TT], BF16)
    ONE128B = sbuf("ONE128B", [P, 1], BF16)
    ONESFW = sbuf("ONESFW", [P, P], F32)
    CONSTV = sbuf("CONSTV", [P, 4], F32)
    E2TMP = ROWB[0:4, 256:512]
    LN2A1 = ROWF[32:33, 0:TOK]
    LN2B = ROWF[64:65, 0:TOK]
    LTT = sbuf("LTT", [P, FC, 16], BF16)
    E1P = sbuf("E1P", [4, NHC, P], BF16)
    O2P = sbuf("O2P", [P, NHC, 16], BF16)
    E2P = sbuf("E2P", [16, FC, P], BF16)
    PVT = sbuf("PVT", [4, TT], BF16)
    GVT = sbuf("GVT", [16, TT], BF16)
    QTI = sbuf("QTI", [P, 8], BF16)
    PRF = sbuf("PRF", [P, FC], F32)
    PRB = sbuf("PRB", [P, FC], BF16)
    PR8 = sbuf("PR8", [P, FC], F32)
    CBUF = {}
    for nm in ["njcol", "iprow", "dwcol", "gpcol", "smask", "smask2",
               "b1c", "b2c", "scl", "decay"]:
        CBUF[nm] = sbuf("c_" + nm, list(dram[nm].shape), F32)

    ARB = sbuf("ARB", [P, 28672], BF16)
    ARF = sbuf("ARF", [P, 4608], F32)

    def _arena(T, off, shape):
        n = int(np.prod(shape[1:]))
        ap = T[0:shape[0], off:off + n]
        if len(shape) > 2:
            pat = "p (" + " ".join(f"d{i}" for i in range(len(shape) - 1)) + ") -> p " + \
                  " ".join(f"d{i}" for i in range(len(shape) - 1))
            ap = ap.rearrange(pat, **{f"d{i}": s for i, s in enumerate(shape[1:])})
        return ap
    arb = lambda off, shape: _arena(ARB, off, shape)
    arf = lambda off, shape: _arena(ARF, off, shape)

    PS = psumt("PS", [P, 4096], F32)
    bank = lambda i, off=0, n=512: PS[:, i * 512 + off:i * 512 + off + n]

    sSY, sGD, sTE, sVE, sSC, sGP, sCC, sST = (sem(x) for x in
        ["sSY", "sGD", "sTE", "sVE", "sSC", "sGP", "sCC", "sST"])
    sX0 = sem("sX0")
    sWW = sem("sWW")
    sWW1 = sem("sWW1")
    sWS = [sem("sWS0"), sem("sWS1")]
    sRL = [sem("sRL0"), sem("sRL1")]
    TE = Stream("te")
    VE = Stream("ve", auto=sVE)
    SC = Stream("sc", auto=sSC)
    GP = Stream("gp", auto=sGP)
    SY = Stream("sy")
    taps = {"n": 0}

    def _finish():
        blk = ent(nc.Block())
        @blk.sync
        def _(e):
            for f in SY.ops:
                f(e)
        @blk.tensor
        def _(e):
            for f in TE.ops:
                f(e)
        @blk.vector
        def _(e):
            for f in VE.ops:
                f(e)
        @blk.scalar
        def _(e):
            for f in SC.ops:
                f(e)
        @blk.gpsimd
        def _(e):
            for f in GP.ops:
                f(e)
        for cm in reversed(ctxs):
            cm.__exit__(None, None, None)

    def tap(name, src_fn, shape, dt=F32):
        if name not in dbg:
            return
        t = nc.dram_tensor("dbg_" + name, list(shape), dt, kind="ExternalOutput").ap()
        GP.w(sVE)
        GP.w(sSC)
        GP.w(sTE)
        GP.op(lambda e, t=t, f=src_fn: e.dma_start(out=t, in_=f()), sST, 16)
        taps["n"] += 1

    ONESF = ROWF[0:1, 0:P]
    ONESB = ROWB[0:1, 0:P]
    col = lambda o, k=1: COLS[:, o:o + k]

    SY.op(lambda e: e.dma_start(out=XR[:, :, :], in_=dram["x0"].rearrange("(c p) t -> p c t", p=P)), sX0, 16)
    v_x0 = sX0.n
    for nm in CBUF:
        GP.op(lambda e, nm=nm: e.dma_start(out=CBUF[nm][:, :], in_=dram[nm]), sGD, 16)
    CBUF["dijs"] = arf(256, (P, P))
    CBUF["bdstrictneg"] = arf(384, (P, P))
    CBUF["bdinclT"] = arf(512, (P, P))
    for nm in ["dijs", "bdstrictneg", "bdinclT"]:
        GP.op(lambda e, nm=nm: e.dma_start(out=CBUF[nm], in_=dram[nm]), sGD, 16)
    v_consts = sGD.n

    GP.w(sGD, v_consts)
    GP.op(lambda e: e.memset(IDC[:, :], 0.0), sGP, 1)
    GP.w(sGP)
    def _init(e):
        e.affine_select(out=IDC[:, :], in_=IDC[:, :], compare_op=AL.not_equal,
                        fill=1.0, base=0, pattern=[[-1, P]], channel_multiplier=1)
        e.memset(ROWF[0:1, 0:P], 1.0)
        e.memset(ONESFW[:, :], 1.0)
        e.memset(CONSTV[:, 0:1], 1e-5)
        e.memset(CONSTV[:, 1:2], -0.5)
        e.memset(CONSTV[:, 2:3], 64.0)
        e.memset(CONSTV[:, 3:4], 0.5)
        e.memset(ROWB[0:1, 0:P], 1.0)
        e.memset(ONE128B[:, :], 1.0)
        return e.tensor_copy(B1C[:, :], CBUF["b1c"][:, :])
    GP.op(_init, sGP, 1)
    v_init = sGP.n

    SC.w(sGD, v_consts)
    SC.op(lambda e: e.activation(ROWF[0:1, 128:128 + H], CBUF["decay"][0:1, :], AF.Sigmoid))
    SC.op(lambda e: e.activation(ROWF[0:1, 128:128 + H], ROWF[0:1, 128:128 + H], AF.Ln), sSC, 1)
    TE.w(sSC)
    TE.w(sGP, v_init)
    def _bc0(e):
        e.matmul(bank(0, 0, 8), ONESF, ROWF[0:1, 128:128 + H], start=True, stop=True)
        e.matmul(bank(0, 8, 8), ONESF, CBUF["smask"][0:1, :], start=True, stop=True)
        e.matmul(bank(0, 16, 8), ONESF, CBUF["smask2"][0:1, :], start=True, stop=True)
        return e.matmul(bank(0, 24, 1), ONESF, CBUF["scl"][0:1, :], start=True, stop=True)
    TE.op(_bc0, sTE, 1)
    SC.w(sTE)
    SC.op(lambda e: e.copy(col(LGC, 8), bank(0, 0, 8)))
    SC.op(lambda e: e.copy(col(SMCo, 8), bank(0, 8, 8)))
    SC.op(lambda e: e.copy(col(48, 8), bank(0, 16, 8)))
    SC.op(lambda e: e.copy(col(SCLCo), bank(0, 24, 1)), sSC, 1)
    SC.op(lambda e: e.mul(col(SCLHo), col(SCLCo), CONSTV[:, 3:4]))
    SC.op(lambda e: e.activation(col(CTC, 8), col(LGC, 8), AF.Exp, scale=CONSTV[:, 2:3]))
    SC.op(lambda e: e.activation(col(NGC, 8), col(LGC, 8), AF.Exp))
    for h in range(H):
        SC.op(lambda e, h=h: e.activation(col(GPC + h), CBUF["gpcol"][:, :], AF.Exp, scale=col(LGC + h)))
        SC.op(lambda e, h=h: e.activation(col(DWC + h), CBUF["dwcol"][:, :], AF.Exp, scale=col(LGC + h)))
    VE.w(sSC)
    VE.op(lambda e: e.tensor_scalar_mul(col(NGC, 8), col(NGC, 8), -1.0))
    VE.w(sGD, v_consts)
    VE.op(lambda e: e.tensor_scalar_mul(B2CH[:, :], CBUF["b2c"][:, :], 0.5), sVE, 1)

    MTMP = arf(0, (P, P))
    NJX = arf(128, (P, 1))
    for h in range(H):
        SC.w(sVE)
        SC.op(lambda e, h=h: e.activation(MTMP, CBUF["dijs"], AF.Exp, scale=col(LGC + h)))
        SC.op(lambda e, h=h: e.activation(NJX, CBUF["njcol"][:, :], AF.Exp, scale=col(LGC + h)))
        SC.op(lambda e, h=h: e.activation(ROWF[0:1, 192:192 + C], CBUF["iprow"][0:1, :], AF.Exp,
                                          scale=COLS[0:1, LGC + h:LGC + h + 1]), sSC, 1)
        VE.w(sSC)
        VE.op(lambda e, h=h: e.tensor_mul(MST[:, h, :], MTMP, CBUF["bdstrictneg"]))
        VE.op(lambda e, h=h: e.tensor_scalar(MIT[:, h, :], CBUF["bdinclT"], NJX[:, 0:1], None, AL.mult))
        VE.op(lambda e, h=h: e.tensor_copy(ROWB[0:1, 128:128 + C], ROWF[0:1, 192:192 + C]), sVE, 1)
        TE.w(sVE)
        TE.op(lambda e, h=h: e.matmul(bank(1, 0, C), ONESB, ROWB[0:1, 128:128 + C], start=True, stop=True), sTE, 1)
        SC.w(sTE)
        SC.op(lambda e, h=h: e.copy(DPOS[:, h, :], bank(1, 0, C)), sSC, 1)
    v_masks = sSC.n

    LTS = ARF[0:16, 1024:1024 + D]
    LTSQ = ARF[0:16, 2048:2048 + D]
    LTM = ARF[0:16, 3072:3072 + 8]
    AB_LTB = 22528

    def emit_lt():
        VE.op(lambda e: e.reduce_sum(LTM[:, 0:1], LTS, axis=AX))
        VE.op(lambda e: e.tensor_scalar_mul(LTM[:, 0:1], LTM[:, 0:1], 1.0 / D), sVE, 1)
        SC.w(sVE)
        SC.op(lambda e: e.activation(LTSQ, LTS, AF.Square), sSC, 1)
        VE.w(sSC)
        VE.op(lambda e: e.reduce_sum(LTM[:, 1:2], LTSQ, axis=AX))
        VE.op(lambda e: e.tensor_scalar_mul(LTM[:, 1:2], LTM[:, 1:2], 1.0 / D))
        VE.op(lambda e: e.tensor_mul(LTM[:, 2:3], LTM[:, 0:1], LTM[:, 0:1]))
        VE.op(lambda e: e.tensor_sub(LTM[:, 1:2], LTM[:, 1:2], LTM[:, 2:3]), sVE, 1)
        SC.w(sVE)
        SC.op(lambda e: e.activation(LTM[:, 2:3], LTM[:, 1:2], AF.Sqrt, bias=CONSTV[0:16, 0:1]), sSC, 1)
        VE.w(sSC)
        VE.op(lambda e: e.reciprocal(LTM[:, 3:4], LTM[:, 2:3]))
        VE.op(lambda e: e.tensor_scalar(LTS, LTS, LTM[:, 0:1],
                                        LTM[:, 3:4], AL.subtract, AL.mult), sVE, 1)
        v_ln = sVE.n
        GP.w(sVE, v_ln)
        GP.op(lambda e: e.dma_start(out=ltd, in_=LTS), sGD, 16)
        v_ltd = sGD.n
        LTB16 = ARB[0:16, AB_LTB:AB_LTB + D]
        VE.op(lambda e: e.tensor_copy(LTB16, LTS), sVE, 1)
        TE.w(sVE)
        for fc in range(FC):
            TE.w(sSC)
            TE.op(lambda e, fc=fc: e.matmul(bank(1, 0, 16), LTB16[:, fc * P:(fc + 1) * P],
                                            IDC[0:16, 0:16], start=True, stop=True), sTE, 1)
            SC.w(sTE)
            SC.op(lambda e, fc=fc: e.copy(LTT[:, fc, :], bank(1, 0, 16)), sSC, 1)
        v_ltt = sSC.n
        GP.w(sGD, v_ltd)
        GP.op(lambda e: e.memset(E1P[0:4, :, :], 0.0), sGP, 1)
        GP.op(lambda e: e.dma_start(out=LTSQ[0:4, :], in_=ltd[0:8:2, :]), sGD, 16)
        VE.w(sGD)
        VE.w(sGP)
        LTB4 = ARB[0:4, AB_LTB:AB_LTB + D]
        VE.w(sTE)
        VE.op(lambda e: e.tensor_copy(LTB4, LTSQ[0:4, :]), sVE, 1)
        GP.w(sVE)
        GP.w(sGP)
        for k in range(4):
            GP.op(lambda e, k=k: e.dma_start(
                out=E1P[k:k + 1, 8 * k:8 * k + 8, :].rearrange("a b c -> a (b c)"),
                in_=LTB4[k:k + 1, :]), sGD, 16)
        GP.w(sGD)
        GP.w(sVE)
        GP.op(lambda e: e.memset(O2P[:, :, :], 0.0), sGP, 1)
        VE.w(sGP)
        VE.w(sSC, v_ltt)
        for hc in range(NHC):
            tp0 = hc // 8
            VE.op(lambda e, hc=hc, tp0=tp0: e.tensor_copy(
                O2P[:, hc, tp0:tp0 + 13:4], LTT[:, hc % 8, 9:16:2]), sVE, 1)
        v_o2p = sVE.n
        GP.op(lambda e: e.memset(E2P[0:16, :, :], 0.0), sGP, 1)
        v_ez = sGP.n
        for k in range(4):
            GP.w(sGP, v_ez)
            GP.w(sSC)
            GP.w(sTE)
            def gq(e, k=k):
                src = ltd[8 + 2 * k:8 + 2 * k + 1, :].rearrange(
                    "a (qh q p) -> (a q) qh p", p=4, qh=2)
                return e.dma_start(out=_r(QTI[:, :], (2, 4)), in_=src)
            GP.op(gq, sGD, 16)
            TE.w(sGD)
            for qh in range(2):
                TE.op(lambda e, qh=qh: e.matmul(bank(1, qh * 128, 128)[0:4, :],
                                                _r(QTI[:, :], (2, 4))[:, qh, :],
                                                IDC[:, :], start=True, stop=True), sTE, 1)
            SC.w(sTE)
            SC.op(lambda e: e.copy(E2TMP, bank(1, 0, 256)[0:4, :]), sSC, 1)
            GP.w(sSC)
            def ge(e, k=k):
                return e.dma_start(
                    out=E2P.rearrange("a b c -> a (b c)")[4 * k:4 * k + 4, 256 * k:256 * k + 256],
                    in_=E2TMP)
            GP.op(ge, sGD, 16)
        return dict(ve=sVE.n, gd=sGD.n, sc=sSC.n)

    if PHASE < 1:
        GP.w(sGD, v_consts)
        GP.w(sGP)
        GP.w(sSC)
        GP.w(sVE)
        GP.w(sTE)
    GP.w(sGD, v_consts)
    GP.op(lambda e: e.dma_start(out=LTS, in_=dram["ltsrc"]), sGD, 16)
    VE.w(sGD)
    v_lt1 = emit_lt()

    AB_XB, AB_XSQ, AB_Y1, AB_H, AB_WG0, AB_WG1 = 0, 4096, 0, 8192, 24576, 26624
    AF_WS = [0, 2048]
    AF_T1 = 4096

    lns = {}

    def ln_ttile(src_slice, ydst, rows_a1b=None):
        XB = arb(AB_XB, (P, FC, TT))
        XSQ = arb(AB_XSQ, (P, FC, TT))
        for fc in range(FC):
            VE.op(lambda e, fc=fc: e.tensor_copy(XB[:, fc, :], src_slice(fc)), sVE, 1)
        v_cast = sVE.n
        SC.w(sVE, v_cast)
        SC.op(lambda e: e.activation(XSQ[:, :, :], XB[:, :, :], AF.Square), sSC, 1)
        v_sq = sSC.n
        TE.w(sVE, v_cast)
        TE.w(sSC, v_sq)
        def _stat(e):
            last = None
            for blk in range(4):
                for fc in range(FC):
                    last = e.matmul(PS[0:1, 3 * 512 + blk * 128:3 * 512 + (blk + 1) * 128],
                                    ONE128B[:, :], XB[:, fc, blk * 128:(blk + 1) * 128],
                                    start=(fc == 0), stop=(fc == FC - 1))
            for blk in range(4):
                for fc in range(FC):
                    last = e.matmul(PS[0:1, 4 * 512 + blk * 128:4 * 512 + (blk + 1) * 128],
                                    ONE128B[:, :], XSQ[:, fc, blk * 128:(blk + 1) * 128],
                                    start=(fc == 0), stop=(fc == FC - 1))
            return last
        TE.op(_stat, sTE, 1)
        v_stat = sTE.n
        lns["stat"] = v_stat
        rm = ROWF[0:1, 1536:1536 + TT]
        rs = ROWF[0:1, 512:512 + TT]
        rb = ROWF[0:1, 1024:1024 + TT]
        VE.w(sTE, v_stat)
        VE.op(lambda e: e.tensor_scalar_mul(rm, bank(3)[0:1, :], 1.0 / D))
        VE.op(lambda e: e.tensor_mul(rb, rm, rm))
        VE.op(lambda e: e.scalar_tensor_tensor(rs, bank(4)[0:1, :], 1.0 / D, rb, AL.mult, AL.subtract), sVE, 1)
        SC.w(sVE)
        SC.op(lambda e: e.activation(rs, rs, AF.Ln, bias=CONSTV[0:1, 0:1]))
        SC.op(lambda e: e.activation(rs, rs, AF.Exp, scale=CONSTV[0:1, 1:2]), sSC, 1)
        VE.w(sSC)
        VE.op(lambda e: e.scalar_tensor_tensor(rb, rm, -1.0, rs, AL.mult, AL.mult), sVE, 1)
        if rows_a1b is not None:
            a1d, bd = rows_a1b
            VE.op(lambda e: e.tensor_scalar_add(a1d, rs, 1.0))
            VE.op(lambda e: e.tensor_copy(bd, rb), sVE, 1)
        TE.w(sVE, sVE.n)
        def _bcast(e):
            e.matmul(bank(4), ONESF, rs, start=True, stop=True)
            return e.matmul(bank(5), ONESF, rb, start=True, stop=True)
        TE.op(_bcast, sTE, 1)
        T1 = arf(AF_T1, (P, TT))
        VE.w(sTE)
        for fc in range(FC):
            VE.op(lambda e, fc=fc: e.tensor_mul(T1, src_slice(fc), bank(4)))
            VE.op(lambda e, fc=fc: e.tensor_add(ydst(fc), T1, bank(5)), sVE, 1)
        return sVE.n

    state = dict(gelu={}, wg={0: None, 1: None}, res={})

    def fire_bcastP():
        GP.w(sVE, state["res"][(1, 3)])
        GP.op(lambda e: e.dma_start(out=pg_in.ap().rearrange("p (c t) -> p c t", t=1),
                                    in_=XR[:, :, 2047:2048],
                                    allow_slow_non_contiguous=True), sGD, 16)
        GP.w(sGD)
        GP.op(lambda e: e.collective_compute(
            "AllGather", AL.bypass, replica_groups=[list(range(8))],
            ins=[pg_in.ap()], outs=[pg_out.ap()]), sCC, 1)

    def emit_ffn(callid, final_store=False):
        Y1 = arb(AB_Y1, (P, FC, TT))
        HT = arb(AB_H, (P, NHC, TT))
        WG = [arb(AB_WG0, (P, FC, 2, P)), arb(AB_WG1, (P, FC, 2, P))]
        WG2 = [arb(AB_WG0, (P, 2, FC, P)), arb(AB_WG1, (P, 2, FC, P))]
        WGF = [AB_WG0, AB_WG1]
        tts = [3, 0, 1, 2] if callid == 1 else list(range(NTT))
        for ti, tt in enumerate(tts):
            ts = tt * TT
            src = lambda fc, ts=ts: XR[:, fc, ts:ts + TT]
            if callid == 1 and ti == 0:
                TE.w(sX0, v_x0)
                VE.w(sX0, v_x0)
            TE.w(sVE, state["res"].get((callid, tts[ti - 1])) if ti else None)
            v_y1 = ln_ttile(src, lambda fc: Y1[:, fc, :])
            TE.w(sVE, v_y1)
            def _l1(e):
                last = None
                for fc in range(FC):
                    last = e.matmul(PS[0:4, 2 * 512:2 * 512 + 512], LTT[:, fc, 1:8:2],
                                    Y1[:, fc, :], start=(fc == 0), stop=(fc == FC - 1))
                return last
            TE.op(_l1, sTE, 1)
            SC.w(sTE)
            SC.op(lambda e: e.mul(PVT[0:4, :], PS[0:4, 2 * 512:2 * 512 + 512],
                                  col(SCLCo)[0:4, :]), sSC, 1)
            v_pv = sSC.n
            first = (callid == 1 and ti == 0)
            for g in range(16):
                bsel = g % 2
                Q = SY if bsel == 0 else GP
                sq_ = sWS[bsel]
                Q.w(sTE, state["wg"][bsel])
                if first:
                    if g >= 2 and g - 2 in state.get("w1c", {}):
                        Q.w(*state["w1c"][g - 2])
                    def _wld(e, g=g, bsel=bsel):
                        src = dram["w1t"].rearrange("(c p) n -> p c n", p=P)[:, :, g * 256:(g + 1) * 256]
                        return e.dma_start(out=_r(ARF[:, AF_WS[bsel]:AF_WS[bsel] + 2048], (FC, 256)), in_=src)
                    Q.op(_wld, sq_, 16)
                    SC.w(sq_)
                    SC.op(lambda e, bsel=bsel: e.copy(
                        ARB[:, WGF[bsel]:WGF[bsel] + 2048],
                        ARF[:, AF_WS[bsel]:AF_WS[bsel] + 2048]), sSC, 1)
                    v_wc = sSC.n
                    Q.w(sSC, v_wc)
                    Q.op(lambda e, g=g, bsel=bsel: e.dma_start(
                        out=w1b.rearrange("(c p) n -> p c n", p=P)[:, :, g * 256:(g + 1) * 256],
                        in_=_r(ARB[:, WGF[bsel]:WGF[bsel] + 2048], (FC, 256))), sq_, 16)
                    state.setdefault("w1c", {})[g] = (sq_, sq_.n)
                    Q.w(sq_)
                else:
                    for vv in state.get("wcfin", []):
                        Q.w(*vv)
                    Q.w(*state["w1c"][g])
                    def _wldc(e, g=g, bsel=bsel):
                        src = w1b.rearrange("(c p) n -> p c n", p=P)[:, :, g * 256:(g + 1) * 256]
                        return e.dma_start(out=_r(ARB[:, WGF[bsel]:WGF[bsel] + 2048], (FC, 256)), in_=src)
                    Q.op(_wldc, sq_, 16)
                    v_wc = None
                for hci in range(2):
                    hc = 2 * g + hci
                    pb = hc % 2
                    if first:
                        TE.w(sSC, v_wc)
                    else:
                        TE.w(sWS[bsel])
                    TE.w(sSC, state["gelu"].get((callid, tt, hc - 2)))
                    TE.w(sSC, v_pv)
                    def _h(e, hci=hci, hc=hc, pb=pb, bsel=bsel):
                        for fc in range(FC):
                            e.matmul(bank(pb), WG[bsel][:, fc, hci, :], Y1[:, fc, :],
                                     start=(fc == 0), stop=False)
                        return e.matmul(bank(pb), E1P[0:4, hc, :], PVT[0:4, :],
                                        start=False, stop=True)
                    TE.op(_h, sTE, 1)
                    SC.w(sTE)
                    state["gelu"][(callid, tt, hc)] = SC.op(
                        lambda e, hc=hc, pb=pb: e.activation(HT[:, hc, :], bank(pb), AF.Gelu,
                                                             bias=B1C[:, hc:hc + 1]), sSC, 1)
                state["wg"][bsel] = sTE.n
            TE.w(sSC, state["gelu"][(callid, tt, NHC - 1)])
            def _g2(e):
                last = None
                for hc in range(NHC):
                    last = e.matmul(PS[0:16, 2 * 512:2 * 512 + 512], O2P[:, hc, :],
                                    HT[:, hc, :], start=(hc == 0), stop=(hc == NHC - 1))
                return last
            TE.op(_g2, sTE, 1)
            SC.w(sTE)
            SC.op(lambda e: e.mul(GVT[0:16, :], PS[0:16, 2 * 512:2 * 512 + 512],
                                  col(SCLHo)[0:16, :]), sSC, 1)
            v_gv = sSC.n
            for g in range(16):
                bsel = g % 2
                Q = SY if bsel == 0 else GP
                sq_ = sWS[bsel]
                Q.w(sTE, state["wg"][bsel])
                if first:
                    if g >= 2 and g - 2 in state.get("w2c", {}):
                        Q.w(*state["w2c"][g - 2])
                    def _w2ld(e, g=g, bsel=bsel):
                        src = dram["w2t"].rearrange("(hc p) n -> p hc n", p=P)[:, 2 * g:2 * g + 2, :]
                        return e.dma_start(out=_r(ARF[:, AF_WS[bsel]:AF_WS[bsel] + 2048], (2, D)), in_=src)
                    Q.op(_w2ld, sq_, 16)
                    VE.w(sq_)
                    VE.op(lambda e, bsel=bsel: e.tensor_scalar(
                        ARB[:, WGF[bsel]:WGF[bsel] + 2048],
                        ARF[:, AF_WS[bsel]:AF_WS[bsel] + 2048],
                        0.5, None, AL.mult), sVE, 1)
                    v_wc2 = sVE.n
                    Q.w(sVE, v_wc2)
                    state[("w2cast", bsel)] = v_wc2
                    Q.op(lambda e, g=g, bsel=bsel: e.dma_start(
                        out=w2b.rearrange("(hc p) n -> p hc n", p=P)[:, 2 * g:2 * g + 2, :],
                        in_=_r(ARB[:, WGF[bsel]:WGF[bsel] + 2048], (2, D))), sq_, 16)
                    state.setdefault("w2c", {})[g] = (sq_, sq_.n)
                    Q.w(sq_)
                else:
                    for vv in state.get("wcfin", []):
                        Q.w(*vv)
                    Q.w(*state["w2c"][g])
                    def _w2ldc(e, g=g, bsel=bsel):
                        src = w2b.rearrange("(hc p) n -> p hc n", p=P)[:, 2 * g:2 * g + 2, :]
                        return e.dma_start(out=_r(ARB[:, WGF[bsel]:WGF[bsel] + 2048], (2, D)), in_=src)
                    Q.op(_w2ldc, sq_, 16)
                if first:
                    TE.w(sVE, v_wc2)
                else:
                    TE.w(sWS[bsel])
                if g == 0:
                    TE.w(sSC, v_gv)
                def _w2(e, g=g, bsel=bsel):
                    last = None
                    for hci in range(2):
                        hc = 2 * g + hci
                        for oc in range(FC):
                            last = e.matmul(bank(oc), WG2[bsel][:, hci, oc, :], HT[:, hc, :],
                                            start=(hc == 0), stop=False)
                    return last
                TE.op(_w2, sTE, 1)
                state["wg"][bsel] = sTE.n
            if first:
                state["wcfin"] = [state["w1c"][14], state["w1c"][15],
                                  state["w2c"][14], state["w2c"][15]]
            TE.w(sSC, v_gv)
            def _up(e):
                last = None
                for oc in range(FC):
                    last = e.matmul(bank(oc), E2P[0:16, oc, :], GVT[0:16, :],
                                    start=False, stop=True)
                return last
            TE.op(_up, sTE, 1)
            v_w2 = sTE.n
            VE.w(sTE, v_w2)
            for oc in range(FC):
                VE.op(lambda e, oc=oc, ts=ts: e.scalar_tensor_tensor(
                    XR[:, oc, ts:ts + TT], bank(oc), B2CH[:, oc:oc + 1],
                    XR[:, oc, ts:ts + TT], AL.add, AL.add), sVE, 1)
            state["res"][(callid, tt)] = sVE.n
            if callid == 1 and tt == 3:
                fire_bcastP()
            tap(f"x_f{callid}_t{tt}", lambda ts=ts: XR[:, :, ts:ts + TT], (P, FC, TT))
            if final_store:
                SY.w(sVE)
                SY.op(lambda e, ts=ts: e.dma_start(
                    out=OUT.rearrange("(c p) t -> p c t", p=P)[:, :, ts:ts + TT],
                    in_=XR[:, :, ts:ts + TT]), sST, 16)
        return

    SY.w(sVE, v_lt1["ve"])
    SY.w(sGD, v_lt1["gd"])
    if PHASE >= 1:
        emit_ffn(1)
    v_ffn1_ve = sVE.n

    def partial_store():
        for tt in range(NTT):
            ts = tt * TT
            SY.w(sVE)
            SY.w(sSC)
            SY.op(lambda e, ts=ts: e.dma_start(
                out=OUT.rearrange("(c p) t -> p c t", p=P)[:, :, ts:ts + TT],
                in_=XR[:, :, ts:ts + TT]), sST, 16)
        SY.w(sST, sST.n)
        GP.w(sST, sST.n)
        _finish()
        return nc, sum(len(s.ops) for s in [TE, VE, SC, GP, SY])

    if PHASE < 2:
        TE.w(sX0, v_x0)
        SY.w(sGD)
        SY.w(sGP)
        SY.w(sSC)
        SY.w(sVE)
        SY.w(sTE)
        return partial_store()


    for tt in range(NTT):
        ts = tt * TT
        TE.w(sVE, state["res"][(1, 2)])
        src = lambda fc, ts=ts: XR[:, fc, ts:ts + TT]
        ydst = lambda fc, ts=ts: YN[:, fc, 1 + ts:1 + ts + TT]
        ln_ttile(src, ydst, rows_a1b=(LN2A1[0:1, ts:ts + TT], LN2B[0:1, ts:ts + TT]))
    v_ln2 = sVE.n
    v_ln2stat = lns["stat"]

    tap("y2", lambda: YN[:, :, :], (P, H, TOK + 1), BF16)

    if PHASE < 3:
        return partial_store()

    AT_WWB, AT_V, AT_XW, AT_XWF = 0, 8192, 12288, 16384
    AT_WD, AT_ATW, AT_UT, AT_Y2S = 18432, 20480, 22528, 24064
    AT_WBB, AT_OW = 25096, AT_V
    WWB = arb(AT_WWB, (P, FC, D))
    WBB = arb(AT_WBB, (P, FC, H))
    wwcast = {0: None, 1: None}
    for piece in range(4):
        SY.w(sTE, v_ln2stat)
        bsel = piece % 2
        def _wwl(e, piece=piece, bsel=bsel):
            src = dram["wwt"].rearrange("(c p) n -> p c n", p=P)[:, :, piece * 256:(piece + 1) * 256]
            return e.dma_start(out=_r(ARF[:, AF_WS[bsel]:AF_WS[bsel] + 2048], (FC, 256)), in_=src)
        if wwcast[bsel] is not None:
            SY.w(sVE, wwcast[bsel])
        else:
            SY.w(sVE, state[("w2cast", bsel)])
        sww_ = sWW if bsel == 0 else sWW1
        SY.op(_wwl, sww_, 16)
        VE.w(sww_)
        VE.op(lambda e, piece=piece, bsel=bsel: e.tensor_copy(
            WWB[:, :, piece * 256:(piece + 1) * 256],
            _r(ARF[:, AF_WS[bsel]:AF_WS[bsel] + 2048], (FC, 256))), sVE, 1)
        wwcast[bsel] = sVE.n
    SY.w(sVE)
    SY.op(lambda e: e.dma_start(out=_r(ARF[:, 0:64], (FC, H)),
                                in_=dram["wbt"].rearrange("(c p) n -> p c n", p=P)), sWW, 16)
    VE.w(sWW)
    VE.op(lambda e: e.tensor_copy(WBB[:, :, :], _r(ARF[:, 0:64], (FC, H))), sVE, 1)
    v_ww = sVE.n
    PCH = arf(3088, (P, 8, FC))
    GP.w(sCC)
    GP.w(sWW, sWW.n)
    GP.w(sWW1, sWW1.n)
    GP.op(lambda e: e.dma_start(out=PCH, in_=pg_out.ap().rearrange("r p c -> p r c")), sGD, 16)
    VE.w(sCC)
    VE.w(sGD)
    for r in range(8):
        if r == 0:
            VE.op(lambda e, r=r: e.tensor_scalar(PRF[:, :], PCH[:, r, :], col(SMCo + r),
                                                 None, AL.mult), sVE, 1)
        else:
            VE.op(lambda e, r=r: e.scalar_tensor_tensor(PRF[:, :], PCH[:, r, :], col(SMCo + r),
                                                        PRF[:, :], AL.mult, AL.add), sVE, 1)
    VE.op(lambda e: e.tensor_copy(PRB[:, :], PRF[:, :]), sVE, 1)
    SC.w(sVE)
    SC.op(lambda e: e.activation(PR8[:, :], PRF[:, :], AF.Square), sSC, 1)
    VE.w(sSC)
    VE.op(lambda e: e.tensor_copy(QTI[:, :], PR8[:, :]), sVE, 1)
    TE.w(sVE)
    def _pst(e):
        e.matmul(bank(1, 0, 1)[0:8, :], PRB[:, :], ONE128B[:, :], start=True, stop=True)
        return e.matmul(bank(1, 1, 1)[0:8, :], QTI[:, :], ONE128B[:, :], start=True, stop=True)
    TE.op(_pst, sTE, 1)
    SC.w(sTE)
    SC.op(lambda e: e.copy(PR8[0:8, 0:2], bank(1, 0, 2)[0:8, :]), sSC, 1)
    VE.w(sSC)
    VE.op(lambda e: e.tensor_copy(QTI[0:8, 0:2], PR8[0:8, 0:2]), sVE, 1)
    TE.w(sVE)
    TE.op(lambda e: e.matmul(bank(1, 2, 2)[0:1, :], ONE128B[0:8, :], QTI[0:8, 0:2], start=True, stop=True), sTE, 1)
    SC.w(sTE)
    SC.op(lambda e: e.copy(ROWF[0:1, 136:138], bank(1, 2, 2)[0:1, :]), sSC, 1)
    VE.w(sSC)
    VE.op(lambda e: e.tensor_scalar_mul(ROWF[0:1, 136:138], ROWF[0:1, 136:138], 1.0 / D))
    VE.op(lambda e: e.tensor_mul(ROWF[0:1, 138:139], ROWF[0:1, 136:137], ROWF[0:1, 136:137]))
    VE.op(lambda e: e.tensor_sub(ROWF[0:1, 137:138], ROWF[0:1, 137:138], ROWF[0:1, 138:139]), sVE, 1)
    SC.w(sVE)
    SC.op(lambda e: e.activation(ROWF[0:1, 137:138], ROWF[0:1, 137:138], AF.Ln, bias=CONSTV[0:1, 0:1]))
    SC.op(lambda e: e.activation(ROWF[0:1, 137:138], ROWF[0:1, 137:138], AF.Exp, scale=CONSTV[0:1, 1:2]), sSC, 1)
    VE.w(sSC)
    VE.op(lambda e: e.scalar_tensor_tensor(ROWF[0:1, 138:139], ROWF[0:1, 136:137], -1.0,
                                           ROWF[0:1, 137:138], AL.mult, AL.mult), sVE, 1)
    TE.w(sVE)
    TE.op(lambda e: e.matmul(bank(1, 8, 2), ONESF, ROWF[0:1, 137:139], start=True, stop=True), sTE, 1)
    SC.w(sTE)
    SC.op(lambda e: e.copy(col(PRSB, 2), bank(1, 8, 2)), sSC, 1)
    VE.w(sSC)
    VE.op(lambda e: e.tensor_scalar(PR8[:, :], PRF[:, :], col(PRSB), col(PRSB + 1),
                                    AL.mult, AL.add), sVE, 1)
    SC.w(sVE)
    SC.op(lambda e: e.activation(PRF[:, :], PR8[:, :], AF.Square), sSC, 1)
    VE.w(sSC)
    VE.op(lambda e: e.tensor_copy(QTI[:, :], PRF[:, :]), sVE, 1)
    TE.w(sVE)
    TE.op(lambda e: e.matmul(bank(1, 16, 1)[0:8, :], QTI[:, :], ONE128B[:, :], start=True, stop=True), sTE, 1)
    SC.w(sTE)
    SC.op(lambda e: e.copy(PRF[0:8, 0:1], bank(1, 16, 1)[0:8, :]))
    SC.op(lambda e: e.activation(PRF[0:8, 0:1], PRF[0:8, 0:1], AF.Sqrt), sSC, 1)
    VE.w(sSC)
    VE.op(lambda e: e.tensor_scalar_max(PRF[0:8, 0:1], PRF[0:8, 0:1], 1e-12))
    VE.op(lambda e: e.reciprocal(PRF[0:8, 0:1], PRF[0:8, 0:1]))
    VE.op(lambda e: e.tensor_copy(QTI[0:8, 0:1], PRF[0:8, 0:1]), sVE, 1)
    TE.w(sVE)
    TE.op(lambda e: e.matmul(bank(1, 24, 8)[0:1, :], QTI[0:8, 0:1], IDC[0:8, 0:8], start=True, stop=True), sTE, 1)
    SC.w(sTE)
    SC.op(lambda e: e.copy(ROWF[0:1, 144:152], bank(1, 24, 8)[0:1, :]), sSC, 1)
    TE.w(sSC)
    TE.op(lambda e: e.matmul(bank(1, 32, 8), ONESF, ROWF[0:1, 144:152], start=True, stop=True), sTE, 1)
    VE.w(sTE)
    VE.op(lambda e: e.tensor_mul(PRF[:, :], PR8[:, :], bank(1, 32, 8)))
    VE.op(lambda e: e.tensor_copy(YN[:, :, 0], PRF[:, :]), sVE, 1)
    v_ynprev = sVE.n
    VE.op(lambda e: e.memset(SF[:, :, :], 0.0))
    VE.op(lambda e: e.memset(SBT[:, :, :], 0.0))
    VE.op(lambda e: e.memset(bank(2), 0.0))
    VE.op(lambda e: e.memset(bank(6), 0.0), sVE, 1)
    v_sz = sVE.n

    Y2S = arb(AT_Y2S, (P, FC, 129))
    AT_UT2 = 25160
    UTS = [[arb(AT_UT, (P, 2, P)), arb(AT_UT + 256, (P, 2, P))],
           [arb(AT_UT2, (P, 2, P)), arb(AT_UT2 + 256, (P, 2, P))]]
    VPD = [arb(AT_UT + 512, (P, 2, 256)), arb(AT_UT2 + 512, (P, 2, 256))]

    def UTPn(p, j):
        return UTS[p][j % 2][:, 0, :]

    def UTPc(p, j):
        return UTS[p][j % 2][:, 1, :]

    def UTPx(p, j):
        return UTS[p][j % 2]
    Vw = lambda tb: arb(AT_V + (tb % 2) * 2048, (P, H, 256))
    XWw = lambda tb: arb(AT_XW + (tb % 2) * 2048, (P, H, 256))
    XWFw = lambda tb: arb(AT_XWF + (tb % 2) * 1024, (P, H, DH))
    WDw = lambda tb: arb(AT_WD + (tb % 2) * 1024, (P, H, DH))
    ATWw = lambda tb: arb(AT_ATW + (tb % 2) * 1024, (P, H, DH))
    OWw = arb(AT_OW, (P, H, TT))

    scan1_done = {}
    spill_done = {}
    win_prev = {}

    def emit_window(tb, spill=True):
        t0 = tb * 128
        if tb >= 2:
            TE.w(sVE, scan1_done.get(tb - 2))
            TE.w(sGD, spill_done.get(tb - 2))
        if win_prev:
            TE.w(sVE, win_prev["ve"])
            TE.w(sSC, win_prev["sc"])
        else:
            TE.w(sVE)
            TE.w(sSC)
        TE.w(sVE, v_ww)
        if tb == 0:
            TE.w(sVE, v_sz)
            TE.w(sVE, v_ln2)
        SC.w(sTE, win_prev.get("q"))
        SC.op(lambda e, t0=t0: e.activation(Y2S[:, :, 1:129], YN[:, :, 1 + t0:1 + t0 + 128],
                                            AF.Square), sSC, 1)
        v_y2s = sSC.n
        def _vp(e, t0=t0):
            last = None
            for half in range(2):
                for fc in range(FC):
                    last = e.matmul(bank(half), YN[:, fc, 1 + t0:1 + t0 + 128],
                                    WWB[:, fc, half * 512:(half + 1) * 512],
                                    start=(fc == 0), stop=(fc == FC - 1))
            for fc in range(FC):
                last = e.matmul(bank(3, 0, 8), YN[:, fc, 1 + t0:1 + t0 + 128],
                                WBB[:, fc, :], start=(fc == 0), stop=(fc == FC - 1))
            return last
        TE.op(_vp, sTE, 1)
        v_vp = sTE.n
        TE.w(sSC, v_y2s)
        def _q(e):
            last = None
            for h in range(H):
                last = e.matmul(bank(3, 128 + h, 1), Y2S[:, h, 1:129], ONE128B[:, :],
                                start=True, stop=True)
            return last
        TE.op(_q, sTE, 1)
        win_prev["q"] = sTE.n
        SC.w(sTE, v_vp)
        SC.op(lambda e, tb=tb: e.activation(BETA[:, tb, :], bank(3, 0, 8), AF.Sigmoid), sSC, 1)
        v_beta = sSC.n
        SC.w(sTE)
        SC.op(lambda e: e.copy(PR8[:, 0:8], bank(3, 128, 8)))
        SC.op(lambda e: e.activation(PR8[:, 0:8], PR8[:, 0:8], AF.Sqrt), sSC, 1)
        v_sqr = sSC.n
        VE.w(sSC, v_beta)
        VE.op(lambda e, tb=tb: e.tensor_mul(BGAM[:, :], BETA[:, tb, :], col(GPC, 8)), sVE, 1)
        VE.w(sSC, v_sqr)
        VE.op(lambda e: e.tensor_scalar_max(PR8[:, 0:8], PR8[:, 0:8], 1e-12))
        VE.op(lambda e: e.reciprocal(PR8[:, 0:8], PR8[:, 0:8]))
        VE.op(lambda e: e.tensor_copy(QTI[:, :], PR8[:, 0:8]), sVE, 1)
        v_rn = sVE.n
        for h in range(H):
            SC.op(lambda e, h=h, tb=tb: e.mul(Vw(tb)[:, h, 0:128],
                                              bank(h // 4, (h % 4) * 128, 128),
                                              BETA[:, tb, h:h + 1]), sSC, 1)
        v_vev = sSC.n
        if KWIN == "vp" or KWIN == "q":
            return dict(ve=sVE.n, sc=sSC.n)
        TE.w(sVE, v_rn)
        def _rr(e):
            last = None
            for h in range(H):
                b, o = (2, h * 128) if h < 4 else (3, (h - 4) * 128)
                last = e.matmul(PS[0:1, b * 512 + o:b * 512 + o + 128],
                                QTI[:, h:h + 1], IDC[:, :], start=True, stop=True)
            return last
        TE.op(_rr, sTE, 1)
        SC.w(sTE)
        SC.op(lambda e: e.copy(ROWF[0:1, 512:1024], bank(2)[0:1, :]))
        SC.op(lambda e: e.copy(ROWF[0:1, 1024:1536], bank(3)[0:1, :]), sSC, 1)
        v_rrow = sSC.n
        TE.w(sSC, v_rrow)
        def _rbc(e):
            e.matmul(bank(2), ONESF, ROWF[0:1, 512:1024], start=True, stop=True)
            return e.matmul(bank(3), ONESF, ROWF[0:1, 1024:1536], start=True, stop=True)
        TE.op(_rbc, sTE, 1)
        VE.w(sTE)
        VE.w(sSC, v_y2s)
        VE.op(lambda e, t0=t0: e.tensor_mul(YN[:, :, 1 + t0:1 + t0 + 128],
                                            YN[:, :, 1 + t0:1 + t0 + 128],
                                            PS[:, 2 * 512:2 * 512 + 1024]), sVE, 1)
        v_yn = sVE.n
        if KWIN == "yn":
            return dict(ve=sVE.n, sc=sSC.n)
        TE.w(sVE, v_yn)
        if tb == 0:
            TE.w(sVE, v_ynprev)
        def _wkt(e, t0=t0):
            last = None
            for h in range(H):
                dst = bank(2 + h // 4, (h % 4) * 128, 128)
                last = e.matmul(dst, YN[:, h, t0:t0 + 128], IDC[:, :],
                                start=True, stop=True)
            return last
        TE.op(_wkt, sTE, 1)
        if KWIN == "wkmm":
            return dict(ve=sVE.n, sc=sSC.n)
        VE.w(sTE)
        SC.w(sTE)
        for h in range(4):
            psrc = bank(2, h * 128, 128)
            VE.op(lambda e, h=h, tb=tb, psrc=psrc: e.tensor_scalar(
                Vw(tb)[:, h, 128:256], psrc, BGAM[:, h:h + 1], None, AL.mult), sVE, 1)
            VE.op(lambda e, h=h, tb=tb, psrc=psrc: e.tensor_scalar(
                WDw(tb)[:, h, :], psrc, col(DWC + h), None, AL.mult), sVE, 1)
        for h in range(4, H):
            psrc = bank(3, (h - 4) * 128, 128)
            SC.op(lambda e, h=h, tb=tb, psrc=psrc: e.mul(
                Vw(tb)[:, h, 128:256], psrc, BGAM[:, h:h + 1]), sSC, 1)
            SC.op(lambda e, h=h, tb=tb, psrc=psrc: e.mul(
                WDw(tb)[:, h, :], psrc, col(DWC + h)), sSC, 1)
        v_wk = sSC.n
        v_wk_ve = sVE.n
        if KWIN == "wk":
            return dict(ve=sVE.n, sc=sSC.n)
        if tb == 0:
            for vv in state.get("wcfin", []):
                VE.w(*vv)
                SC.w(*vv)
        ut_free = {0: [(sSC, v_vev)], 1: [(sVE, v_wk_ve)]}

        def EV(p):
            return (VE, sVE) if p == 0 else (SC, sSC)

        def emit_pair(h0):
            hs = (h0, h0 + 1)
            ev = {}

            def gaq(e, h, t0=t0):
                b0 = (h % 2) * 2 * 512
                return e.matmul(PS[:, b0:b0 + 128], YN[:, h, t0:t0 + 128],
                                YN[:, h, t0:t0 + 128], start=True, stop=True)

            for h in hs:
                p = h % 2
                for s_, v_ in ut_free[p]:
                    TE.w(s_, v_)
                TE.op(lambda e, h=h: gaq(e, h), sTE, 1)
                VE.w(sTE, sTE.n)
                VE.op(lambda e, h=h, p=p, tb=tb: e.scalar_tensor_tensor(
                    UTPn(p, 0), bank(2 * p, 0, 128), BETA[:, tb, h:h + 1],
                    MST[:, h, :], AL.mult, AL.mult), sVE, 1)
                ev[(h, "a")] = sVE.n
            for h in hs:
                p = h % 2
                E, sE = EV(p)
                TE.w(sVE, ev[(h, "a")])
                TE.op(lambda e, p=p: e.matmul(bank(2 * p, 128, 128), UTPn(p, 0),
                                              IDC[:, :], start=True, stop=True), sTE, 1)
                E.w(sTE, sTE.n)
                if p == 0:
                    E.op(lambda e: e.tensor_copy(UTPc(0, 0), bank(0, 128, 128)), sE, 1)
                else:
                    E.op(lambda e: e.copy(UTPc(1, 0), bank(2, 128, 128)), sE, 1)
                ev[(h, "at")] = sE.n
            cur = {h: 0 for h in hs}
            for j in range(6):
                for h in hs:
                    p = h % 2
                    E, sE = EV(p)
                    TE.w(sE, ev[(h, "at")] if j == 0 else ev[(h, "u", j - 1)])
                    def _app(e, h=h, p=p, j=j, c=cur[h], tb=tb):
                        rv = Vw(tb)[:, h, :] if j == 0 else VPD[p][:, c, :]
                        e.matmul(bank(2 * p + 1, 0, 256), IDC[:, :], rv,
                                 start=True, stop=False)
                        return e.matmul(bank(2 * p + 1, 0, 256), UTPc(p, j), rv,
                                        start=False, stop=True)
                    TE.op(_app, sTE, 1)
                    v_appm = sTE.n
                    if j < 5:
                        def _sq(e, p=p, j=j):
                            e.matmul(bank(2 * p, 256, 128), UTPc(p, j), UTPn(p, j),
                                     start=True, stop=True)
                            return e.matmul(bank(2 * p, 384, 128), UTPn(p, j), UTPc(p, j),
                                            start=True, stop=True)
                        TE.op(_sq, sTE, 1)
                        v_sqm = sTE.n
                    E.w(sTE, v_appm)
                    if j == 5:
                        if p == 0:
                            E.op(lambda e, h=h, tb=tb: e.tensor_copy(
                                XWw(tb)[:, h, :], bank(1, 0, 256)), sE, 1)
                        else:
                            E.op(lambda e, h=h, tb=tb: e.copy(
                                XWw(tb)[:, h, :], bank(3, 0, 256)), sE, 1)
                    else:
                        c = cur[h]
                        if p == 0:
                            E.op(lambda e, c=c: e.tensor_copy(
                                VPD[0][:, 1 - c, :], bank(1, 0, 256)), sE, 1)
                        else:
                            E.op(lambda e, c=c: e.copy(
                                VPD[1][:, 1 - c, :], bank(3, 0, 256)), sE, 1)
                        cur[h] = 1 - c
                        E.w(sTE, v_sqm)
                        psq = _r(PS[:, (2 * p) * 512 + 256:(2 * p) * 512 + 512], (2, P))
                        if p == 0:
                            E.op(lambda e, j=j, psq=psq: e.tensor_copy(
                                UTPx(0, j + 1)[:, :, :], psq), sE, 1)
                        else:
                            E.op(lambda e, j=j, psq=psq: e.copy(
                                UTPx(1, j + 1)[:, :, :], psq), sE, 1)
                    ev[(h, "u", j)] = sE.n
            vat = {}
            for h in hs:
                p = h % 2
                E, sE = EV(p)
                TE.w(sE, ev[(h, "u", 4)])
                def _at2(e, h=h, p=p, t0=t0):
                    b0 = 2 * p * 512
                    return e.matmul(PS[:, b0 + 256:b0 + 384], YN[:, h, t0:t0 + 128],
                                    YN[:, h, 1 + t0:1 + t0 + 128],
                                    start=True, stop=True)
                TE.op(_at2, sTE, 1)
                vat[h] = sTE.n
            for h in hs:
                p = h % 2
                VE.w(sTE, vat[h])
                VE.op(lambda e, h=h, p=p, tb=tb: e.tensor_mul(
                    ATWw(tb)[:, h, :], bank(2 * p, 256, 128), MIT[:, h, :]), sVE, 1)
                ev[(h, "att")] = sVE.n
            for h in hs:
                p = h % 2
                E, sE = EV(p)
                TE.w(sVE, ev[(h, "att")])
                TE.w(sE, ev[(h, "u", 5)])
                def _xwf(e, h=h, p=p, tb=tb):
                    return e.matmul(bank(2 * p, 384, 128), XWw(tb)[:, h, 128:256],
                                    IDC[:, :], start=True, stop=True)
                TE.op(_xwf, sTE, 1)
                E.w(sTE, sTE.n)
                if p == 0:
                    E.op(lambda e, h=h, tb=tb: e.tensor_scalar(
                        XWFw(tb)[:, h, :], bank(0, 384, 128), col(NGC + h),
                        None, AL.mult), sE, 1)
                else:
                    E.op(lambda e, h=h, tb=tb: e.mul(
                        XWFw(tb)[:, h, :], bank(2, 384, 128), col(NGC + h)), sE, 1)
                ut_free[p] = [(sE, sE.n)]

        for h0 in range(0, H, 2):
            emit_pair(h0)
        win_prev["ve"] = ut_free[0][0][1]
        win_prev["sc"] = ut_free[1][0][1]
        v_win = dict(ve=sVE.n, sc=sSC.n)
        if not spill:
            spill_done[tb] = sGD.n
            return v_win
        GP.w(sVE, v_win["ve"])
        GP.w(sSC, v_win["sc"])
        GP.op(lambda e, tb=tb: e.dma_start(out=xspill[tb], in_=XWw(tb).rearrange("p a b -> p (a b)")), sGD, 16)
        GP.op(lambda e, tb=tb: e.dma_start(out=wspill[tb], in_=XWFw(tb).rearrange("p a b -> p (a b)")), sGD, 16)
        GP.op(lambda e, tb=tb: e.dma_start(out=aspill[tb], in_=ATWw(tb).rearrange("p a b -> p (a b)")), sGD, 16)
        GP.op(lambda e, tb=tb: e.dma_start(out=dspill[tb], in_=WDw(tb).rearrange("p a b -> p (a b)")), sGD, 16)
        spill_done[tb] = sGD.n
        GP.w(sGD)
        return v_win

    def _sudst(h):
        return bank(5, h * 128, 128) if h < 4 else bank(7, (h - 4) * 128, 128)

    scanp = {"A": {}, "B": {}}

    def emit_scan_step(n, with_o, ow_slot=None, win=None):
        tb, c = n // 2, n % 2
        r0 = c * 64
        halves = (("A", 0, 4), ("B", 4, 6))
        pv = {X: dict(scanp[X]) for X, _, _ in halves}
        if win is not None:
            TE.w(sVE, win["ve"])
            TE.w(sSC, win["sc"])
        vne = {}
        vnv = {}
        for X, h0, bk in halves:
            if not pv[X]:
                TE.w(sSC)
                TE.w(sVE)
            else:
                TE.w(sSC, pv[X]["sbt"])
                TE.w(sVE, pv[X].get("oev"))
            def _vn(e, tb=tb, r0=r0, h0=h0, bk=bk):
                e.matmul(PS[r0:r0 + 64, bk * 512:bk * 512 + 512],
                         IDC[r0:r0 + 64, r0:r0 + 64],
                         XWw(tb)[r0:r0 + 64, h0:h0 + 4, 0:128],
                         start=True, stop=False)
                last = None
                for h in range(h0, h0 + 4):
                    dst = PS[r0:r0 + 64, bk * 512 + (h % 4) * 128:bk * 512 + (h % 4) * 128 + 128]
                    last = e.matmul(dst, XWFw(tb)[:, h, r0:r0 + 64], SBT[:, h, :],
                                    start=False, stop=(h % 4 == 3))
                return last
            TE.op(_vn, sTE, 1)
            vnv[X] = sTE.n
        for X, h0, bk in halves:
            SC.w(sTE, vnv[X])
            SC.op(lambda e, r0=r0, h0=h0, bk=bk: e.copy(
                _r(VNS[r0:r0 + 64, h0 * 128:h0 * 128 + 512], (4, 128)),
                PS[r0:r0 + 64, bk * 512:bk * 512 + 512]), sSC, 1)
            vne[X] = sSC.n
        VNSv = _r(VNS[:, :], (H, 128))
        ovv = {}
        suv = {}
        for X, h0, bk in halves:
            if with_o:
                TE.w(sSC, vne[X])
                def _o(e, tb=tb, n=n, r0=r0, h0=h0, bk=bk):
                    last = None
                    for h in range(h0, h0 + 4):
                        dst = bank(bk, (h - h0) * 64, 64)
                        e.matmul(dst, SBT[:, h, :], YN[:, h, 1 + n * 64:1 + n * 64 + 64],
                                 start=True, stop=False)
                        last = e.matmul(dst, VNSv[r0:r0 + 64, h, :],
                                        ATWw(tb)[r0:r0 + 64, h, r0:r0 + 64],
                                        start=False, stop=True)
                    return last
                TE.op(_o, sTE, 1)
                ovv[X] = sTE.n
            else:
                TE.w(sSC, vne[X])
            TE.w(sVE, pv[X].get("sfm"))
            def _su(e, tb=tb, r0=r0, h0=h0):
                last = None
                for h in range(h0, h0 + 4):
                    last = e.matmul(_sudst(h), WDw(tb)[r0:r0 + 64, h, :],
                                    VNSv[r0:r0 + 64, h, :], start=True, stop=True)
                return last
            TE.op(_su, sTE, 1)
            suv[X] = sTE.n
        for X, h0, bk in halves:
            oev = pv[X].get("oev")
            if with_o:
                VE.w(sTE, ovv[X])
                VE.op(lambda e, ow_slot=ow_slot, h0=h0, bk=bk: e.tensor_mul(
                    OWw[:, h0:h0 + 4, ow_slot * 64:ow_slot * 64 + 64],
                    _r(PS[:, bk * 512:bk * 512 + 256], (4, 64)),
                    DPOS[:, h0:h0 + 4, :]), sVE, 1)
                oev = sVE.n
            VE.w(sTE, suv[X])
            for h in range(h0, h0 + 4):
                VE.op(lambda e, h=h: e.scalar_tensor_tensor(
                    SF[:, h, :], SF[:, h, :], col(CTC + h), _sudst(h),
                    AL.mult, AL.add), sVE, 1)
            v_sf = sVE.n
            SC.w(sVE, v_sf)
            SC.op(lambda e, h0=h0: e.copy(SBT[:, h0:h0 + 4, :], SF[:, h0:h0 + 4, :]),
                  sSC, 1)
            scanp[X] = dict(vne=vne[X], oev=oev, sbt=sSC.n, sfm=v_sf)
        return dict(te=sTE.n, ve=sVE.n, sc=sSC.n)

    for tb in range(NTB):
        vwin = emit_window(tb, spill=(KSUB != "w"))
        if KSUB == "full":
            for c in range(2):
                st = emit_scan_step(2 * tb + c, with_o=False,
                                    win=(vwin if c == 0 else None))
            scan1_done[tb] = st["ve"]
        else:
            scan1_done[tb] = sVE.n
    v_scan1 = dict(te=sTE.n, ve=sVE.n, sc=sSC.n)
    if KSUB != "full":
        return partial_store()

    if PHASE < 4:
        return partial_store()

    WRB = arb(AT_WWB, (P, FC, D))
    wrc = None
    for piece in range(4):
        SY.w(sVE, v_scan1["ve"])
        bsel = 0
        def _wrl(e, piece=piece, bsel=bsel):
            src = dram["wrt"].rearrange("(c p) n -> p c n", p=P)[:, :, piece * 256:(piece + 1) * 256]
            return e.dma_start(out=_r(ARF[:, AF_WS[bsel]:AF_WS[bsel] + 2048], (FC, 256)), in_=src)
        if wrc is not None:
            SY.w(sVE, wrc)
        SY.op(_wrl, sWW, 16)
        VE.w(sWW)
        VE.op(lambda e, piece=piece, bsel=bsel: e.tensor_copy(
            WRB[:, :, piece * 256:(piece + 1) * 256],
            _r(ARF[:, AF_WS[bsel]:AF_WS[bsel] + 2048], (FC, 256))), sVE, 1)
        wrc = sVE.n
    v_wr = sVE.n

    GP.w(sVE, v_scan1["ve"])
    GP.w(sSC, v_scan1["sc"])
    GP.op(lambda e: e.dma_start(out=sg_in.ap(), in_=SF.rearrange("p a b -> p (a b)")), sGD, 16)
    GP.w(sGD)
    GP.op(lambda e: e.collective_compute(
        "AllGather", AL.bypass, replica_groups=[list(range(8))],
        ins=[sg_in.ap()], outs=[sg_out.ap()]), sCC, 1)
    SGCH = arf(2048, (P, 2, H * DH))
    GP.w(sCC)
    VE.w(sCC)
    for b2 in range(4):
        GP.w(sVE)
        GP.op(lambda e, b2=b2: e.dma_start(
            out=SGCH, in_=sg_out.ap()[2 * b2:2 * b2 + 2].rearrange("r p c -> p r c")), sGD, 16)
        VE.w(sGD)
        for i in range(2):
            r = 2 * b2 + i
            if r == 0:
                VE.op(lambda e, r=r, i=i: e.tensor_scalar(
                    SF.rearrange("p a b -> p (a b)"), SGCH[:, i, :],
                    col(SMCo + r), None, AL.mult), sVE, 1)
            else:
                VE.op(lambda e, r=r, i=i: e.scalar_tensor_tensor(
                    SF.rearrange("p a b -> p (a b)"), SGCH[:, i, :], col(SMCo + r),
                    SF.rearrange("p a b -> p (a b)"), AL.mult, AL.add), sVE, 1)
    v_sinit = sVE.n
    SC.w(sVE, v_sinit)
    SC.op(lambda e: e.copy(SBT[:, :, :], SF[:, :, :]), sSC, 1)
    v_sinit_sc = sSC.n

    if PHASE < 5:
        return partial_store()


    reload_done = {}
    scan2_done = {}

    def emit_reload(tb):
        SY.w(sVE, scan2_done.get(tb - 2))
        if tb < 2:
            SY.w(sGD, spill_done[tb + 14])
            SY.w(sVE, v_scan1["ve"])
            SY.w(sTE, v_scan1["te"])
        sr = sRL[tb % 2]
        SY.op(lambda e, tb=tb: e.dma_start(out=XWw(tb).rearrange("p a b -> p (a b)"), in_=xspill[tb]), sr, 16)
        SY.op(lambda e, tb=tb: e.dma_start(out=XWFw(tb).rearrange("p a b -> p (a b)"), in_=wspill[tb]), sr, 16)
        SY.op(lambda e, tb=tb: e.dma_start(out=ATWw(tb).rearrange("p a b -> p (a b)"), in_=aspill[tb]), sr, 16)
        SY.op(lambda e, tb=tb: e.dma_start(out=WDw(tb).rearrange("p a b -> p (a b)"), in_=dspill[tb]), sr, 16)
        reload_done[tb] = (sr, sr.n)

    emit_reload(0)
    emit_reload(1)
    stage6_done = {}
    bcast2_fired = [False]
    lt3v = [None]

    for w in range(4):
        for k in range(8):
            n = w * 8 + k
            tb, c = n // 2, n % 2
            if c == 0:
                TE.w(*reload_done[tb])
                if tb + 2 < NTB:
                    pass
            if n == 0:
                TE.w(sVE, v_sinit)
                TE.w(sSC, v_sinit_sc)
            st = emit_scan_step(n, with_o=True, ow_slot=k)
            if c == 1:
                scan2_done[tb] = st["ve"]
                if tb + 2 < NTB:
                    emit_reload(tb + 2)
        ws = w * TT
        TE.w(sVE, st["ve"])
        TE.w(sSC, st["sc"])
        TE.w(sVE, v_wr)
        def _a1(e, ws=ws):
            return e.matmul(bank(4), ONESFW[32:33, :], LN2A1[0:1, ws:ws + TT], start=True, stop=True)
        TE.op(_a1, sTE, 1)
        for ocg in range(2):
            TE.w(sVE)
            def _wrm(e, ocg=ocg, ws=ws):
                last = None
                for oci in range(4):
                    oc = ocg * 4 + oci
                    for h in range(H):
                        last = e.matmul(bank(oci), WRB[:, h, oc * 128:(oc + 1) * 128],
                                        OWw[:, h, :], start=(h == 0), stop=False)
                    last = e.matmul(bank(oci), ONESFW[64:65, :], LN2B[0:1, ws:ws + TT],
                                    start=False, stop=True)
                return last
            TE.op(_wrm, sTE, 1)
            v_wrm = sTE.n
            T1 = arf(AF_T1, (P, TT))
            VE.w(sTE, v_wrm)
            for oci in range(4):
                oc = ocg * 4 + oci
                VE.op(lambda e, oc=oc, ws=ws: e.tensor_mul(T1, XR[:, oc, ws:ws + TT], bank(4)))
                VE.op(lambda e, oc=oc, oci=oci, ws=ws: e.scalar_tensor_tensor(
                    XR[:, oc, ws:ws + TT], bank(oci), 1.0, T1, AL.mult, AL.add), sVE, 1)
        stage6_done[w] = sVE.n
        if scanp["A"]:
            scanp["A"]["oev"] = sVE.n
        tap(f"x2_w{w}", lambda ws=ws: XR[:, :, ws:ws + TT], (P, FC, TT))
        if w == 0:
            GP.w(sVE, stage6_done[0])
            GP.op(lambda e: e.dma_start(out=lg_in.ap().rearrange("p (c t) -> p c t", t=16),
                                        in_=XR[:, :, 1:17]), sGD, 16)
            GP.w(sGD)
            GP.op(lambda e: e.collective_compute(
                "AllGather", AL.bypass, replica_groups=[list(range(8))],
                ins=[lg_in.ap()], outs=[lg_out.ap()]), sCC, 1)
    v_scan2 = dict(te=sTE.n, ve=sVE.n, sc=sSC.n)
    tap("ow_last", lambda: OWw[:, :, :], (P, H, TT), BF16)

    if PHASE < 6:
        return partial_store()

    LCH = arf(0, (P, 8, P))
    XLT = arf(3152, (P, P))
    XLTB = arb(23552, (P, P))
    GP.w(sCC)
    GP.op(lambda e: e.dma_start(out=LCH, in_=lg_out.ap().rearrange("r p c -> p r c")), sGD, 16)
    VE.w(sCC)
    VE.w(sVE, v_scan2["ve"])
    VE.w(sGD)
    for r in range(8):
        if r == 0:
            VE.op(lambda e, r=r: e.tensor_scalar(XLT[:, :], LCH[:, r, :], COLS[:, 48 + r:48 + r + 1],
                                                 None, AL.mult), sVE, 1)
        else:
            VE.op(lambda e, r=r: e.scalar_tensor_tensor(XLT[:, :], LCH[:, r, :],
                                                        COLS[:, 48 + r:48 + r + 1],
                                                        XLT[:, :], AL.mult, AL.add), sVE, 1)
    VE.op(lambda e: e.tensor_copy(XLTB[:, :], XLT[:, :]), sVE, 1)
    v_xlt = sVE.n
    TE.w(sVE, v_xlt)
    TE.w(sSC, v_scan2["sc"])
    for fc in range(FC):
        TE.w(sSC)
        TE.op(lambda e, fc=fc: e.matmul(bank(1, 0, 128)[0:16, :], _r(XLTB[:, :], (FC, 16))[:, fc, :],
                                        IDC[:, 0:P], start=True, stop=True), sTE, 1)
        SC.w(sTE)
        SC.op(lambda e, fc=fc: e.copy(LTS[0:16, fc * P:(fc + 1) * P], bank(1, 0, 128)[0:16, :]), sSC, 1)
    VE.w(sSC)
    lt3v[0] = emit_lt()

    TE.w(sVE, lt3v[0]["ve"])
    TE.w(sGD, lt3v[0]["gd"])
    TE.w(sSC, lt3v[0]["sc"])
    state["wg"] = {0: sTE.n, 1: sTE.n}
    SY.w(sVE, lt3v[0]["ve"])
    SY.w(sGD, lt3v[0]["gd"])
    VE.w(sTE)
    emit_ffn(2, final_store=True)

    SY.w(sST, sST.n)
    GP.w(sST, sST.n)
    _finish()
    n_ops = sum(len(s.ops) for s in [TE, VE, SC, GP, SY])
    return nc, n_ops


def _host_constants():
    i = np.arange(P)
    im = (i % 64).astype(np.float32)
    x, y = im[:, None], im[None, :]
    blkm = (i[:, None] // 64) == (i[None, :] // 64)
    return dict(
        dijs=np.where(blkm, x - y, 0.0).astype(np.float32),
        bdstrictneg=np.where(blkm & (x > y), -1.0, 0.0).astype(np.float32),
        bdinclT=np.where(blkm & (x <= y), 1.0, 0.0).astype(np.float32),
        njcol=(-(im) - 1.0).astype(np.float32)[:, None],
        iprow=(np.arange(C) + 1.0).astype(np.float32)[None, :],
        dwcol=(63.0 - im).astype(np.float32)[:, None],
        gpcol=im.astype(np.float32)[:, None],
    )


_NC_CACHE = {}


def kernel(L0_attn_out, L0_attn_W_write, L0_attn_W_read, L0_attn_W_beta,
           L0_attn_decay, L0_ffn_W1, L0_ffn_b1, L0_ffn_W2, L0_ffn_b2,
           L0_ffn_scale):
    B, T, Dm = L0_attn_out.shape
    f32 = np.float32
    x0 = np.ascontiguousarray(L0_attn_out, f32)
    consts = _host_constants()
    w1t = np.ascontiguousarray(L0_ffn_W1.T, f32)
    w2t = np.ascontiguousarray(L0_ffn_W2.T, f32)
    wwt = np.ascontiguousarray(L0_attn_W_write.T, f32)
    wrt = np.ascontiguousarray(L0_attn_W_read.T, f32)
    wbt = np.ascontiguousarray(L0_attn_W_beta.T, f32)
    b1c = np.ascontiguousarray(np.asarray(L0_ffn_b1, f32).reshape(NHC, P).T, f32)
    b2c = np.ascontiguousarray(np.asarray(L0_ffn_b2, f32).reshape(FC, P).T, f32)
    scl = np.asarray(L0_ffn_scale, f32).reshape(1, 1)
    dec = np.asarray(L0_attn_decay, f32).reshape(1, H)

    key = "nc"
    if key not in _NC_CACHE:
        _NC_CACHE[key] = build_nc()
    nc, n_ops = _NC_CACHE[key]

    in_maps = []
    for c in range(8):
        b, j = c // 2, c % 2
        t0 = j * TOK
        sm = np.zeros((1, 8), f32)
        sm2 = np.zeros((1, 8), f32)
        sm2[0, 2 * b] = 1.0
        if j == 1:
            sm[0, 2 * b] = 1.0
        m = dict(
            x0=np.ascontiguousarray(x0[b, t0:t0 + TOK, :].T),
            ltsrc=np.ascontiguousarray(x0[b, 1:17, :]),
            w1t=w1t, w2t=w2t, wwt=wwt, wrt=wrt, wbt=wbt,
            b1c=b1c, b2c=b2c, scl=scl, decay=dec,
            smask=sm, smask2=sm2, **consts,
        )
        in_maps.append(m)

    res = run_bass_kernel_spmd(nc, in_maps, core_ids=list(range(8)))
    out = np.empty((B, T, Dm), f32)
    for c in range(8):
        b, j = c // 2, c % 2
        t0 = j * TOK
        out[b, t0:t0 + TOK, :] = res.results[c]["out"].T
    return out
